# version 01
# baseline (speedup 1.0000x reference)
import numpy as np
import ml_dtypes
import concourse.bass as bass
import concourse.mybir as mybir
from concourse.bass_utils import run_bass_kernel_spmd

F32 = mybir.dt.float32
BF16 = mybir.dt.bfloat16
U8 = mybir.dt.uint8
AF = mybir.ActivationFunctionType
ALU = mybir.AluOpType
AX = mybir.AxisListType.X

L, D, H, SEG = 2048, 1024, 16, 64
NT, NCH = 16, 16
NE = 8
NCORE = 8
LN32 = float(np.log(32.0))

DEBUG = False


def _host_consts():
    bf = ml_dtypes.bfloat16
    f = np.arange(128)
    jj, rr = np.meshgrid(np.arange(32), np.arange(32), indexing="ij")
    e8 = np.zeros((8, 1024), dtype=bf)
    for e in range(8):
        e8[e, e * 128:(e + 1) * 128] = 1
    return {
        "idnf32": np.eye(128, dtype=np.float32),
        "tri01": (f[None, :] >= f[:, None]).astype(bf),
        "strictu": (f[None, :] < f[:, None]).astype(bf),
        "negi": (-30000.0 * np.eye(128)).astype(bf),
        "rep2": (np.arange(128)[None, :] % 64 == np.arange(64)[:, None]).astype(np.float32),
        "emat": np.clip(rr - 1 - jj, 0, 31).astype(np.float32),
        "ltri": (jj < rr).astype(np.float32),
        "e8": e8,
        "onesrow": np.ones((1, 512), dtype=bf),
        "ones64": np.ones((1, 64), dtype=bf),
        "onescol": np.ones((128, 1), dtype=bf),
        "ones32f": np.ones((1, 32), dtype=np.float32),
        "onesf128": np.ones((1, 128), dtype=np.float32),
    }


class Sem:
    __slots__ = ("h", "n")

    def __init__(self, h):
        self.h = h
        self.n = 0


class Tape:
    ENGS = ("sync", "tensor", "vector", "scalar", "gpsimd")

    def __init__(self):
        self.ops = {e: [] for e in self.ENGS}
        self.waited = {e: {} for e in self.ENGS}

    def add(self, eng, fn, w=(), inc=None, amt=1):
        ws = []
        for s, v in w:
            assert v is not None and v > 0, f"bad wait {v} on {eng}"
            k = id(s)
            if self.waited[eng].get(k, 0) >= v:
                continue
            self.waited[eng][k] = v
            ws.append((s.h, v))
        pair, tick = None, None
        if inc is not None:
            inc.n += amt
            tick = inc.n
            pair = (inc.h, amt)
        self.ops[eng].append((fn, tuple(ws), pair))
        return tick

    def replay(self, eng, e):
        for fn, ws, pair in self.ops[eng]:
            for h, v in ws:
                e.wait_ge(h, v)
            inst = fn(e)
            if pair is not None:
                inst.then_inc(pair[0], pair[1])


def _finish_early(nc, P, waits):
    P.add("sync", lambda e: e.nop(), w=[(s, v) for s, v in waits if v > 0])
    with nc.Block() as block:
        @block.sync
        def _(e):
            P.replay("sync", e)

        @block.tensor
        def _(e):
            P.replay("tensor", e)

        @block.vector
        def _(e):
            P.replay("vector", e)

        @block.scalar
        def _(e):
            P.replay("scalar", e)

        @block.gpsimd
        def _(e):
            P.replay("gpsimd", e)
    return nc


def build(stop_stage=99):
    import os
    nc = bass.Bass("TRN2", target_bir_lowering=False,
                   detect_race_conditions=False)
    nc.dram_tensor("salt" + os.environ.get("KSALT", "0"), [1, 1], F32,
                   kind="Internal")
    HC = _host_consts()

    x = nc.dram_tensor("x", [L, D], F32, kind="ExternalInput").ap()
    Wmats = [nc.dram_tensor(n, [D, D], F32, kind="ExternalInput").ap()
             for n in ("Wq", "Wk", "Wv", "Wc")]
    terr = nc.dram_tensor("terr", [1, SEG], F32, kind="ExternalInput").ap()
    idxd = nc.dram_tensor("idxd", [H, SEG], F32, kind="ExternalInput").ap()
    megad = nc.dram_tensor("megad", [1, H], F32, kind="ExternalInput").ap()
    w1d = nc.dram_tensor("w1", [NE, D, 512], F32, kind="ExternalInput").ap()
    b1d = nc.dram_tensor("b1", [NE, 512], F32, kind="ExternalInput").ap()
    w2d = nc.dram_tensor("w2", [NE, 512, D], F32, kind="ExternalInput").ap()
    b2d = nc.dram_tensor("b2", [NE, D], F32, kind="ExternalInput").ap()
    cent = nc.dram_tensor("cent", [NE, D], F32, kind="ExternalInput").ap()
    wrt = nc.dram_tensor("wrt", [NE, D], F32, kind="ExternalInput").ap()
    cin = {k: nc.dram_tensor("c_" + k, list(v.shape),
                             F32 if v.dtype == np.float32 else BF16,
                             kind="ExternalInput").ap()
           for k, v in HC.items()}
    out = nc.dram_tensor("out", [L, D], F32, kind="ExternalOutput").ap()
    wbf = [nc.dram_tensor(f"wbf{i}", [D, D], BF16, kind="Internal").ap()
           for i in range(4)]
    w1bf = nc.dram_tensor("w1bf", [NE, 128, 4096], BF16, kind="Internal").ap()
    w2bf = nc.dram_tensor("w2bf", [NE, 128, 4096], BF16, kind="Internal").ap()
    dbg = {}
    if DEBUG:
        for n, shp, dt in [("d_xn", [128, 16384], BF16), ("d_q", [128, 16384], BF16),
                           ("d_k", [128, 16384], BF16), ("d_v", [128, 16384], BF16),
                           ("d_y", [128, 16384], BF16), ("d_g", [8, 2048], BF16),
                           ("d_pcs", [32, 2048], BF16), ("d_wt", [128, 512], F32)]:
            dbg[n] = nc.dram_tensor(n, shp, dt, kind="ExternalOutput").ap()

    ar_xn = nc.alloc_sbuf_tensor("ar_xn", [128, 32768], U8).ap()
    ar_q = nc.alloc_sbuf_tensor("ar_q", [128, 32768], U8).ap()
    ar_k = nc.alloc_sbuf_tensor("ar_k", [128, 32768], U8).ap()
    ar_v = nc.alloc_sbuf_tensor("ar_v", [128, 32768], U8).ap()
    ar_y = nc.alloc_sbuf_tensor("ar_y", [128, 32768], U8).ap()
    ar_t = nc.alloc_sbuf_tensor("ar_t", [128, 28672], U8).ap()

    xn_fm = ar_xn.bitcast(BF16)
    q_fm = ar_q.bitcast(BF16)
    k_fm = ar_k.bitcast(BF16)
    v_tm = ar_v.bitcast(BF16)
    y_fm = ar_y.bitcast(BF16)
    wTb = [ar_y[:, 0:16384].bitcast(BF16), ar_y[:, 16384:32768].bitcast(BF16)]
    hprime = ar_q.bitcast(BF16)
    wcT = ar_k[:, 0:16384].bitcast(BF16)
    xst2 = [ar_k[:, 16384:20480].bitcast(F32), ar_k[:, 20480:24576].bitcast(F32)]
    outst = [ar_k[:, 24576:28672].bitcast(F32), ar_k[:, 28672:32768].bitcast(F32)]
    w1b = [ar_v[:, 0:8192].bitcast(BF16), ar_v[:, 8192:16384].bitcast(BF16)]
    w2b = [ar_v[:, 16384:24576].bitcast(BF16), ar_v[:, 24576:32768].bitcast(BF16)]

    xstage = [ar_t[:, 0:4096].bitcast(F32), ar_t[:, 4096:8192].bitcast(F32)]
    xn_st = [ar_t[:, 8192:12288].bitcast(F32), ar_t[:, 12288:16384].bitcast(F32)]
    sqdump = ar_t[:, 16384:20480].bitcast(F32)
    lnS = ar_t[:, 20480:24576].bitcast(F32)
    prepS = ar_t[:, 24576:28672].bitcast(F32)
    Ptm = [ar_t[:, 0:4096].bitcast(BF16), ar_t[:, 4096:8192].bitcast(BF16)]
    stripB = Ptm
    band1 = [ar_t[:, 8192:8448].bitcast(BF16), ar_t[:, 8448:8704].bitcast(BF16)]
    band2 = nc.alloc_sbuf_tensor("band2", [128, 128], BF16).ap()
    pcsS = [ar_t[:, 8960:13056].bitcast(BF16), ar_t[:, 13056:17152].bitcast(BF16)]
    wTh = [ar_t[:, 17152:19200].bitcast(F32), ar_t[:, 19200:21248].bitcast(F32)]
    atS = ar_t[:, 21248:25344].bitcast(F32)
    hgel = [ar_t[:, 0:2048].bitcast(F32), ar_t[:, 2048:4096].bitcast(F32)]
    b1row = ar_t[0:1, 4096:12288].bitcast(BF16)
    b2_sb = ar_t[0:8, 12288:14336].bitcast(BF16)

    csb = {k: nc.alloc_sbuf_tensor("s_" + k, list(HC[k].shape),
                                   F32 if HC[k].dtype == np.float32 else BF16).ap()
           for k in HC}
    g_t8 = nc.alloc_sbuf_tensor("g_t8", [8, 2048], BF16).ap()
    twrow = nc.alloc_sbuf_tensor("twrow", [1, 128], F32).ap()
    owrep2 = nc.alloc_sbuf_tensor("owrep2", [128, 128], F32).ap()
    crep = nc.alloc_sbuf_tensor("crep", [128, 16], F32).ap()
    lnmega = nc.alloc_sbuf_tensor("lnmega", [1, 16], F32).ap()
    cnrb = nc.alloc_sbuf_tensor("cnrb", [128, 128], BF16).ap()
    centF = nc.alloc_sbuf_tensor("centF", [8, 1024], F32).ap()
    routeF = nc.alloc_sbuf_tensor("routeF", [8, 1024], F32).ap()

    msum, sq2 = lnS[:, 0:2], lnS[:, 2:4]
    mmean, varv = lnS[:, 4:6], lnS[:, 6:8]
    rstd, negm = lnS[:, 8:10], lnS[:, 10:12]
    recn, tvar = lnS[:, 12:14], lnS[:, 24:26]
    m1c, m2c, nm1 = lnS[:, 14:16], lnS[:, 16:18], lnS[:, 18:20]
    gsum, grec = lnS[:, 20:22], lnS[:, 22:24]
    lgec = [lnS[:, 32:48], lnS[:, 48:64]]
    lgf = [lnS[:, 64:72], lnS[:, 72:80]]
    ge1 = [lnS[:, 96:104], lnS[:, 104:112]]
    mskd = [lnS[:, 112:120], lnS[:, 120:128]]
    kept = [lnS[:, 128:136], lnS[:, 136:144]]
    egk = [lnS[:, 144:152], lnS[:, 152:160]]
    gtm = [lnS[:, 160:168], lnS[:, 168:176]]

    cn2, lncn, rcn = prepS[0:8, 0:1], prepS[0:8, 1:2], prepS[0:8, 2:3]
    idxF = prepS[0:16, 8:72]
    c16 = prepS[0:16, 72:136]
    callT = prepS[0:64, 136:152]
    megae, megac = prepS[0:1, 152:168], prepS[0:1, 168:184]
    twe, twp, tw64 = prepS[0:1, 184:248], prepS[0:1, 248:312], prepS[0:1, 312:376]
    ow64 = prepS[0:1, 376:440]
    owrow = prepS[0:1, 440:568]
    sq8b = prepS[0:8, 504:1016]

    zparts = atS[:, 0:8]
    Zc, rZc = atS[:, 8:10], atS[:, 10:12]
    EA, ATf = atS[0:32, 16:48], atS[0:32, 48:80]
    rnb32 = atS[0:32, 80:112]
    lnm32, Pcol = atS[0:32, 112:113], atS[0:32, 113:114]
    rn0, rnorm = atS[0:32, 114:115], atS[0:32, 115:116]
    rnrow = atS[0:1, 116:148]
    ztrow, ztr2 = atS[0:1, 148:276], atS[0:1, 276:404]
    rztw_sb = atS[:, 404:532]
    tcomb = atS[:, 532:660]
    t2comb = atS[:, 660:788]
    ATb = atS[0:32, 800:816].bitcast(BF16)
    A2b = atS[0:32, 816:832].bitcast(BF16)
    Pcolb = atS[0:32, 832:833].bitcast(BF16)
    rztwb = atS[0:1, 836:900].bitcast(BF16)
    cvec = atS[:, 900:916].bitcast(BF16)

    PS = nc.alloc_psum_tensor("PS", [128, 4096], F32).ap()

    sT, sV, sS = Sem(nc.alloc_semaphore("sT")), Sem(nc.alloc_semaphore("sV")), \
        Sem(nc.alloc_semaphore("sS"))
    dS, dG, sO = Sem(nc.alloc_semaphore("dS")), Sem(nc.alloc_semaphore("dG")), \
        Sem(nc.alloc_semaphore("sO"))

    P = Tape()

    def T(fn, w=(), inc=True):
        return P.add("tensor", fn, w, sT if inc else None)

    def V(fn, w=(), inc=True):
        return P.add("vector", fn, w, sV if inc else None)

    def S(fn, w=(), inc=True):
        return P.add("scalar", fn, w, sS if inc else None)

    def SY(fn, w=()):
        return P.add("sync", fn, w, dS, 16)

    def SYo(fn, w=()):
        return P.add("sync", fn, w, sO, 16)

    def G(fn, w=()):
        return P.add("gpsimd", fn, w, dG, 16)

    def Tdrain():
        P.add("tensor", lambda e: e.nop(), w=[(sT, sT.n)] if sT.n else [])

    bank_free = {b: [] for b in range(8)}

    def bfree(b, sem, tick):
        bank_free[b] = [(sem, tick)]

    def bwait(*banks):
        ws = []
        for b in banks:
            ws += bank_free[b]
        return ws

    for k in HC:
        SY(lambda e, d=csb[k], s=cin[k]: e.dma_start(out=d, in_=s))
    SY(lambda e: e.dma_start(out=idxF, in_=idxd))
    SY(lambda e: e.dma_start(out=megae, in_=megad))
    SY(lambda e: e.dma_start(out=twe, in_=terr))
    SY(lambda e: e.dma_start(out=centF, in_=cent))
    SY(lambda e: e.dma_start(out=routeF, in_=wrt))
    t_prein = dS.n
    tw_wbf = [G(lambda e, d=wbf[i], s=Wmats[i]: e.dma_start(out=d, in_=s))
              for i in range(4)]
    t_w1bf, t_w2bf = [], []
    for e_ in range(NE):
        t_w1bf.append(G(lambda e, i=e_: e.dma_start(
            out=w1bf[i], in_=w1d[i].rearrange("(ic p) h -> p ic h", p=128))))
        t_w2bf.append(G(lambda e, i=e_: e.dma_start(
            out=w2bf[i], in_=w2d[i].rearrange("(hc p) o -> p hc o", p=128))))

    wp = [(dS, t_prein)]
    a1 = S(lambda e: e.activation(out=sq8b, in_=centF[:, 0:512], func=AF.Square,
                                  accum_out=cn2), w=wp)
    a2 = S(lambda e: e.activation(out=sq8b, in_=centF[:, 512:1024], func=AF.Square,
                                  accum_out=lncn), w=[(sS, a1)])
    a3 = V(lambda e: e.tensor_tensor(out=cn2, in0=cn2, in1=lncn, op=ALU.add),
           w=[(sS, a2)])
    a4 = S(lambda e: e.activation(out=lncn, in_=cn2, func=AF.Ln), w=[(sV, a3)])
    a5 = S(lambda e: e.activation(out=rcn, in_=lncn, func=AF.Exp, scale=-0.5),
           w=[(sS, a4)])
    a6 = V(lambda e: e.tensor_scalar(out=centF, in0=centF, scalar1=rcn,
                                     scalar2=1.0 / 32.0, op0=ALU.mult,
                                     op1=ALU.mult), w=[(sS, a5)])
    tt = None
    for ic in range(8):
        T(lambda e, ic=ic: e.transpose(
            PS[:, 3584 + ic * 16:3592 + ic * 16],
            centF[0:8, ic * 128:(ic + 1) * 128], csb["idnf32"][0:8, 0:8]),
            w=[(sV, a6)], inc=False)
        tt = T(lambda e, ic=ic: e.transpose(
            PS[:, 3592 + ic * 16:3600 + ic * 16],
            routeF[0:8, ic * 128:(ic + 1) * 128], csb["idnf32"][0:8, 0:8]),
            w=[(dS, t_prein)], inc=(ic == 7))
    a7 = V(lambda e: e.tensor_copy(out=cnrb, in_=PS[:, 3584:3712]), w=[(sT, tt)])
    bfree(7, sV, a7)
    b1_ = S(lambda e: e.activation(out=c16, in_=idxF, func=AF.Exp), w=wp)
    b2_ = T(lambda e: e.transpose(PS[0:64, 3072:3088], c16[0:16, 0:64],
                                  csb["idnf32"][0:16, 0:16]), w=[(sS, b1_)])
    b3 = V(lambda e: e.tensor_copy(out=callT, in_=PS[0:64, 3072:3088]), w=[(sT, b2_)])
    b4 = T(lambda e: e.matmul(PS[:, 3088:3104], csb["rep2"], callT,
                              start=True, stop=True), w=[(sV, b3)])
    t_crep = V(lambda e: e.tensor_copy(out=crep, in_=PS[:, 3088:3104]), w=[(sT, b4)])
    c1_ = S(lambda e: e.activation(out=megac, in_=megae, func=AF.Exp, scale=64.0), w=wp)
    c2_ = V(lambda e: e.tensor_scalar_min(out=megac, in0=megac, scalar1=2.0),
            w=[(sS, c1_)])
    t_lnmega = S(lambda e: e.activation(out=lnmega, in_=megac, func=AF.Ln),
                 w=[(sV, c2_)])
    d1 = S(lambda e: e.activation(out=twp, in_=twe, func=AF.Exp, scale=-1.0), w=wp)
    d2 = V(lambda e: e.tensor_scalar_add(out=twp, in0=twp, scalar1=1.0), w=[(sS, d1)])
    d3 = V(lambda e: e.reciprocal(out=tw64, in_=twp), w=[(sV, d2)])
    d4 = V(lambda e: e.tensor_copy(out=twrow[0:1, 0:64], in_=tw64), w=[(sV, d3)])
    t_twrow = V(lambda e: e.tensor_copy(out=twrow[0:1, 64:128], in_=tw64),
                w=[(sV, d3)])
    d6 = V(lambda e: e.tensor_scalar(out=ow64, in0=tw64, scalar1=-1.0, scalar2=1.0,
                                     op0=ALU.mult, op1=ALU.add), w=[(sV, d3)])
    d7 = V(lambda e: e.tensor_copy(out=owrow[0:1, 0:64], in_=ow64), w=[(sV, d6)])
    d8 = V(lambda e: e.tensor_copy(out=owrow[0:1, 64:128], in_=ow64), w=[(sV, d6)])
    d9 = T(lambda e: e.matmul(PS[:, 3104:3232], csb["onesf128"], owrow,
                              start=True, stop=True), w=[(sV, d7), (sV, d8)])
    t_owrep = V(lambda e: e.tensor_copy(out=owrep2, in_=PS[:, 3104:3232]),
                w=[(sT, d9)])
    bfree(6, sV, t_owrep)

    t_xn = [0] * NT
    t_lnS = [0] * NT
    t_lnV = [0] * NT
    t_xtr = [0] * NT
    for tc in range(NT):
        par = tc % 2
        w_x = [(sS, t_lnS[tc - 2]), (sV, t_lnV[tc - 2])] if tc >= 2 else []
        tx = SY(lambda e, tc=tc, p=par: e.dma_start(
            out=xstage[p], in_=x[tc * 128:(tc + 1) * 128, :]), w=w_x)
        wx = [(dS, tx)]
        va = V(lambda e, p=par: e.reduce_sum(out=msum[:, p:p + 1], in_=xstage[p],
                                             axis=AX), w=wx)
        sa = S(lambda e, p=par: e.activation(out=sqdump, in_=xstage[p],
                                             func=AF.Square,
                                             accum_out=sq2[:, p:p + 1]), w=wx)
        vb = V(lambda e, p=par: e.tensor_scalar_mul(
            out=mmean[:, p:p + 1], in0=msum[:, p:p + 1], scalar1=-1.0 / D),
            w=[(sV, va)])
        vc = V(lambda e, p=par: e.tensor_scalar(
            out=tvar[:, p:p + 1], in0=mmean[:, p:p + 1], scalar1=mmean[:, p:p + 1],
            scalar2=None, op0=ALU.mult), w=[(sV, vb)])
        vd = V(lambda e, p=par: e.tensor_scalar_mul(
            out=varv[:, p:p + 1], in0=sq2[:, p:p + 1], scalar1=1.0 / D),
            w=[(sS, sa), (sV, vc)])
        ve = V(lambda e, p=par: e.tensor_tensor(
            out=varv[:, p:p + 1], in0=varv[:, p:p + 1], in1=tvar[:, p:p + 1],
            op=ALU.subtract), w=[(sV, vd)])
        vf = V(lambda e, p=par: e.tensor_scalar_add(
            out=tvar[:, p:p + 1], in0=varv[:, p:p + 1], scalar1=1e-5), w=[(sV, ve)])
        sb_ = S(lambda e, p=par: e.activation(out=rstd[:, p:p + 1],
                                              in_=tvar[:, p:p + 1], func=AF.Ln),
                w=[(sV, vf)])
        sc_ = S(lambda e, p=par: e.activation(out=rstd[:, p:p + 1],
                                              in_=rstd[:, p:p + 1], func=AF.Exp,
                                              scale=-0.5), w=[(sS, sb_)])
        vh = V(lambda e, p=par: e.tensor_scalar(
            out=recn[:, p:p + 1], in0=varv[:, p:p + 1], scalar1=rstd[:, p:p + 1],
            scalar2=rstd[:, p:p + 1], op0=ALU.mult, op1=ALU.mult), w=[(sS, sc_)])
        si = S(lambda e, p=par: e.activation(out=recn[:, p:p + 1],
                                             in_=recn[:, p:p + 1], func=AF.Ln),
               w=[(sV, vh)])
        sj = S(lambda e, p=par: e.activation(out=recn[:, p:p + 1],
                                             in_=recn[:, p:p + 1], func=AF.Exp,
                                             scale=-0.5), w=[(sS, si)])
        w_xn = [(sS, sc_)] + ([(sT, t_xtr[tc - 2])] if tc >= 2 else [])
        sk = V(lambda e, p=par: e.tensor_scalar(
            out=xn_st[p], in0=xstage[p], scalar1=mmean[:, p:p + 1],
            scalar2=rstd[:, p:p + 1], op0=ALU.add, op1=ALU.mult), w=w_xn)
        t_lnS[tc] = sa
        base = par * 1024
        wtr = [(sV, sk)] + (bwait(par * 2, par * 2 + 1) if tc >= 2 else [])
        ttr = None
        for ic in range(8):
            ttr = T(lambda e, p=par, ic=ic, b=base: e.transpose(
                PS[:, b + ic * 128: b + (ic + 1) * 128],
                xn_st[p][:, ic * 128:(ic + 1) * 128], csb["idnf32"]),
                w=wtr, inc=(ic == 7))
        t_xtr[tc] = ttr
        vk = V(lambda e, tc=tc, b=base: e.tensor_copy(
            out=xn_fm.rearrange("p (ic t) -> p ic t", ic=8)[:, :, tc * 128:(tc + 1) * 128],
            in_=PS[:, b:b + 1024].rearrange("p (ic j) -> p ic j", ic=8)),
            w=[(sT, ttr)])
        t_xn[tc] = vk
        bfree(par * 2, sV, vk)
        bfree(par * 2 + 1, sV, vk)
        rb = 2048 + par * 512
        wr = [(sV, vk), (sV, a7)] + (bwait(4 + par) if tc >= 2 else [])
        trm = None
        for ic in range(8):
            trm = T(lambda e, ic=ic, tc=tc, rb=rb: e.matmul(
                PS[:, rb:rb + 16],
                xn_fm[:, ic * 2048 + tc * 128: ic * 2048 + (tc + 1) * 128],
                cnrb[:, ic * 16:(ic + 1) * 16],
                start=(ic == 0), stop=(ic == 7), skip_group_check=True),
                w=wr, inc=(ic == 7))
        r1 = V(lambda e, p=par, rb=rb: e.tensor_copy(out=lgec[p], in_=PS[:, rb:rb + 16]),
               w=[(sT, trm)])
        bfree(4 + par, sV, r1)
        r2 = V(lambda e, p=par: e.scalar_tensor_tensor(
            out=lgf[p], in0=lgec[p][:, 0:8], scalar=recn[:, p:p + 1],
            in1=lgec[p][:, 8:16], op0=ALU.mult, op1=ALU.add),
            w=[(sV, r1), (sS, sj)])
        r3 = V(lambda e, p=par: e.reduce_max(out=m1c[:, p:p + 1], in_=lgf[p], axis=AX),
               w=[(sV, r2)])
        r4 = V(lambda e, p=par: e.tensor_scalar(
            out=ge1[p], in0=lgf[p], scalar1=m1c[:, p:p + 1], scalar2=None,
            op0=ALU.is_ge), w=[(sV, r3)])
        r5 = V(lambda e, p=par: e.scalar_tensor_tensor(
            out=mskd[p], in0=ge1[p], scalar=-1e30, in1=lgf[p],
            op0=ALU.mult, op1=ALU.add), w=[(sV, r4)])
        r6 = V(lambda e, p=par: e.reduce_max(out=m2c[:, p:p + 1], in_=mskd[p], axis=AX),
               w=[(sV, r5)])
        r7 = V(lambda e, p=par: e.tensor_scalar(
            out=kept[p], in0=lgf[p], scalar1=m2c[:, p:p + 1], scalar2=None,
            op0=ALU.is_ge), w=[(sV, r6)])
        r8 = V(lambda e, p=par: e.tensor_scalar_mul(
            out=nm1[:, p:p + 1], in0=m1c[:, p:p + 1], scalar1=-1.0), w=[(sV, r3)])
        r9 = S(lambda e, p=par: e.activation(out=egk[p], in_=lgf[p], func=AF.Exp,
                                             bias=nm1[:, p:p + 1]), w=[(sV, r8)])
        r10 = V(lambda e, p=par: e.tensor_tensor(out=egk[p], in0=egk[p], in1=kept[p],
                                                 op=ALU.mult), w=[(sS, r9), (sV, r7)])
        r11 = V(lambda e, p=par: e.reduce_sum(out=gsum[:, p:p + 1], in_=egk[p], axis=AX),
                w=[(sV, r10)])
        r12 = V(lambda e, p=par: e.reciprocal(out=grec[:, p:p + 1],
                                              in_=gsum[:, p:p + 1]), w=[(sV, r11)])
        r13 = V(lambda e, p=par: e.tensor_scalar_mul(
            out=gtm[p], in0=egk[p], scalar1=grec[:, p:p + 1]), w=[(sV, r12)])
        gb = 3072 + par * 128
        r14 = T(lambda e, p=par, gb=gb: e.transpose(
            PS[0:8, gb:gb + 128], gtm[p], csb["idnf32"]), w=[(sV, r13)] + bwait(6))
        r15 = V(lambda e, tc=tc, gb=gb: e.tensor_copy(
            out=g_t8[0:8, tc * 128:(tc + 1) * 128], in_=PS[0:8, gb:gb + 128]),
            w=[(sT, r14)])
        t_lnV[tc] = r15
        bfree(6, sV, r15)
    t_ln_done = t_lnV[NT - 1]
    t_xn_all = t_xn[NT - 1]
    if stop_stage <= 1:
        return _finish_early(nc, P, [(sT, sT.n), (sV, sV.n), (sS, sS.n),
                                     (dS, dS.n), (dG, dG.n)])

    t_wT = [0, 0, 0]
    for m in range(2):
        tk = None
        for ic in range(8):
            tk = SY(lambda e, m=m, ic=ic: e.dma_start(
                out=wTb[m][:, ic * 1024:(ic + 1) * 1024],
                in_=wbf[m][:, ic * 128:(ic + 1) * 128], transpose=True),
                w=[(dG, tw_wbf[m])])
        t_wT[m] = tk

    def mm_proj(dst_fm, wsrc, t_wready):
        tick_last = None
        for oc in range(8):
            for tg in range(4):
                bk = (oc * 4 + tg) % 2
                w0 = [(dS, t_wready), (sV, t_xn_all)] + bwait(bk)
                tm = None
                for ic in range(8):
                    tm = T(lambda e, oc=oc, tg=tg, ic=ic, bk=bk, ws=wsrc: e.matmul(
                        PS[:, bk * 512:(bk + 1) * 512],
                        ws[:, ic * 1024 + oc * 128: ic * 1024 + (oc + 1) * 128],
                        xn_fm[:, ic * 2048 + tg * 512: ic * 2048 + (tg + 1) * 512],
                        start=(ic == 0), stop=(ic == 7), skip_group_check=True),
                        w=w0, inc=(ic == 7))
                tv_ = V(lambda e, oc=oc, tg=tg, bk=bk, d=dst_fm: e.tensor_copy(
                    out=d[:, oc * 2048 + tg * 512: oc * 2048 + (tg + 1) * 512],
                    in_=PS[:, bk * 512:(bk + 1) * 512]), w=[(sT, tm)])
                bfree(bk, sV, tv_)
                tick_last = tm
        return tick_last

    t_qmm = mm_proj(q_fm, wTb[0], t_wT[0])
    mm_proj(k_fm, wTb[1], t_wT[1])
    tk = None
    for ic in range(8):
        tk = SY(lambda e, ic=ic: e.dma_start(
            out=wTb[0][:, ic * 1024:(ic + 1) * 1024],
            in_=wbf[2][:, ic * 128:(ic + 1) * 128], transpose=True),
            w=[(dG, tw_wbf[2]), (sT, t_qmm)])
    t_wT[2] = tk
    for tc in range(NT):
        for og in range(2):
            bk = (tc * 2 + og) % 2
            w0 = [(dS, t_wT[2])] + bwait(bk)
            tm = None
            for ic in range(8):
                tm = T(lambda e, tc=tc, og=og, ic=ic, bk=bk: e.matmul(
                    PS[:, bk * 512:(bk + 1) * 512],
                    xn_fm[:, ic * 2048 + tc * 128: ic * 2048 + (tc + 1) * 128],
                    wTb[0][:, ic * 1024 + og * 512: ic * 1024 + (og + 1) * 512],
                    start=(ic == 0), stop=(ic == 7), skip_group_check=True),
                    w=w0, inc=(ic == 7))
            tv_ = V(lambda e, tc=tc, og=og, bk=bk: e.tensor_copy(
                out=v_tm[:, tc * 1024 + og * 512: tc * 1024 + (og + 1) * 512],
                in_=PS[:, bk * 512:(bk + 1) * 512]), w=[(sT, tm)])
            bfree(bk, sV, tv_)
    t_qkv_T = sT.n
    if stop_stage <= 2:
        return _finish_early(nc, P, [(sT, sT.n), (sV, sV.n), (sS, sS.n),
                                     (dS, dS.n), (dG, dG.n)])

    war_pt = [[], []]
    war_cvec = []
    war_pcs = [[], []]
    war_wTh = [[], []]
    war_band1 = [[], []]
    war_band2 = []
    war_zt = []
    war_rz = []
    war_bandacc = []
    t_comb_prev = 0

    tbz0 = V(lambda e: e.memset(band1[0], 0.0), w=[(sT, t_qkv_T)])
    tbz1 = V(lambda e: e.memset(band1[1], 0.0), w=[(sT, t_qkv_T)])
    war_band1 = [[(sV, tbz0)], [(sV, tbz1)]]

    for h in range(H):
        hp = h % 2
        hb = hp * 64
        icb = (h // 2) * 2048
        tp = (0, 64) if hp else None
        qsl_all = q_fm[hb:hb + 64, icb:icb + 2048]
        ksl_all = k_fm[hb:hb + 64, icb:icb + 2048]

        import os as _os2
        _kmin = _os2.environ.get("KMIN", "")
        if _kmin:
            t_wTr = V(lambda e, hp=hp: e.memset(wTh[hp], 0.001),
                      w=[(sT, t_qkv_T)])
            t_pcs = [sT.n if sT.n else 1] * NT
            t_pcsev = t_wTr
            twt = sT.n if sT.n else 1
        _kmin_noA = _kmin in ("1", "3")
        _kmin_noScan = _kmin in ("1", "2")
        Tdrain()
        if _kmin_noA if _kmin else False:
            tcv = None
        else:
            tcv = V(lambda e: e.memset(cvec, 0.0), w=war_cvec)
        t_pcs = [0] * NT
        t_exp = {}
        for i in ([] if (_kmin and _kmin_noA) else range(NT)):
            W = 128 * (i + 1)
            par = i % 2
            ngrp = (W + 511) // 512
            qsl = qsl_all[:, i * 128:(i + 1) * 128]
            for Gi in range(ngrp):
                n0, n1 = 512 * Gi, min(W, 512 * Gi + 512)
                w0 = bwait(Gi)
                if h == 0 and i == 0 and Gi == 0:
                    w0 = w0 + [(sT, t_qkv_T)]
                lastg = (Gi == ngrp - 1)
                if not lastg:
                    tsm = T(lambda e, n0=n0, n1=n1, q=qsl, ks=ksl_all: e.matmul(
                        PS[:, n0:n1], q, ks[:, n0:n1],
                        start=True, stop=True, skip_group_check=True), w=w0)
                else:
                    T(lambda e, n0=n0, n1=n1, q=qsl, ks=ksl_all: e.matmul(
                        PS[:, n0:n1], q, ks[:, n0:n1],
                        start=True, stop=False, skip_group_check=True),
                        w=w0, inc=False)
                    tsm = T(lambda e, i=i: e.matmul(
                        PS[:, i * 128:(i + 1) * 128], csb["strictu"], csb["negi"],
                        start=False, stop=True, skip_group_check=True))
                wx = [(sT, tsm)]
                if i >= 2:
                    wx.append((sT, t_pcs[i - 2]))
                wx += war_pt[par]
                te = S(lambda e, p=par, n0=n0, n1=n1, g=Gi: e.activation(
                    out=Ptm[p][:, n0:n1], in_=PS[:, n0:n1], func=AF.Exp,
                    scale=0.125, accum_out=zparts[:, p * 4 + g: p * 4 + g + 1]),
                    w=wx)
                t_exp[(i, Gi)] = te
                bfree(Gi, sS, te)
            war_pt[par] = []
            z1 = V(lambda e, p=par, ng=ngrp: e.reduce_sum(
                out=Zc[:, p:p + 1], in_=zparts[:, p * 4: p * 4 + ng], axis=AX),
                w=[(sS, t_exp[(i, ngrp - 1)])])
            z2 = V(lambda e, p=par: e.reciprocal(out=rZc[:, p:p + 1],
                                                 in_=Zc[:, p:p + 1]), w=[(sV, z1)])
            wc = [(sV, z2), (sV, tcv), (sV, t_crep)]
            if i > 0:
                V(lambda e, i=i: e.memset(cvec[:, 2 * i - 2: 2 * i], 0.0),
                  w=[(sT, t_pcs[i - 1])], inc=False)
            cv1 = V(lambda e, i=i, h=h, p=par: e.tensor_tensor(
                out=cvec[0:64, 2 * i: 2 * i + 1], in0=crep[0:64, h:h + 1],
                in1=rZc[0:64, p:p + 1], op=ALU.mult), w=wc)
            cv2 = V(lambda e, i=i, h=h, p=par: e.tensor_tensor(
                out=cvec[64:128, 2 * i + 1: 2 * i + 2], in0=crep[64:128, h:h + 1],
                in1=rZc[64:128, p:p + 1], op=ALU.mult), w=wc)
            tpc = None
            for Gi in range(ngrp):
                n0, n1 = 512 * Gi, min(W, 512 * Gi + 512)
                first = (i == 4 * Gi)
                wpc = [(sV, cv1), (sV, cv2), (sS, t_exp[(i, Gi)])]
                if first and n1 < 512 * (Gi + 1):
                    tz = V(lambda e, p=par, n1=n1, Gi=Gi: e.memset(
                        Ptm[p][:, n1:512 * (Gi + 1)], 0.0),
                        w=[(sT, t_pcs[i - 2])] if i >= 2 else [])
                    wpc.append((sV, tz))
                    n1 = 512 * (Gi + 1)
                if first:
                    wpc += bwait(4 + Gi) + war_pcs[hp]
                tpc = T(lambda e, n0=n0, n1=n1, first=first, i=i: e.matmul(
                    PS[0:32, 2048 + n0: 2048 + n1], cvec, Ptm[i % 2][:, n0:n1],
                    start=first, stop=(i == NT - 1), skip_group_check=True),
                    w=wpc, inc=(Gi == ngrp - 1))
            t_pcs[i] = tpc
        if _kmin and _kmin_noA:
            pass
        else:
            war_cvec = [(sT, t_pcs[NT - 1])]
        if (not _kmin) or not _kmin_noScan:
            if stop_stage == 21 and h == 0:
                return _finish_early(nc, P, [(sT, sT.n), (sV, sV.n), (sS, sS.n),
                                             (dS, dS.n), (dG, dG.n)])
            tevp = None
            for g in range(4):
                tevp = V(lambda e, g=g, hp=hp: e.tensor_copy(
                    out=pcsS[hp][0:32, 512 * g: 512 * (g + 1)],
                    in_=PS[0:32, 2048 + 512 * g: 2048 + 512 * (g + 1)]),
                    w=[(sT, t_pcs[NT - 1])] + war_pcs[hp])
                bfree(4 + g, sV, tevp)
            t_pcsev = tevp
            war_pcs[hp] = []

            s1_ = T(lambda e, h=h: e.matmul(PS[0:32, 3584:3585], csb["ones32f"],
                                            lnmega[0:1, h:h + 1], start=True, stop=True,
                                            skip_group_check=True),
                    w=[(sS, t_lnmega)] + bwait(7))
            s2_ = V(lambda e: e.tensor_copy(out=lnm32, in_=PS[0:32, 3584:3585]),
                    w=[(sT, s1_)])
            s3_ = V(lambda e: e.tensor_scalar_mul(out=EA, in0=csb["emat"], scalar1=lnm32),
                    w=[(sV, s2_)])
            s4_ = S(lambda e: e.activation(out=EA, in_=EA, func=AF.Exp), w=[(sV, s3_)])
            s5_ = V(lambda e: e.tensor_tensor(out=ATf, in0=EA, in1=csb["ltri"],
                                              op=ALU.mult), w=[(sS, s4_)])
            s6_ = V(lambda e: e.tensor_copy(out=ATb, in_=ATf), w=[(sV, s5_)])
            s7_ = V(lambda e, hp=hp: e.reduce_sum(out=Pcol, in_=pcsS[hp][0:32, :], axis=AX),
                    w=[(sV, t_pcsev)])
            s8_ = V(lambda e: e.tensor_copy(out=Pcolb[0:32, 0:1], in_=Pcol), w=[(sV, s7_)])
            s9_ = T(lambda e: e.matmul(PS[0:32, 3585:3586], ATb, Pcolb[0:32, 0:1],
                                       start=True, stop=True, skip_group_check=True),
                    w=[(sV, s6_), (sV, s8_)])
            s10_ = V(lambda e: e.tensor_scalar_add(out=rn0, in0=PS[0:32, 3585:3586],
                                                   scalar1=1e-5), w=[(sT, s9_)])
            s11_ = V(lambda e: e.reciprocal(out=rnorm, in_=rn0), w=[(sV, s10_)])
            s12_ = T(lambda e: e.transpose(PS[0:1, 3586:3618], rnorm,
                                           csb["idnf32"][0:32, 0:32]), w=[(sV, s11_)])
            s13_ = V(lambda e: e.tensor_copy(out=rnrow, in_=PS[0:1, 3586:3618]),
                     w=[(sT, s12_)])
            s14_ = T(lambda e: e.matmul(PS[0:32, 3618:3650], csb["ones32f"], rnrow,
                                        start=True, stop=True, skip_group_check=True),
                     w=[(sV, s13_)])
            s15_ = V(lambda e: e.tensor_copy(out=rnb32, in_=PS[0:32, 3618:3650]),
                     w=[(sT, s14_)])
            s16_ = V(lambda e: e.tensor_tensor(out=A2b, in0=ATf, in1=rnb32, op=ALU.mult),
                     w=[(sV, s15_)])
            bfree(7, sV, s15_)
            twt = None
            for c in range(NCH):
                twt = T(lambda e, c=c, hp=hp: e.matmul(
                    PS[:, 3072 + 32 * c: 3072 + 32 * (c + 1)],
                    pcsS[hp][0:32, 128 * c: 128 * (c + 1)], A2b,
                    start=True, stop=True, skip_group_check=True),
                    w=[(sV, s16_)] + (bwait(6) if c == 0 else []), inc=(c == NCH - 1))
            war_pcs[hp] = [(sT, twt)]
            t_wTr = V(lambda e, hp=hp: e.tensor_copy(out=wTh[hp], in_=PS[:, 3072:3584]),
                      w=[(sT, twt)] + war_wTh[hp])
            if stop_stage == 22 and h == 0:
                return _finish_early(nc, P, [(sT, sT.n), (sV, sV.n), (sS, sS.n),
                                             (dS, dS.n), (dG, dG.n)])

        Tdrain()
        sfree = {0: list(bank_free[0]), 1: list(bank_free[1])}
        t_stm = {}
        t_wg = {}
        t_pv = [0] * NCH
        t_bmm = [0] * NCH
        t_zt = [0] * NCH
        t_ztread = [0] * NCH
        t_rzev = [0] * NCH
        t_cmbm = [0] * NCH
        t_cmb = [0] * NCH
        t_b1w = [0] * NCH
        for c in range(NCH):
            Gmin = c // 4
            Tdrain()
            ksl = ksl_all[:, c * 128:(c + 1) * 128]
            G2b = Gmin
            G1b = (128 * c + 128) // 512 if c < NCH - 1 else -1
            off0 = 128 * c - 512 * G2b
            sb0 = G2b % 2
            tb2 = None
            tb1 = None
            for Gi in range(Gmin, 4):
                n0, n1 = max(128 * c, 512 * Gi), 512 * (Gi + 1)
                sb = Gi % 2
                off = n0 - 512 * Gi
                w0 = list(sfree[sb])
                sfree[sb] = []
                tsm = T(lambda e, n0=n0, n1=n1, sb=sb, off=off, k=ksl,
                        qs=qsl_all: e.matmul(
                    PS[:, sb * 512 + off: sb * 512 + off + (n1 - n0)],
                    k, qs[:, n0:n1], start=True, stop=True, skip_group_check=True),
                    w=w0)
                t_stm[(c, Gi)] = tsm
                wact = []
                if Gi == G2b:
                    wb2 = [(sT, tsm)]
                    if c > 0:
                        wb2 += [(sT, t_bmm[c - 1]), (sT, t_zt[c - 1])]
                    else:
                        wb2 += war_band2
                    tb2 = S(lambda e, sb0=sb0, off0=off0: e.activation(
                        out=band2,
                        in_=PS[:, sb0 * 512 + off0: sb0 * 512 + off0 + 128],
                        func=AF.Exp, scale=0.125), w=wb2)
                    sfree[sb0].append((sS, tb2))
                    wact.append((sS, tb2))
                if Gi == G1b:
                    colB1 = 128 * c + 128
                    off1 = colB1 - 512 * G1b
                    sb1 = G1b % 2
                    wb1 = [(sT, tsm)]
                    if c > 0:
                        wb1 += [(sT, t_bmm[c - 1]), (sT, t_zt[c - 1])]
                    else:
                        wb1 += war_band1[(c + 1) % 2]
                    tb1 = S(lambda e, sb1=sb1, off1=off1, c=c: e.activation(
                        out=band1[(c + 1) % 2][64:128, 0:64],
                        in_=PS[64:128, sb1 * 512 + off1: sb1 * 512 + off1 + 64],
                        func=AF.Exp, scale=0.125), w=wb1)
                    t_b1w[c + 1] = tb1
                    sfree[sb1].append((sS, tb1))
                    wact.append((sS, tb1))
                nr = (n1 - n0) // 64
                r0 = n0 // 64
                wwg = [(sT, tsm), (sV, t_wTr)] + wact
                if c >= 2:
                    wwg.append((sT, t_pv[c - 2]))
                wwg += war_pt[c % 2]
                twg = V(lambda e, n0=n0, n1=n1, sb=sb, off=off, nr=nr, r0=r0,
                        c=c, hp=hp: e.tensor_tensor(
                    out=stripB[c % 2][:, n0:n1].rearrange("p (r z) -> p r z", z=64),
                    in0=PS[:, sb * 512 + off: sb * 512 + off + (n1 - n0)]
                        .rearrange("p (r z) -> p r z", z=64),
                    in1=wTh[hp][:, 32 * c + r0: 32 * c + r0 + nr]
                        .unsqueeze(2).broadcast_to([128, nr, 64]),
                    op=ALU.mult), w=wwg)
                t_wg[(c, Gi)] = twg
                sfree[sb].append((sV, twg))
            war_pt[c % 2] = []
            tb2m = V(lambda e: e.tensor_tensor(out=band2, in0=band2,
                                               in1=csb["tri01"],
                                               op=ALU.mult), w=[(sS, tb2)])
            if stop_stage == 51 and h == 0 and c == 0:
                return _finish_early(nc, P, [(sT, sT.n), (sV, sV.n), (sS, sS.n),
                                             (dS, dS.n), (dG, dG.n)])
            tpv = None
            for Qd in range(Gmin, 4):
                n0q, n1q = max(512 * Qd, 128 * c), 512 * (Qd + 1)
                qoff = n0q - 512 * Qd
                wpv = [(sV, t_wg[(c, Qd)])]
                if c == 0:
                    wpv += bwait(2 + Qd)
                    if t_comb_prev:
                        wpv.append((sV, t_comb_prev))
                vsl = v_tm[:, c * 1024 + h * 64: c * 1024 + (h + 1) * 64]
                tpv = T(lambda e, Qd=Qd, n0q=n0q, n1q=n1q, qoff=qoff, c=c,
                        vs=vsl, hb=hb, tp=tp: e.matmul(
                    PS[hb:hb + 64,
                       1024 + 512 * Qd + qoff: 1024 + 512 * Qd + qoff + (n1q - n0q)],
                    vs, stripB[c % 2][:, n0q:n1q],
                    start=(c == 0), stop=(c == 4 * Qd + 3), skip_group_check=True,
                    tile_position=tp), w=wpv, inc=(Qd == 3))
            t_pv[c] = tpv
            if stop_stage == 52 and h == 0 and c == 0:
                return _finish_early(nc, P, [(sT, sT.n), (sV, sV.n), (sS, sS.n),
                                             (dS, dS.n), (dG, dG.n)])
            wbm = [(sV, tb2m)]
            if c > 0:
                wbm += [(sV, t_cmbm[c - 1]), (sS, t_b1w[c])]
                vsl1 = v_tm[:, (c - 1) * 1024 + h * 64: (c - 1) * 1024 + (h + 1) * 64]
                T(lambda e, c=c, vs=vsl1, hb=hb, tp=tp: e.matmul(
                    PS[hb:hb + 64, 3072:3200], vs, band1[c % 2],
                    start=True, stop=False, skip_group_check=True,
                    tile_position=tp), w=wbm, inc=False)
            else:
                wbm += war_bandacc + bwait(6)
            vsl = v_tm[:, c * 1024 + h * 64: c * 1024 + (h + 1) * 64]
            t_bmm[c] = T(lambda e, c=c, vs=vsl, hb=hb, tp=tp: e.matmul(
                PS[hb:hb + 64, 3072:3200], vs, band2,
                start=(c == 0), stop=True, skip_group_check=True,
                tile_position=tp), w=wbm)
            if stop_stage == 53 and h == 0 and c == 0:
                return _finish_early(nc, P, [(sT, sT.n), (sV, sV.n), (sS, sS.n),
                                             (dS, dS.n), (dG, dG.n)])
            Tdrain()
            wzt = [(sV, tb2m)]
            if c > 0:
                wzt += [(sV, t_ztread[c - 1]), (sS, t_b1w[c])]
                T(lambda e, c=c: e.matmul(
                    PS[0:1, 3712:3840], csb["onescol"], band1[c % 2],
                    start=True, stop=False, skip_group_check=True), w=wzt, inc=False)
            else:
                wzt += war_zt
            t_zt[c] = T(lambda e, c=c: e.matmul(
                PS[0:1, 3712:3840], csb["onescol"], band2,
                start=(c == 0), stop=True, skip_group_check=True), w=wzt)
            if stop_stage == 54 and h == 0 and c == 0:
                return _finish_early(nc, P, [(sT, sT.n), (sV, sV.n), (sS, sS.n),
                                             (dS, dS.n), (dG, dG.n)])
            u1 = V(lambda e: e.tensor_copy(out=ztrow, in_=PS[0:1, 3712:3840]),
                   w=[(sT, t_zt[c])])
            t_ztread[c] = u1
            u2 = V(lambda e: e.reciprocal(out=ztr2, in_=ztrow), w=[(sV, u1)])
            u3 = V(lambda e: e.tensor_tensor(out=rztwb, in0=ztr2, in1=twrow,
                                             op=ALU.mult), w=[(sV, u2), (sV, t_twrow)])
            wrz = [(sV, u3)]
            if c > 0:
                wrz.append((sV, t_rzev[c - 1]))
            else:
                wrz += war_rz
            trz = T(lambda e, hb=hb, tp=tp: e.matmul(
                PS[hb:hb + 64, 3328:3456], csb["ones64"], rztwb,
                start=True, stop=True, skip_group_check=True, tile_position=tp),
                w=wrz)
            if stop_stage == 55 and h == 0 and c == 0:
                return _finish_early(nc, P, [(sT, sT.n), (sV, sV.n), (sS, sS.n),
                                             (dS, dS.n), (dG, dG.n)])
            u4 = V(lambda e, hb=hb: e.tensor_copy(
                out=rztw_sb[hb:hb + 64, :], in_=PS[hb:hb + 64, 3328:3456]),
                w=[(sT, trz)])
            t_rzev[c] = u4
            u5 = V(lambda e, hb=hb: e.tensor_tensor(
                out=tcomb[hb:hb + 64, :], in0=PS[hb:hb + 64, 3072:3200],
                in1=rztw_sb[hb:hb + 64, :], op=ALU.mult),
                w=[(sV, u4), (sT, t_bmm[c])])
            t_cmbm[c] = u5
            qc0 = 1024 + 512 * (c // 4) + (c % 4) * 128
            u6 = V(lambda e, qc0=qc0, hb=hb: e.tensor_tensor(
                out=t2comb[hb:hb + 64, :], in0=PS[hb:hb + 64, qc0:qc0 + 128],
                in1=owrep2[hb:hb + 64, :], op=ALU.mult),
                w=[(sT, t_pv[c]), (sV, t_owrep)])
            u7 = V(lambda e, c=c, hb=hb, icb=icb: e.tensor_tensor(
                out=y_fm[hb:hb + 64, icb + c * 128: icb + (c + 1) * 128],
                in0=tcomb[hb:hb + 64, :], in1=t2comb[hb:hb + 64, :], op=ALU.add),
                w=[(sV, u5), (sV, u6)])
            t_cmb[c] = u7
            if stop_stage == 23 + c and h == 0:
                return _finish_early(nc, P, [(sT, sT.n), (sV, sV.n), (sS, sS.n),
                                             (dS, dS.n), (dG, dG.n)])
        t_comb_prev = t_cmb[NCH - 1]
        war_pt[0] = [(sT, t_pv[NCH - 2])]
        war_pt[1] = [(sT, t_pv[NCH - 1])]
        war_wTh[hp] = [(sV, t_wg[(NCH - 1, 3)])]
        war_band1 = [[(sT, t_bmm[NCH - 2]), (sT, t_zt[NCH - 2])],
                     [(sT, t_bmm[NCH - 1]), (sT, t_zt[NCH - 1])]]
        war_band2 = [(sT, t_bmm[NCH - 1]), (sT, t_zt[NCH - 1])]
        war_zt = [(sV, t_ztread[NCH - 1])]
        war_rz = [(sV, t_rzev[NCH - 1])]
        war_bandacc = [(sV, t_cmbm[NCH - 1])]
        bank_free[6] = [(sV, t_cmbm[NCH - 1])]
        bank_free[7] = [(sV, t_ztread[NCH - 1])]
        for qd in range(4):
            bank_free[2 + qd] = [(sV, t_cmb[NCH - 1])]
        for sb in (0, 1):
            bank_free[sb] = list(sfree[sb])
        if stop_stage == 3 and h == 0:
            return _finish_early(nc, P, [(sT, sT.n), (sV, sV.n), (sS, sS.n),
                                         (dS, dS.n), (dG, dG.n)])

    t_attn_T = sT.n
    if stop_stage <= 4:
        return _finish_early(nc, P, [(sT, sT.n), (sV, sV.n), (sS, sS.n),
                                     (dS, dS.n), (dG, dG.n)])
    t_attn_V = sV.n
    t_attn_S = sS.n

    Tdrain()
    w_att = [(sT, t_attn_T), (sV, t_attn_V), (sS, t_attn_S)]
    twc = None
    for ic in range(8):
        twc = SY(lambda e, ic=ic: e.dma_start(
            out=wcT[:, ic * 1024:(ic + 1) * 1024],
            in_=wbf[3][:, ic * 128:(ic + 1) * 128], transpose=True),
            w=[(dG, tw_wbf[3])] + w_att)
    t_wcT = twc
    t_b1 = G(lambda e: e.dma_start(
        out=b1row, in_=b1d.rearrange("e h -> (e h)").unsqueeze(0)), w=w_att)
    t_b2 = G(lambda e: e.dma_start(out=b2_sb, in_=b2d), w=w_att)

    t_w1use = {}
    t_w2use = {}
    t_hm = {0: 0, 1: 0}
    t_eo_hl = {}
    t_evac = {}
    t_odma = {}
    for s_ in range(4):
        for e_ in range(NE):
            eb = e_ % 2
            m = s_ * NE + e_
            wld = [(dG, t_w1bf[e_])] + w_att
            if m >= 2:
                wld.append((sT, t_w1use[m - 2]))
            tld = SY(lambda e, i=e_, eb=eb: e.dma_start(out=w1b[eb], in_=w1bf[i]),
                     w=wld)
            gcb = 2 + eb
            tgbc = T(lambda e, i=e_, s=s_, gcb=gcb: e.matmul(
                PS[:, gcb * 512:(gcb + 1) * 512], csb["e8"][:, i * 128:(i + 1) * 128],
                g_t8[0:8, s * 512:(s + 1) * 512], start=True, stop=True,
                skip_group_check=True), w=[(sV, t_ln_done)] + bwait(gcb))
            tlast = None
            thm = None
            for hc in range(4):
                bk = (e_ * 4 + hc) % 2
                w0 = [(dS, tld)] + bwait(bk)
                for icc in range(8):
                    T(lambda e, icc=icc, hc=hc, eb=eb, s=s_, bk=bk: e.matmul(
                        PS[:, bk * 512:(bk + 1) * 512],
                        w1b[eb][:, icc * 512 + hc * 128: icc * 512 + (hc + 1) * 128],
                        xn_fm[:, icc * 2048 + s * 512: icc * 2048 + (s + 1) * 512],
                        start=(icc == 0), stop=False, skip_group_check=True),
                        w=w0, inc=False)
                tb1m = T(lambda e, i=e_, hc=hc, bk=bk: e.matmul(
                    PS[:, bk * 512:(bk + 1) * 512],
                    b1row[0:1, i * 512 + hc * 128: i * 512 + (hc + 1) * 128],
                    csb["onesrow"], start=False, stop=True, skip_group_check=True),
                    w=[(dG, t_b1)])
                tlast = tb1m
                wgl = [(sT, tb1m)]
                if t_hm[bk]:
                    wgl.append((sV, t_hm[bk]))
                tgl = S(lambda e, bk=bk: e.activation(
                    out=hgel[bk], in_=PS[:, bk * 512:(bk + 1) * 512], func=AF.Gelu),
                    w=wgl)
                bfree(bk, sS, tgl)
                whm = [(sS, tgl), (sT, tgbc)]
                if s_ > 0 and (s_ * NE + e_) in ():
                    pass
                if s_ > 0:
                    whm.append((sT, t_eo_hl[s_ - 1]))
                thm = V(lambda e, i=e_, hc=hc, bk=bk, gcb=gcb: e.tensor_tensor(
                    out=hprime[:, (i * 4 + hc) * 512:(i * 4 + hc + 1) * 512],
                    in0=hgel[bk], in1=PS[:, gcb * 512:(gcb + 1) * 512], op=ALU.mult),
                    w=whm)
                t_hm[bk] = thm
            bfree(gcb, sV, thm)
            t_w1use[m] = tlast
        treg = [0] * 8
        for e_ in range(NE):
            eb = e_ % 2
            m = s_ * NE + e_
            wld = [(dG, t_w2bf[e_])] + w_att
            if m >= 2:
                wld.append((sT, t_w2use[m - 2]))
            tld = SY(lambda e, i=e_, eb=eb: e.dma_start(out=w2b[eb], in_=w2bf[i]),
                     w=wld)
            tm = None
            for hc in range(4):
                for reg in range(8):
                    tc4, og = reg // 2, reg % 2
                    first = (e_ == 0 and hc == 0)
                    w0 = [(dS, tld)]
                    if first:
                        w0 += bwait(reg)
                    hsl = hprime[:, (e_ * 4 + hc) * 512 + tc4 * 128:
                                 (e_ * 4 + hc) * 512 + (tc4 + 1) * 128]
                    tm = T(lambda e, hs=hsl, eb=eb, hc=hc, og=og, reg=reg,
                           first=first: e.matmul(
                        PS[:, reg * 512:(reg + 1) * 512], hs,
                        w2b[eb][:, hc * 1024 + og * 512: hc * 1024 + (og + 1) * 512],
                        start=first, stop=False, skip_group_check=True),
                        w=w0, inc=(hc == 3 and reg == 7))
            t_w2use[m] = tm
        t_eo_hl[s_] = t_w2use[s_ * NE + NE - 1]
        for reg in range(8):
            tc4, og = reg // 2, reg % 2
            gtc = s_ * 4 + tc4
            T(lambda e, gtc=gtc, og=og, reg=reg: e.matmul(
                PS[:, reg * 512:(reg + 1) * 512],
                g_t8[0:8, gtc * 128:(gtc + 1) * 128],
                b2_sb[0:8, og * 512:(og + 1) * 512],
                start=False, stop=False, skip_group_check=True),
                w=[(dG, t_b2)], inc=False)
            for icc in range(8):
                treg[reg] = T(lambda e, icc=icc, gtc=gtc, og=og, reg=reg: e.matmul(
                    PS[:, reg * 512:(reg + 1) * 512],
                    y_fm[:, icc * 2048 + gtc * 128: icc * 2048 + (gtc + 1) * 128],
                    wcT[:, icc * 1024 + og * 512: icc * 1024 + (og + 1) * 512],
                    start=False, stop=(icc == 7), skip_group_check=True),
                    w=[(dS, t_wcT)], inc=(icc == 7))
        for tc4 in range(4):
            gtc = s_ * 4 + tc4
            pe = gtc % 2
            wxl = w_att + ([(sV, t_evac[gtc - 2])] if gtc >= 2 else [])
            txl = SY(lambda e, gtc=gtc, pe=pe: e.dma_start(
                out=xst2[pe], in_=x[gtc * 128:(gtc + 1) * 128, :]), w=wxl)
            tev = None
            for og in range(2):
                reg = tc4 * 2 + og
                wev = [(sT, treg[reg]), (dS, txl)]
                if gtc >= 2:
                    wev.append((sO, t_odma[gtc - 2]))
                tev = V(lambda e, pe=pe, og=og, reg=reg: e.tensor_tensor(
                    out=outst[pe][:, og * 512:(og + 1) * 512],
                    in0=PS[:, reg * 512:(reg + 1) * 512],
                    in1=xst2[pe][:, og * 512:(og + 1) * 512], op=ALU.add), w=wev)
                bfree(reg, sV, tev)
            t_evac[gtc] = tev
            t_odma[gtc] = SYo(lambda e, gtc=gtc, pe=pe: e.dma_start(
                out=out[gtc * 128:(gtc + 1) * 128, :], in_=outst[pe]),
                w=[(sV, tev)])

    if DEBUG:
        wd = [(sO, t_odma[NT - 1])]
        G(lambda e: e.dma_start(out=dbg["d_xn"], in_=xn_fm), w=wd)
        G(lambda e: e.dma_start(out=dbg["d_q"], in_=q_fm), w=wd)
        G(lambda e: e.dma_start(out=dbg["d_k"], in_=k_fm), w=wd)
        G(lambda e: e.dma_start(out=dbg["d_v"], in_=v_tm), w=wd)
        G(lambda e: e.dma_start(out=dbg["d_y"], in_=y_fm), w=wd)
        G(lambda e: e.dma_start(out=dbg["d_g"], in_=g_t8), w=wd)
        G(lambda e: e.dma_start(out=dbg["d_pcs"], in_=pcsS[1][0:32, :]), w=wd)
        G(lambda e: e.dma_start(out=dbg["d_wt"], in_=wTh[1]), w=wd)
        P.add("gpsimd", lambda e: e.nop(), w=[(dG, dG.n)])

    P.add("sync", lambda e: e.nop(), w=[(sO, 16 * NT)])

    with nc.Block() as block:
        @block.sync
        def _(e):
            P.replay("sync", e)

        @block.tensor
        def _(e):
            P.replay("tensor", e)

        @block.vector
        def _(e):
            P.replay("vector", e)

        @block.scalar
        def _(e):
            P.replay("scalar", e)

        @block.gpsimd
        def _(e):
            P.replay("gpsimd", e)

    return nc


_NC_CACHE = None


def kernel(**inputs):
    global _NC_CACHE
    xs = np.asarray(inputs["L0_attn_hidden_states"])
    base = {
        "Wq": np.ascontiguousarray(inputs["L0_attn_Wq"], np.float32),
        "Wk": np.ascontiguousarray(inputs["L0_attn_Wk"], np.float32),
        "Wv": np.ascontiguousarray(inputs["L0_attn_Wv"], np.float32),
        "Wc": np.ascontiguousarray(inputs["L0_attn_Wc"], np.float32),
        "terr": np.ascontiguousarray(inputs["L0_attn_terrace_mix"],
                                     np.float32).reshape(1, SEG),
        "idxd": np.ascontiguousarray(inputs["L0_attn_index_decays"], np.float32),
        "megad": np.ascontiguousarray(inputs["L0_attn_mega_decays"],
                                      np.float32).reshape(1, H),
        "w1": np.ascontiguousarray(inputs["L0_ffn_w1"], np.float32),
        "b1": np.ascontiguousarray(inputs["L0_ffn_b1"], np.float32),
        "w2": np.ascontiguousarray(inputs["L0_ffn_w2"], np.float32),
        "b2": np.ascontiguousarray(inputs["L0_ffn_b2"], np.float32),
        "cent": np.ascontiguousarray(inputs["L0_ffn_centroids"], np.float32),
        "wrt": np.ascontiguousarray(inputs["L0_ffn_w_route"], np.float32),
    }
    for k, v in _host_consts().items():
        base["c_" + k] = v
    if _NC_CACHE is None:
        _NC_CACHE = build()
    nc = _NC_CACHE
    in_maps = []
    for c in range(NCORE):
        m = dict(base)
        m["x"] = np.ascontiguousarray(xs[c], np.float32)
        in_maps.append(m)
    res = run_bass_kernel_spmd(nc, in_maps, core_ids=list(range(NCORE)))
    return np.stack([np.asarray(res.results[c]["out"]) for c in range(NCORE)],
                    axis=0)


# version 02
# speedup vs baseline: 1.1938x; 1.1938x over previous
import numpy as np
import ml_dtypes
import concourse.bass as bass
import concourse.mybir as mybir
from concourse.bass_utils import run_bass_kernel_spmd

F32 = mybir.dt.float32
BF16 = mybir.dt.bfloat16
U8 = mybir.dt.uint8
AF = mybir.ActivationFunctionType
ALU = mybir.AluOpType
AX = mybir.AxisListType.X

L, D, H, SEG = 2048, 1024, 16, 64
NT, NCH = 16, 16
NE = 8
NCORE = 8
LN32 = float(np.log(32.0))

DEBUG = False


def _host_consts():
    bf = ml_dtypes.bfloat16
    f = np.arange(128)
    jj, rr = np.meshgrid(np.arange(32), np.arange(32), indexing="ij")
    e8 = np.zeros((8, 1024), dtype=bf)
    for e in range(8):
        e8[e, e * 128:(e + 1) * 128] = 1
    return {
        "idnf32": np.eye(128, dtype=np.float32),
        "tri01": (f[None, :] >= f[:, None]).astype(bf),
        "strictu": (f[None, :] < f[:, None]).astype(bf),
        "negi": (-30000.0 * np.eye(128)).astype(bf),
        "rep2": (np.arange(128)[None, :] % 64 == np.arange(64)[:, None]).astype(np.float32),
        "emat": np.clip(rr - 1 - jj, 0, 31).astype(np.float32),
        "ltri": (jj < rr).astype(np.float32),
        "e8": e8,
        "onesrow": np.ones((1, 512), dtype=bf),
        "ones64": np.ones((1, 64), dtype=bf),
        "onescol": np.ones((128, 1), dtype=bf),
        "ones32f": np.ones((1, 32), dtype=np.float32),
        "onesf128": np.ones((1, 128), dtype=np.float32),
    }


class Sem:
    __slots__ = ("h", "n")

    def __init__(self, h):
        self.h = h
        self.n = 0


class Tape:
    ENGS = ("sync", "tensor", "vector", "scalar", "gpsimd")

    def __init__(self):
        self.ops = {e: [] for e in self.ENGS}
        self.waited = {e: {} for e in self.ENGS}

    def add(self, eng, fn, w=(), inc=None, amt=1):
        ws = []
        for s, v in w:
            assert v is not None and v > 0, f"bad wait {v} on {eng}"
            k = id(s)
            if self.waited[eng].get(k, 0) >= v:
                continue
            self.waited[eng][k] = v
            ws.append((s.h, v))
        pair, tick = None, None
        if inc is not None:
            inc.n += amt
            tick = inc.n
            pair = (inc.h, amt)
        self.ops[eng].append((fn, tuple(ws), pair))
        return tick

    def replay(self, eng, e):
        for fn, ws, pair in self.ops[eng]:
            for h, v in ws:
                e.wait_ge(h, v)
            inst = fn(e)
            if pair is not None:
                inst.then_inc(pair[0], pair[1])


def _finish_early(nc, P, waits):
    P.add("sync", lambda e: e.nop(), w=[(s, v) for s, v in waits if v > 0])
    with nc.Block() as block:
        @block.sync
        def _(e):
            P.replay("sync", e)

        @block.tensor
        def _(e):
            P.replay("tensor", e)

        @block.vector
        def _(e):
            P.replay("vector", e)

        @block.scalar
        def _(e):
            P.replay("scalar", e)

        @block.gpsimd
        def _(e):
            P.replay("gpsimd", e)
    return nc


def build(stop_stage=99):
    import os
    nc = bass.Bass("TRN2", target_bir_lowering=False,
                   detect_race_conditions=False)
    nc.dram_tensor("salt" + os.environ.get("KSALT", "0"), [1, 1], F32,
                   kind="Internal")
    HC = _host_consts()

    x = nc.dram_tensor("x", [L, D], F32, kind="ExternalInput").ap()
    Wmats = [nc.dram_tensor(n, [D, D], F32, kind="ExternalInput").ap()
             for n in ("Wq", "Wk", "Wv", "Wc")]
    terr = nc.dram_tensor("terr", [1, SEG], F32, kind="ExternalInput").ap()
    idxd = nc.dram_tensor("idxd", [H, SEG], F32, kind="ExternalInput").ap()
    megad = nc.dram_tensor("megad", [1, H], F32, kind="ExternalInput").ap()
    w1d = nc.dram_tensor("w1", [NE, D, 512], F32, kind="ExternalInput").ap()
    b1d = nc.dram_tensor("b1", [NE, 512], F32, kind="ExternalInput").ap()
    w2d = nc.dram_tensor("w2", [NE, 512, D], F32, kind="ExternalInput").ap()
    b2d = nc.dram_tensor("b2", [NE, D], F32, kind="ExternalInput").ap()
    cent = nc.dram_tensor("cent", [NE, D], F32, kind="ExternalInput").ap()
    wrt = nc.dram_tensor("wrt", [NE, D], F32, kind="ExternalInput").ap()
    cin = {k: nc.dram_tensor("c_" + k, list(v.shape),
                             F32 if v.dtype == np.float32 else BF16,
                             kind="ExternalInput").ap()
           for k, v in HC.items()}
    out = nc.dram_tensor("out", [L, D], F32, kind="ExternalOutput").ap()
    wbf = [nc.dram_tensor(f"wbf{i}", [D, D], BF16, kind="Internal").ap()
           for i in range(4)]
    w1bf = nc.dram_tensor("w1bf", [NE, 128, 4096], BF16, kind="Internal").ap()
    w2bf = nc.dram_tensor("w2bf", [NE, 128, 4096], BF16, kind="Internal").ap()
    dbg = {}
    if DEBUG:
        for n, shp, dt in [("d_xn", [128, 16384], BF16), ("d_q", [128, 16384], BF16),
                           ("d_k", [128, 16384], BF16), ("d_v", [128, 16384], BF16),
                           ("d_y", [128, 16384], BF16), ("d_g", [8, 2048], BF16),
                           ("d_pcs", [32, 2048], BF16), ("d_wt", [128, 512], F32)]:
            dbg[n] = nc.dram_tensor(n, shp, dt, kind="ExternalOutput").ap()

    ar_xn = nc.alloc_sbuf_tensor("ar_xn", [128, 32768], U8).ap()
    ar_q = nc.alloc_sbuf_tensor("ar_q", [128, 32768], U8).ap()
    ar_k = nc.alloc_sbuf_tensor("ar_k", [128, 32768], U8).ap()
    ar_v = nc.alloc_sbuf_tensor("ar_v", [128, 32768], U8).ap()
    ar_y = nc.alloc_sbuf_tensor("ar_y", [128, 32768], U8).ap()
    ar_t = nc.alloc_sbuf_tensor("ar_t", [128, 28672], U8).ap()

    xn_fm = ar_xn.bitcast(BF16)
    q_fm = ar_q.bitcast(BF16)
    k_fm = ar_k.bitcast(BF16)
    v_tm = ar_v.bitcast(BF16)
    y_fm = ar_y.bitcast(BF16)
    wTb = [ar_y[:, 0:16384].bitcast(BF16), ar_y[:, 16384:32768].bitcast(BF16)]
    hprime = ar_q.bitcast(BF16)
    wcT = ar_k[:, 0:16384].bitcast(BF16)
    xst2 = [ar_k[:, 16384:20480].bitcast(F32), ar_k[:, 20480:24576].bitcast(F32)]
    outst = [ar_k[:, 24576:28672].bitcast(F32), ar_k[:, 28672:32768].bitcast(F32)]
    w1b = [ar_v[:, 0:8192].bitcast(BF16), ar_v[:, 8192:16384].bitcast(BF16)]
    w2b = [ar_v[:, 16384:24576].bitcast(BF16), ar_v[:, 24576:32768].bitcast(BF16)]

    xstage = [ar_t[:, 0:4096].bitcast(F32), ar_t[:, 4096:8192].bitcast(F32)]
    xn_st = [ar_t[:, 8192:12288].bitcast(F32), ar_t[:, 12288:16384].bitcast(F32)]
    sqdump = ar_t[:, 16384:20480].bitcast(F32)
    lnS = ar_t[:, 20480:24576].bitcast(F32)
    prepS = ar_t[:, 24576:28672].bitcast(F32)
    Ptm = [ar_t[:, 0:4096].bitcast(BF16), ar_t[:, 4096:8192].bitcast(BF16)]
    stripB = Ptm
    band1 = [ar_t[:, 8192:8448].bitcast(BF16), ar_t[:, 8448:8704].bitcast(BF16)]
    band2P = [nc.alloc_sbuf_tensor("band2", [128, 128], BF16).ap(),
              ar_t[:, 8704:8960].bitcast(BF16)]
    pcsS = [ar_t[:, 8960:13056].bitcast(BF16), ar_t[:, 13056:17152].bitcast(BF16)]
    wTh = [ar_t[:, 17152:19200].bitcast(F32), ar_t[:, 19200:21248].bitcast(F32)]
    atS = ar_t[:, 21248:25344].bitcast(F32)
    hgel = [ar_t[:, 0:2048].bitcast(F32), ar_t[:, 2048:4096].bitcast(F32)]
    b1row = ar_t[0:1, 4096:12288].bitcast(BF16)
    b2_sb = ar_t[0:8, 12288:14336].bitcast(BF16)

    csb = {k: nc.alloc_sbuf_tensor("s_" + k, list(HC[k].shape),
                                   F32 if HC[k].dtype == np.float32 else BF16).ap()
           for k in HC}
    g_t8 = nc.alloc_sbuf_tensor("g_t8", [8, 2048], BF16).ap()
    twrow = nc.alloc_sbuf_tensor("twrow", [1, 128], F32).ap()
    owrep2 = nc.alloc_sbuf_tensor("owrep2", [128, 128], F32).ap()
    crep = nc.alloc_sbuf_tensor("crep", [128, 16], F32).ap()
    lnmega = nc.alloc_sbuf_tensor("lnmega", [1, 16], F32).ap()
    cnrb = nc.alloc_sbuf_tensor("cnrb", [128, 128], BF16).ap()
    centF = nc.alloc_sbuf_tensor("centF", [8, 1024], F32).ap()
    routeF = nc.alloc_sbuf_tensor("routeF", [8, 1024], F32).ap()

    msum, sq2 = lnS[:, 0:2], lnS[:, 2:4]
    mmean, varv = lnS[:, 4:6], lnS[:, 6:8]
    rstd, negm = lnS[:, 8:10], lnS[:, 10:12]
    recn, tvar = lnS[:, 12:14], lnS[:, 24:26]
    m1c, m2c, nm1 = lnS[:, 14:16], lnS[:, 16:18], lnS[:, 18:20]
    gsum, grec = lnS[:, 20:22], lnS[:, 22:24]
    lgec = [lnS[:, 32:48], lnS[:, 48:64]]
    lgf = [lnS[:, 64:72], lnS[:, 72:80]]
    ge1 = [lnS[:, 96:104], lnS[:, 104:112]]
    mskd = [lnS[:, 112:120], lnS[:, 120:128]]
    kept = [lnS[:, 128:136], lnS[:, 136:144]]
    egk = [lnS[:, 144:152], lnS[:, 152:160]]
    gtm = [lnS[:, 160:168], lnS[:, 168:176]]

    cn2, lncn, rcn = prepS[0:8, 0:1], prepS[0:8, 1:2], prepS[0:8, 2:3]
    idxF = prepS[0:16, 8:72]
    c16 = prepS[0:16, 72:136]
    callT = prepS[0:64, 136:152]
    megae, megac = prepS[0:1, 152:168], prepS[0:1, 168:184]
    twe, twp, tw64 = prepS[0:1, 184:248], prepS[0:1, 248:312], prepS[0:1, 312:376]
    ow64 = prepS[0:1, 376:440]
    owrow = prepS[0:1, 440:568]
    sq8b = prepS[0:8, 504:1016]

    zparts = atS[:, 0:8]
    Zc, rZc = atS[:, 8:10], atS[:, 10:12]
    EA, ATf = atS[0:32, 16:48], atS[0:32, 48:80]
    rnb32 = atS[0:32, 80:112]
    lnm32, Pcol = atS[0:32, 112:113], atS[0:32, 113:114]
    rn0, rnorm = atS[0:32, 114:115], atS[0:32, 115:116]
    rnrow = atS[0:1, 116:148]
    ztrowP = [atS[0:1, 148:276], prepS[0:1, 0:128]]
    ztr2P = [atS[0:1, 276:404], prepS[0:1, 128:256]]
    rztw_sbP = [atS[:, 404:532], prepS[:, 320:448]]
    tcombP = [atS[:, 532:660], prepS[:, 448:576]]
    t2combP = [atS[:, 660:788], prepS[:, 576:704]]
    ATb = atS[0:32, 800:816].bitcast(BF16)
    A2b = atS[0:32, 816:832].bitcast(BF16)
    Pcolb = atS[0:32, 832:833].bitcast(BF16)
    rztwbP = [atS[0:1, 836:900].bitcast(BF16),
              ar_t[0:1, 25600:25856].bitcast(BF16)]
    cvec = atS[:, 900:916].bitcast(BF16)

    PS = nc.alloc_psum_tensor("PS", [128, 4096], F32).ap()

    sT, sV, sS = Sem(nc.alloc_semaphore("sT")), Sem(nc.alloc_semaphore("sV")), \
        Sem(nc.alloc_semaphore("sS"))
    dS, dG, sO = Sem(nc.alloc_semaphore("dS")), Sem(nc.alloc_semaphore("dG")), \
        Sem(nc.alloc_semaphore("sO"))

    P = Tape()

    def T(fn, w=(), inc=True):
        return P.add("tensor", fn, w, sT if inc else None)

    def V(fn, w=(), inc=True):
        return P.add("vector", fn, w, sV if inc else None)

    def S(fn, w=(), inc=True):
        return P.add("scalar", fn, w, sS if inc else None)

    def SY(fn, w=()):
        return P.add("sync", fn, w, dS, 16)

    def SYo(fn, w=()):
        return P.add("sync", fn, w, sO, 16)

    def G(fn, w=()):
        return P.add("gpsimd", fn, w, dG, 16)

    def Tdrain():
        P.add("tensor", lambda e: e.nop(), w=[(sT, sT.n)] if sT.n else [])

    bank_free = {b: [] for b in range(8)}

    def bfree(b, sem, tick):
        bank_free[b] = [(sem, tick)]

    def bwait(*banks):
        ws = []
        for b in banks:
            ws += bank_free[b]
        return ws

    for k in HC:
        SY(lambda e, d=csb[k], s=cin[k]: e.dma_start(out=d, in_=s))
    SY(lambda e: e.dma_start(out=idxF, in_=idxd))
    SY(lambda e: e.dma_start(out=megae, in_=megad))
    SY(lambda e: e.dma_start(out=twe, in_=terr))
    SY(lambda e: e.dma_start(out=centF, in_=cent))
    SY(lambda e: e.dma_start(out=routeF, in_=wrt))
    t_prein = dS.n
    tw_wbf = [G(lambda e, d=wbf[i], s=Wmats[i]: e.dma_start(out=d, in_=s))
              for i in range(4)]
    t_w1bf, t_w2bf = [], []
    for e_ in range(NE):
        t_w1bf.append(G(lambda e, i=e_: e.dma_start(
            out=w1bf[i], in_=w1d[i].rearrange("(ic p) h -> p ic h", p=128))))
        t_w2bf.append(G(lambda e, i=e_: e.dma_start(
            out=w2bf[i], in_=w2d[i].rearrange("(hc p) o -> p hc o", p=128))))

    wp = [(dS, t_prein)]
    a1 = S(lambda e: e.activation(out=sq8b, in_=centF[:, 0:512], func=AF.Square,
                                  accum_out=cn2), w=wp)
    a2 = S(lambda e: e.activation(out=sq8b, in_=centF[:, 512:1024], func=AF.Square,
                                  accum_out=lncn), w=[(sS, a1)])
    a3 = V(lambda e: e.tensor_tensor(out=cn2, in0=cn2, in1=lncn, op=ALU.add),
           w=[(sS, a2)])
    a4 = S(lambda e: e.activation(out=lncn, in_=cn2, func=AF.Ln), w=[(sV, a3)])
    a5 = S(lambda e: e.activation(out=rcn, in_=lncn, func=AF.Exp, scale=-0.5),
           w=[(sS, a4)])
    a6 = V(lambda e: e.tensor_scalar(out=centF, in0=centF, scalar1=rcn,
                                     scalar2=1.0 / 32.0, op0=ALU.mult,
                                     op1=ALU.mult), w=[(sS, a5)])
    tt = None
    for ic in range(8):
        T(lambda e, ic=ic: e.transpose(
            PS[:, 3584 + ic * 16:3592 + ic * 16],
            centF[0:8, ic * 128:(ic + 1) * 128], csb["idnf32"][0:8, 0:8]),
            w=[(sV, a6)], inc=False)
        tt = T(lambda e, ic=ic: e.transpose(
            PS[:, 3592 + ic * 16:3600 + ic * 16],
            routeF[0:8, ic * 128:(ic + 1) * 128], csb["idnf32"][0:8, 0:8]),
            w=[(dS, t_prein)], inc=(ic == 7))
    a7 = V(lambda e: e.tensor_copy(out=cnrb, in_=PS[:, 3584:3712]), w=[(sT, tt)])
    bfree(7, sV, a7)
    b1_ = S(lambda e: e.activation(out=c16, in_=idxF, func=AF.Exp), w=wp)
    b2_ = T(lambda e: e.transpose(PS[0:64, 3072:3088], c16[0:16, 0:64],
                                  csb["idnf32"][0:16, 0:16]), w=[(sS, b1_)])
    b3 = V(lambda e: e.tensor_copy(out=callT, in_=PS[0:64, 3072:3088]), w=[(sT, b2_)])
    b4 = T(lambda e: e.matmul(PS[:, 3088:3104], csb["rep2"], callT,
                              start=True, stop=True), w=[(sV, b3)])
    t_crep = V(lambda e: e.tensor_copy(out=crep, in_=PS[:, 3088:3104]), w=[(sT, b4)])
    c1_ = S(lambda e: e.activation(out=megac, in_=megae, func=AF.Exp, scale=64.0), w=wp)
    c2_ = V(lambda e: e.tensor_scalar_min(out=megac, in0=megac, scalar1=2.0),
            w=[(sS, c1_)])
    t_lnmega = S(lambda e: e.activation(out=lnmega, in_=megac, func=AF.Ln),
                 w=[(sV, c2_)])
    d1 = S(lambda e: e.activation(out=twp, in_=twe, func=AF.Exp, scale=-1.0), w=wp)
    d2 = V(lambda e: e.tensor_scalar_add(out=twp, in0=twp, scalar1=1.0), w=[(sS, d1)])
    d3 = V(lambda e: e.reciprocal(out=tw64, in_=twp), w=[(sV, d2)])
    d4 = V(lambda e: e.tensor_copy(out=twrow[0:1, 0:64], in_=tw64), w=[(sV, d3)])
    t_twrow = V(lambda e: e.tensor_copy(out=twrow[0:1, 64:128], in_=tw64),
                w=[(sV, d3)])
    d6 = V(lambda e: e.tensor_scalar(out=ow64, in0=tw64, scalar1=-1.0, scalar2=1.0,
                                     op0=ALU.mult, op1=ALU.add), w=[(sV, d3)])
    d7 = V(lambda e: e.tensor_copy(out=owrow[0:1, 0:64], in_=ow64), w=[(sV, d6)])
    d8 = V(lambda e: e.tensor_copy(out=owrow[0:1, 64:128], in_=ow64), w=[(sV, d6)])
    d9 = T(lambda e: e.matmul(PS[:, 3104:3232], csb["onesf128"], owrow,
                              start=True, stop=True), w=[(sV, d7), (sV, d8)])
    t_owrep = V(lambda e: e.tensor_copy(out=owrep2, in_=PS[:, 3104:3232]),
                w=[(sT, d9)])
    bfree(6, sV, t_owrep)

    t_xn = [0] * NT
    t_lnS = [0] * NT
    t_lnV = [0] * NT
    t_xtr = [0] * NT
    for tc in range(NT):
        par = tc % 2
        w_x = [(sS, t_lnS[tc - 2]), (sV, t_lnV[tc - 2])] if tc >= 2 else []
        tx = SY(lambda e, tc=tc, p=par: e.dma_start(
            out=xstage[p], in_=x[tc * 128:(tc + 1) * 128, :]), w=w_x)
        wx = [(dS, tx)]
        va = V(lambda e, p=par: e.reduce_sum(out=msum[:, p:p + 1], in_=xstage[p],
                                             axis=AX), w=wx)
        sa = S(lambda e, p=par: e.activation(out=sqdump, in_=xstage[p],
                                             func=AF.Square,
                                             accum_out=sq2[:, p:p + 1]), w=wx)
        vb = V(lambda e, p=par: e.tensor_scalar_mul(
            out=mmean[:, p:p + 1], in0=msum[:, p:p + 1], scalar1=-1.0 / D),
            w=[(sV, va)])
        vc = V(lambda e, p=par: e.tensor_scalar(
            out=tvar[:, p:p + 1], in0=mmean[:, p:p + 1], scalar1=mmean[:, p:p + 1],
            scalar2=None, op0=ALU.mult), w=[(sV, vb)])
        vd = V(lambda e, p=par: e.tensor_scalar_mul(
            out=varv[:, p:p + 1], in0=sq2[:, p:p + 1], scalar1=1.0 / D),
            w=[(sS, sa), (sV, vc)])
        ve = V(lambda e, p=par: e.tensor_tensor(
            out=varv[:, p:p + 1], in0=varv[:, p:p + 1], in1=tvar[:, p:p + 1],
            op=ALU.subtract), w=[(sV, vd)])
        vf = V(lambda e, p=par: e.tensor_scalar_add(
            out=tvar[:, p:p + 1], in0=varv[:, p:p + 1], scalar1=1e-5), w=[(sV, ve)])
        sb_ = S(lambda e, p=par: e.activation(out=rstd[:, p:p + 1],
                                              in_=tvar[:, p:p + 1], func=AF.Ln),
                w=[(sV, vf)])
        sc_ = S(lambda e, p=par: e.activation(out=rstd[:, p:p + 1],
                                              in_=rstd[:, p:p + 1], func=AF.Exp,
                                              scale=-0.5), w=[(sS, sb_)])
        vh = V(lambda e, p=par: e.tensor_scalar(
            out=recn[:, p:p + 1], in0=varv[:, p:p + 1], scalar1=rstd[:, p:p + 1],
            scalar2=rstd[:, p:p + 1], op0=ALU.mult, op1=ALU.mult), w=[(sS, sc_)])
        si = S(lambda e, p=par: e.activation(out=recn[:, p:p + 1],
                                             in_=recn[:, p:p + 1], func=AF.Ln),
               w=[(sV, vh)])
        sj = S(lambda e, p=par: e.activation(out=recn[:, p:p + 1],
                                             in_=recn[:, p:p + 1], func=AF.Exp,
                                             scale=-0.5), w=[(sS, si)])
        w_xn = [(sS, sc_)] + ([(sT, t_xtr[tc - 2])] if tc >= 2 else [])
        sk = V(lambda e, p=par: e.tensor_scalar(
            out=xn_st[p], in0=xstage[p], scalar1=mmean[:, p:p + 1],
            scalar2=rstd[:, p:p + 1], op0=ALU.add, op1=ALU.mult), w=w_xn)
        t_lnS[tc] = sa
        base = par * 1024
        wtr = [(sV, sk)] + (bwait(par * 2, par * 2 + 1) if tc >= 2 else [])
        ttr = None
        for ic in range(8):
            ttr = T(lambda e, p=par, ic=ic, b=base: e.transpose(
                PS[:, b + ic * 128: b + (ic + 1) * 128],
                xn_st[p][:, ic * 128:(ic + 1) * 128], csb["idnf32"]),
                w=wtr, inc=(ic == 7))
        t_xtr[tc] = ttr
        vk = V(lambda e, tc=tc, b=base: e.tensor_copy(
            out=xn_fm.rearrange("p (ic t) -> p ic t", ic=8)[:, :, tc * 128:(tc + 1) * 128],
            in_=PS[:, b:b + 1024].rearrange("p (ic j) -> p ic j", ic=8)),
            w=[(sT, ttr)])
        t_xn[tc] = vk
        bfree(par * 2, sV, vk)
        bfree(par * 2 + 1, sV, vk)
        rb = 2048 + par * 512
        wr = [(sV, vk), (sV, a7)] + (bwait(4 + par) if tc >= 2 else [])
        trm = None
        for ic in range(8):
            trm = T(lambda e, ic=ic, tc=tc, rb=rb: e.matmul(
                PS[:, rb:rb + 16],
                xn_fm[:, ic * 2048 + tc * 128: ic * 2048 + (tc + 1) * 128],
                cnrb[:, ic * 16:(ic + 1) * 16],
                start=(ic == 0), stop=(ic == 7), skip_group_check=True),
                w=wr, inc=(ic == 7))
        r1 = V(lambda e, p=par, rb=rb: e.tensor_copy(out=lgec[p], in_=PS[:, rb:rb + 16]),
               w=[(sT, trm)])
        bfree(4 + par, sV, r1)
        r2 = V(lambda e, p=par: e.scalar_tensor_tensor(
            out=lgf[p], in0=lgec[p][:, 0:8], scalar=recn[:, p:p + 1],
            in1=lgec[p][:, 8:16], op0=ALU.mult, op1=ALU.add),
            w=[(sV, r1), (sS, sj)])
        r3 = V(lambda e, p=par: e.reduce_max(out=m1c[:, p:p + 1], in_=lgf[p], axis=AX),
               w=[(sV, r2)])
        r4 = V(lambda e, p=par: e.tensor_scalar(
            out=ge1[p], in0=lgf[p], scalar1=m1c[:, p:p + 1], scalar2=None,
            op0=ALU.is_ge), w=[(sV, r3)])
        r5 = V(lambda e, p=par: e.scalar_tensor_tensor(
            out=mskd[p], in0=ge1[p], scalar=-1e30, in1=lgf[p],
            op0=ALU.mult, op1=ALU.add), w=[(sV, r4)])
        r6 = V(lambda e, p=par: e.reduce_max(out=m2c[:, p:p + 1], in_=mskd[p], axis=AX),
               w=[(sV, r5)])
        r7 = V(lambda e, p=par: e.tensor_scalar(
            out=kept[p], in0=lgf[p], scalar1=m2c[:, p:p + 1], scalar2=None,
            op0=ALU.is_ge), w=[(sV, r6)])
        r8 = V(lambda e, p=par: e.tensor_scalar_mul(
            out=nm1[:, p:p + 1], in0=m1c[:, p:p + 1], scalar1=-1.0), w=[(sV, r3)])
        r9 = S(lambda e, p=par: e.activation(out=egk[p], in_=lgf[p], func=AF.Exp,
                                             bias=nm1[:, p:p + 1]), w=[(sV, r8)])
        r10 = V(lambda e, p=par: e.tensor_tensor(out=egk[p], in0=egk[p], in1=kept[p],
                                                 op=ALU.mult), w=[(sS, r9), (sV, r7)])
        r11 = V(lambda e, p=par: e.reduce_sum(out=gsum[:, p:p + 1], in_=egk[p], axis=AX),
                w=[(sV, r10)])
        r12 = V(lambda e, p=par: e.reciprocal(out=grec[:, p:p + 1],
                                              in_=gsum[:, p:p + 1]), w=[(sV, r11)])
        r13 = V(lambda e, p=par: e.tensor_scalar_mul(
            out=gtm[p], in0=egk[p], scalar1=grec[:, p:p + 1]), w=[(sV, r12)])
        gb = 3072 + par * 128
        r14 = T(lambda e, p=par, gb=gb: e.transpose(
            PS[0:8, gb:gb + 128], gtm[p], csb["idnf32"]), w=[(sV, r13)] + bwait(6))
        r15 = V(lambda e, tc=tc, gb=gb: e.tensor_copy(
            out=g_t8[0:8, tc * 128:(tc + 1) * 128], in_=PS[0:8, gb:gb + 128]),
            w=[(sT, r14)])
        t_lnV[tc] = r15
        bfree(6, sV, r15)
    t_ln_done = t_lnV[NT - 1]
    t_xn_all = t_xn[NT - 1]
    if stop_stage <= 1:
        return _finish_early(nc, P, [(sT, sT.n), (sV, sV.n), (sS, sS.n),
                                     (dS, dS.n), (dG, dG.n)])

    t_wT = [0, 0, 0]
    for m in range(2):
        tk = None
        for ic in range(8):
            tk = SY(lambda e, m=m, ic=ic: e.dma_start(
                out=wTb[m][:, ic * 1024:(ic + 1) * 1024],
                in_=wbf[m][:, ic * 128:(ic + 1) * 128], transpose=True),
                w=[(dG, tw_wbf[m])])
        t_wT[m] = tk

    def mm_proj(dst_fm, wsrc, t_wready):
        tick_last = None
        for oc in range(8):
            for tg in range(4):
                bk = (oc * 4 + tg) % 2
                w0 = [(dS, t_wready), (sV, t_xn_all)] + bwait(bk)
                tm = None
                for ic in range(8):
                    tm = T(lambda e, oc=oc, tg=tg, ic=ic, bk=bk, ws=wsrc: e.matmul(
                        PS[:, bk * 512:(bk + 1) * 512],
                        ws[:, ic * 1024 + oc * 128: ic * 1024 + (oc + 1) * 128],
                        xn_fm[:, ic * 2048 + tg * 512: ic * 2048 + (tg + 1) * 512],
                        start=(ic == 0), stop=(ic == 7), skip_group_check=True),
                        w=w0, inc=(ic == 7))
                tv_ = V(lambda e, oc=oc, tg=tg, bk=bk, d=dst_fm: e.tensor_copy(
                    out=d[:, oc * 2048 + tg * 512: oc * 2048 + (tg + 1) * 512],
                    in_=PS[:, bk * 512:(bk + 1) * 512]), w=[(sT, tm)])
                bfree(bk, sV, tv_)
                tick_last = tm
        return tick_last

    t_qmm = mm_proj(q_fm, wTb[0], t_wT[0])
    mm_proj(k_fm, wTb[1], t_wT[1])
    tk = None
    for ic in range(8):
        tk = SY(lambda e, ic=ic: e.dma_start(
            out=wTb[0][:, ic * 1024:(ic + 1) * 1024],
            in_=wbf[2][:, ic * 128:(ic + 1) * 128], transpose=True),
            w=[(dG, tw_wbf[2]), (sT, t_qmm)])
    t_wT[2] = tk
    for tc in range(NT):
        for og in range(2):
            bk = (tc * 2 + og) % 2
            w0 = [(dS, t_wT[2])] + bwait(bk)
            tm = None
            for ic in range(8):
                tm = T(lambda e, tc=tc, og=og, ic=ic, bk=bk: e.matmul(
                    PS[:, bk * 512:(bk + 1) * 512],
                    xn_fm[:, ic * 2048 + tc * 128: ic * 2048 + (tc + 1) * 128],
                    wTb[0][:, ic * 1024 + og * 512: ic * 1024 + (og + 1) * 512],
                    start=(ic == 0), stop=(ic == 7), skip_group_check=True),
                    w=w0, inc=(ic == 7))
            tv_ = V(lambda e, tc=tc, og=og, bk=bk: e.tensor_copy(
                out=v_tm[:, tc * 1024 + og * 512: tc * 1024 + (og + 1) * 512],
                in_=PS[:, bk * 512:(bk + 1) * 512]), w=[(sT, tm)])
            bfree(bk, sV, tv_)
    t_qkv_T = sT.n
    if stop_stage <= 2:
        return _finish_early(nc, P, [(sT, sT.n), (sV, sV.n), (sS, sS.n),
                                     (dS, dS.n), (dG, dG.n)])

    war_pt = [[], []]
    war_cvec = []
    war_pcs = [[], []]
    war_wTh = [[], []]
    war_band1 = [[], []]
    war_band2 = [[], []]
    war_zt = [[], []]
    war_rz = [[], []]
    war_bandacc = [[], []]
    t_comb_prev = 0

    tbz0 = V(lambda e: e.memset(band1[0], 0.0), w=[(sT, t_qkv_T)])
    tbz1 = V(lambda e: e.memset(band1[1], 0.0), w=[(sT, t_qkv_T)])
    war_band1 = [[(sV, tbz0)], [(sV, tbz1)]]

    for h in range(H):
        hp = h % 2
        hb = hp * 64
        icb = (h // 2) * 2048
        tp = (0, 64) if hp else None
        qsl_all = q_fm[hb:hb + 64, icb:icb + 2048]
        ksl_all = k_fm[hb:hb + 64, icb:icb + 2048]

        import os as _os2
        _kmin = _os2.environ.get("KMIN", "")
        if _kmin:
            t_wTr = V(lambda e, hp=hp: e.memset(wTh[hp], 0.001),
                      w=[(sT, t_qkv_T)])
            t_pcs = [sT.n if sT.n else 1] * NT
            t_pcsev = t_wTr
            twt = sT.n if sT.n else 1
        _kmin_noA = _kmin in ("1", "3")
        _kmin_noScan = _kmin in ("1", "2")
        Tdrain()
        if _kmin_noA if _kmin else False:
            tcv = None
        else:
            tcv = V(lambda e: e.memset(cvec, 0.0), w=war_cvec)
        t_pcs = [0] * NT
        t_exp = {}
        for i in ([] if (_kmin and _kmin_noA) else range(NT)):
            W = 128 * (i + 1)
            par = i % 2
            ngrp = (W + 511) // 512
            qsl = qsl_all[:, i * 128:(i + 1) * 128]
            for Gi in range(ngrp):
                n0, n1 = 512 * Gi, min(W, 512 * Gi + 512)
                w0 = bwait(Gi)
                if h == 0 and i == 0 and Gi == 0:
                    w0 = w0 + [(sT, t_qkv_T)]
                lastg = (Gi == ngrp - 1)
                if not lastg:
                    tsm = T(lambda e, n0=n0, n1=n1, q=qsl, ks=ksl_all: e.matmul(
                        PS[:, n0:n1], q, ks[:, n0:n1],
                        start=True, stop=True, skip_group_check=True), w=w0)
                else:
                    T(lambda e, n0=n0, n1=n1, q=qsl, ks=ksl_all: e.matmul(
                        PS[:, n0:n1], q, ks[:, n0:n1],
                        start=True, stop=False, skip_group_check=True),
                        w=w0, inc=False)
                    tsm = T(lambda e, i=i: e.matmul(
                        PS[:, i * 128:(i + 1) * 128], csb["strictu"], csb["negi"],
                        start=False, stop=True, skip_group_check=True))
                wx = [(sT, tsm)]
                if i >= 2:
                    wx.append((sT, t_pcs[i - 2]))
                wx += war_pt[par]
                te = S(lambda e, p=par, n0=n0, n1=n1, g=Gi: e.activation(
                    out=Ptm[p][:, n0:n1], in_=PS[:, n0:n1], func=AF.Exp,
                    scale=0.125, accum_out=zparts[:, p * 4 + g: p * 4 + g + 1]),
                    w=wx)
                t_exp[(i, Gi)] = te
                bfree(Gi, sS, te)
            war_pt[par] = []
            z1 = V(lambda e, p=par, ng=ngrp: e.reduce_sum(
                out=Zc[:, p:p + 1], in_=zparts[:, p * 4: p * 4 + ng], axis=AX),
                w=[(sS, t_exp[(i, ngrp - 1)])])
            z2 = V(lambda e, p=par: e.reciprocal(out=rZc[:, p:p + 1],
                                                 in_=Zc[:, p:p + 1]), w=[(sV, z1)])
            wc = [(sV, z2), (sV, tcv), (sV, t_crep)]
            if i > 0:
                V(lambda e, i=i: e.memset(cvec[:, 2 * i - 2: 2 * i], 0.0),
                  w=[(sT, t_pcs[i - 1])], inc=False)
            cv1 = V(lambda e, i=i, h=h, p=par: e.tensor_tensor(
                out=cvec[0:64, 2 * i: 2 * i + 1], in0=crep[0:64, h:h + 1],
                in1=rZc[0:64, p:p + 1], op=ALU.mult), w=wc)
            cv2 = V(lambda e, i=i, h=h, p=par: e.tensor_tensor(
                out=cvec[64:128, 2 * i + 1: 2 * i + 2], in0=crep[64:128, h:h + 1],
                in1=rZc[64:128, p:p + 1], op=ALU.mult), w=wc)
            tpc = None
            for Gi in range(ngrp):
                n0, n1 = 512 * Gi, min(W, 512 * Gi + 512)
                first = (i == 4 * Gi)
                wpc = [(sV, cv1), (sV, cv2), (sS, t_exp[(i, Gi)])]
                if first and n1 < 512 * (Gi + 1):
                    tz = V(lambda e, p=par, n1=n1, Gi=Gi: e.memset(
                        Ptm[p][:, n1:512 * (Gi + 1)], 0.0),
                        w=[(sT, t_pcs[i - 2])] if i >= 2 else [])
                    wpc.append((sV, tz))
                    n1 = 512 * (Gi + 1)
                if first:
                    wpc += bwait(4 + Gi) + war_pcs[hp]
                tpc = T(lambda e, n0=n0, n1=n1, first=first, i=i: e.matmul(
                    PS[0:32, 2048 + n0: 2048 + n1], cvec, Ptm[i % 2][:, n0:n1],
                    start=first, stop=(i == NT - 1), skip_group_check=True),
                    w=wpc, inc=(Gi == ngrp - 1))
            t_pcs[i] = tpc
        if _kmin and _kmin_noA:
            pass
        else:
            war_cvec = [(sT, t_pcs[NT - 1])]
        if (not _kmin) or not _kmin_noScan:
            if stop_stage == 21 and h == 0:
                return _finish_early(nc, P, [(sT, sT.n), (sV, sV.n), (sS, sS.n),
                                             (dS, dS.n), (dG, dG.n)])
            tevp = None
            for g in range(4):
                tevp = V(lambda e, g=g, hp=hp: e.tensor_copy(
                    out=pcsS[hp][0:32, 512 * g: 512 * (g + 1)],
                    in_=PS[0:32, 2048 + 512 * g: 2048 + 512 * (g + 1)]),
                    w=[(sT, t_pcs[NT - 1])] + war_pcs[hp])
                bfree(4 + g, sV, tevp)
            t_pcsev = tevp
            war_pcs[hp] = []

            s1_ = T(lambda e, h=h: e.matmul(PS[0:32, 3584:3585], csb["ones32f"],
                                            lnmega[0:1, h:h + 1], start=True, stop=True,
                                            skip_group_check=True),
                    w=[(sS, t_lnmega)] + bwait(7))
            s2_ = V(lambda e: e.tensor_copy(out=lnm32, in_=PS[0:32, 3584:3585]),
                    w=[(sT, s1_)])
            s3_ = V(lambda e: e.tensor_scalar_mul(out=EA, in0=csb["emat"], scalar1=lnm32),
                    w=[(sV, s2_)])
            s4_ = S(lambda e: e.activation(out=EA, in_=EA, func=AF.Exp), w=[(sV, s3_)])
            s5_ = V(lambda e: e.tensor_tensor(out=ATf, in0=EA, in1=csb["ltri"],
                                              op=ALU.mult), w=[(sS, s4_)])
            s6_ = V(lambda e: e.tensor_copy(out=ATb, in_=ATf), w=[(sV, s5_)])
            s7_ = V(lambda e, hp=hp: e.reduce_sum(out=Pcol, in_=pcsS[hp][0:32, :], axis=AX),
                    w=[(sV, t_pcsev)])
            s8_ = V(lambda e: e.tensor_copy(out=Pcolb[0:32, 0:1], in_=Pcol), w=[(sV, s7_)])
            s9_ = T(lambda e: e.matmul(PS[0:32, 3585:3586], ATb, Pcolb[0:32, 0:1],
                                       start=True, stop=True, skip_group_check=True),
                    w=[(sV, s6_), (sV, s8_)])
            s10_ = V(lambda e: e.tensor_scalar_add(out=rn0, in0=PS[0:32, 3585:3586],
                                                   scalar1=1e-5), w=[(sT, s9_)])
            s11_ = V(lambda e: e.reciprocal(out=rnorm, in_=rn0), w=[(sV, s10_)])
            s12_ = T(lambda e: e.transpose(PS[0:1, 3586:3618], rnorm,
                                           csb["idnf32"][0:32, 0:32]), w=[(sV, s11_)])
            s13_ = V(lambda e: e.tensor_copy(out=rnrow, in_=PS[0:1, 3586:3618]),
                     w=[(sT, s12_)])
            s14_ = T(lambda e: e.matmul(PS[0:32, 3618:3650], csb["ones32f"], rnrow,
                                        start=True, stop=True, skip_group_check=True),
                     w=[(sV, s13_)])
            s15_ = V(lambda e: e.tensor_copy(out=rnb32, in_=PS[0:32, 3618:3650]),
                     w=[(sT, s14_)])
            s16_ = V(lambda e: e.tensor_tensor(out=A2b, in0=ATf, in1=rnb32, op=ALU.mult),
                     w=[(sV, s15_)])
            bfree(7, sV, s15_)
            twt = None
            for c in range(NCH):
                twt = T(lambda e, c=c, hp=hp: e.matmul(
                    PS[:, 3072 + 32 * c: 3072 + 32 * (c + 1)],
                    pcsS[hp][0:32, 128 * c: 128 * (c + 1)], A2b,
                    start=True, stop=True, skip_group_check=True),
                    w=[(sV, s16_)] + (bwait(6) if c == 0 else []), inc=(c == NCH - 1))
            war_pcs[hp] = [(sT, twt)]
            t_wTr = V(lambda e, hp=hp: e.tensor_copy(out=wTh[hp], in_=PS[:, 3072:3584]),
                      w=[(sT, twt)] + war_wTh[hp])
            if stop_stage == 22 and h == 0:
                return _finish_early(nc, P, [(sT, sT.n), (sV, sV.n), (sS, sS.n),
                                             (dS, dS.n), (dG, dG.n)])

        Tdrain()
        sfree = {0: list(bank_free[0]), 1: list(bank_free[1])}
        t_stm = {}
        t_wg = {}
        t_pv = [0] * NCH
        t_bmm = [0] * NCH
        t_zt = [0] * NCH
        t_ztread = [0] * NCH
        t_rzev = [0] * NCH
        t_cmbm = [0] * NCH
        t_cmb = [0] * NCH
        t_b1w = [0] * NCH
        for c in range(NCH):
            Gmin = c // 4
            Tdrain()
            ksl = ksl_all[:, c * 128:(c + 1) * 128]
            G2b = Gmin
            G1b = (128 * c + 128) // 512 if c < NCH - 1 else -1
            off0 = 128 * c - 512 * G2b
            sb0 = G2b % 2
            tb2 = None
            tb1 = None
            for Gi in range(Gmin, 4):
                n0, n1 = max(128 * c, 512 * Gi), 512 * (Gi + 1)
                sb = Gi % 2
                off = n0 - 512 * Gi
                w0 = list(sfree[sb])
                sfree[sb] = []
                tsm = T(lambda e, n0=n0, n1=n1, sb=sb, off=off, k=ksl,
                        qs=qsl_all: e.matmul(
                    PS[:, sb * 512 + off: sb * 512 + off + (n1 - n0)],
                    k, qs[:, n0:n1], start=True, stop=True, skip_group_check=True),
                    w=w0)
                t_stm[(c, Gi)] = tsm
                wact = []
                if Gi == G2b:
                    wb2 = [(sT, tsm)]
                    if c > 1:
                        wb2 += [(sT, t_bmm[c - 2]), (sT, t_zt[c - 2])]
                    else:
                        wb2 += war_band2[c % 2]
                    tb2 = S(lambda e, sb0=sb0, off0=off0, c=c: e.activation(
                        out=band2P[c % 2],
                        in_=PS[:, sb0 * 512 + off0: sb0 * 512 + off0 + 128],
                        func=AF.Exp, scale=0.125), w=wb2)
                    sfree[sb0].append((sS, tb2))
                    wact.append((sS, tb2))
                if Gi == G1b:
                    colB1 = 128 * c + 128
                    off1 = colB1 - 512 * G1b
                    sb1 = G1b % 2
                    wb1 = [(sT, tsm)]
                    if c > 0:
                        wb1 += [(sT, t_bmm[c - 1]), (sT, t_zt[c - 1])]
                    else:
                        wb1 += war_band1[(c + 1) % 2]
                    tb1 = S(lambda e, sb1=sb1, off1=off1, c=c: e.activation(
                        out=band1[(c + 1) % 2][64:128, 0:64],
                        in_=PS[64:128, sb1 * 512 + off1: sb1 * 512 + off1 + 64],
                        func=AF.Exp, scale=0.125), w=wb1)
                    t_b1w[c + 1] = tb1
                    sfree[sb1].append((sS, tb1))
                    wact.append((sS, tb1))
                nr = (n1 - n0) // 64
                r0 = n0 // 64
                wwg = [(sT, tsm), (sV, t_wTr)] + wact
                if c >= 2:
                    wwg.append((sT, t_pv[c - 2]))
                wwg += war_pt[c % 2]
                twg = V(lambda e, n0=n0, n1=n1, sb=sb, off=off, nr=nr, r0=r0,
                        c=c, hp=hp: e.tensor_tensor(
                    out=stripB[c % 2][:, n0:n1].rearrange("p (r z) -> p r z", z=64),
                    in0=PS[:, sb * 512 + off: sb * 512 + off + (n1 - n0)]
                        .rearrange("p (r z) -> p r z", z=64),
                    in1=wTh[hp][:, 32 * c + r0: 32 * c + r0 + nr]
                        .unsqueeze(2).broadcast_to([128, nr, 64]),
                    op=ALU.mult), w=wwg)
                t_wg[(c, Gi)] = twg
                sfree[sb].append((sV, twg))
            war_pt[c % 2] = []
            tb2m = V(lambda e, c=c: e.tensor_tensor(out=band2P[c % 2],
                                               in0=band2P[c % 2],
                                               in1=csb["tri01"],
                                               op=ALU.mult), w=[(sS, tb2)])
            if stop_stage == 51 and h == 0 and c == 0:
                return _finish_early(nc, P, [(sT, sT.n), (sV, sV.n), (sS, sS.n),
                                             (dS, dS.n), (dG, dG.n)])
            tpv = None
            for Qd in range(Gmin, 4):
                n0q, n1q = max(512 * Qd, 128 * c), 512 * (Qd + 1)
                qoff = n0q - 512 * Qd
                wpv = [(sV, t_wg[(c, Qd)])]
                if c == 0:
                    wpv += bwait(2 + Qd)
                    if t_comb_prev:
                        wpv.append((sV, t_comb_prev))
                vsl = v_tm[:, c * 1024 + h * 64: c * 1024 + (h + 1) * 64]
                tpv = T(lambda e, Qd=Qd, n0q=n0q, n1q=n1q, qoff=qoff, c=c,
                        vs=vsl, hb=hb, tp=tp: e.matmul(
                    PS[hb:hb + 64,
                       1024 + 512 * Qd + qoff: 1024 + 512 * Qd + qoff + (n1q - n0q)],
                    vs, stripB[c % 2][:, n0q:n1q],
                    start=(c == 0), stop=(c == 4 * Qd + 3), skip_group_check=True,
                    tile_position=tp), w=wpv, inc=(Qd == 3))
            t_pv[c] = tpv
            if stop_stage == 52 and h == 0 and c == 0:
                return _finish_early(nc, P, [(sT, sT.n), (sV, sV.n), (sS, sS.n),
                                             (dS, dS.n), (dG, dG.n)])
            bac = 3072 + (c % 2) * 128
            wbm = [(sV, tb2m)]
            if c > 1:
                wbm += [(sV, t_cmbm[c - 2])]
            else:
                wbm += war_bandacc[c % 2] + bwait(6)
            if c > 0:
                wbm += [(sS, t_b1w[c])]
                vsl1 = v_tm[:, (c - 1) * 1024 + h * 64: (c - 1) * 1024 + (h + 1) * 64]
                T(lambda e, c=c, vs=vsl1, hb=hb, tp=tp, bac=bac: e.matmul(
                    PS[hb:hb + 64, bac:bac + 128], vs, band1[c % 2],
                    start=True, stop=False, skip_group_check=True,
                    tile_position=tp), w=wbm, inc=False)
            vsl = v_tm[:, c * 1024 + h * 64: c * 1024 + (h + 1) * 64]
            t_bmm[c] = T(lambda e, c=c, vs=vsl, hb=hb, tp=tp, bac=bac: e.matmul(
                PS[hb:hb + 64, bac:bac + 128], vs, band2P[c % 2],
                start=(c <= 1), stop=True, skip_group_check=True,
                tile_position=tp), w=wbm)
            if stop_stage == 53 and h == 0 and c == 0:
                return _finish_early(nc, P, [(sT, sT.n), (sV, sV.n), (sS, sS.n),
                                             (dS, dS.n), (dG, dG.n)])
            Tdrain()
            ztc = 3712 + (c % 2) * 128
            wzt = [(sV, tb2m)]
            if c > 1:
                wzt += [(sV, t_ztread[c - 2])]
            else:
                wzt += war_zt[c % 2]
            if c > 0:
                wzt += [(sS, t_b1w[c])]
                T(lambda e, c=c, ztc=ztc: e.matmul(
                    PS[0:1, ztc:ztc + 128], csb["onescol"], band1[c % 2],
                    start=True, stop=False, skip_group_check=True), w=wzt, inc=False)
            t_zt[c] = T(lambda e, c=c, ztc=ztc: e.matmul(
                PS[0:1, ztc:ztc + 128], csb["onescol"], band2P[c % 2],
                start=(c <= 1), stop=True, skip_group_check=True), w=wzt)
            if stop_stage == 54 and h == 0 and c == 0:
                return _finish_early(nc, P, [(sT, sT.n), (sV, sV.n), (sS, sS.n),
                                             (dS, dS.n), (dG, dG.n)])
            pr2 = c % 2
            u1 = V(lambda e, pr2=pr2, ztc=ztc: e.tensor_copy(
                out=ztrowP[pr2], in_=PS[0:1, ztc:ztc + 128]), w=[(sT, t_zt[c])])
            t_ztread[c] = u1
            u2 = V(lambda e, pr2=pr2: e.reciprocal(out=ztr2P[pr2],
                                                   in_=ztrowP[pr2]), w=[(sV, u1)])
            u3 = V(lambda e, pr2=pr2: e.tensor_tensor(
                out=rztwbP[pr2], in0=ztr2P[pr2], in1=twrow, op=ALU.mult),
                w=[(sV, u2), (sV, t_twrow)])
            rzc = 3328 + pr2 * 128
            wrz = [(sV, u3)]
            if c > 1:
                wrz.append((sV, t_rzev[c - 2]))
            else:
                wrz += war_rz[pr2]
            trz = T(lambda e, hb=hb, tp=tp, pr2=pr2, rzc=rzc: e.matmul(
                PS[hb:hb + 64, rzc:rzc + 128], csb["ones64"], rztwbP[pr2],
                start=True, stop=True, skip_group_check=True, tile_position=tp),
                w=wrz)
            if stop_stage == 55 and h == 0 and c == 0:
                return _finish_early(nc, P, [(sT, sT.n), (sV, sV.n), (sS, sS.n),
                                             (dS, dS.n), (dG, dG.n)])
            u4 = V(lambda e, hb=hb, pr2=pr2, rzc=rzc: e.tensor_copy(
                out=rztw_sbP[pr2][hb:hb + 64, :],
                in_=PS[hb:hb + 64, rzc:rzc + 128]), w=[(sT, trz)])
            t_rzev[c] = u4
            u5 = V(lambda e, hb=hb, pr2=pr2, bac=bac: e.tensor_tensor(
                out=tcombP[pr2][hb:hb + 64, :], in0=PS[hb:hb + 64, bac:bac + 128],
                in1=rztw_sbP[pr2][hb:hb + 64, :], op=ALU.mult),
                w=[(sV, u4), (sT, t_bmm[c])])
            t_cmbm[c] = u5
            qc0 = 1024 + 512 * (c // 4) + (c % 4) * 128
            u6 = V(lambda e, qc0=qc0, hb=hb, pr2=pr2: e.tensor_tensor(
                out=t2combP[pr2][hb:hb + 64, :], in0=PS[hb:hb + 64, qc0:qc0 + 128],
                in1=owrep2[hb:hb + 64, :], op=ALU.mult),
                w=[(sT, t_pv[c]), (sV, t_owrep)])
            u7 = V(lambda e, c=c, hb=hb, icb=icb, pr2=pr2: e.tensor_tensor(
                out=y_fm[hb:hb + 64, icb + c * 128: icb + (c + 1) * 128],
                in0=tcombP[pr2][hb:hb + 64, :], in1=t2combP[pr2][hb:hb + 64, :],
                op=ALU.add), w=[(sV, u5), (sV, u6)])
            t_cmb[c] = u7
            if stop_stage == 23 + c and h == 0:
                return _finish_early(nc, P, [(sT, sT.n), (sV, sV.n), (sS, sS.n),
                                             (dS, dS.n), (dG, dG.n)])
        t_comb_prev = t_cmb[NCH - 1]
        war_pt[0] = [(sT, t_pv[NCH - 2])]
        war_pt[1] = [(sT, t_pv[NCH - 1])]
        war_wTh[hp] = [(sV, t_wg[(NCH - 1, 3)])]
        war_band1 = [[(sT, t_bmm[NCH - 2]), (sT, t_zt[NCH - 2])],
                     [(sT, t_bmm[NCH - 1]), (sT, t_zt[NCH - 1])]]
        war_band2 = [[(sT, t_bmm[NCH - 2]), (sT, t_zt[NCH - 2])],
                     [(sT, t_bmm[NCH - 1]), (sT, t_zt[NCH - 1])]]
        war_zt = [[(sV, t_ztread[NCH - 2])], [(sV, t_ztread[NCH - 1])]]
        war_rz = [[(sV, t_rzev[NCH - 2])], [(sV, t_rzev[NCH - 1])]]
        war_bandacc = [[(sV, t_cmbm[NCH - 2])], [(sV, t_cmbm[NCH - 1])]]
        bank_free[6] = [(sV, t_cmbm[NCH - 1])]
        bank_free[7] = [(sV, t_ztread[NCH - 1])]
        for qd in range(4):
            bank_free[2 + qd] = [(sV, t_cmb[NCH - 1])]
        for sb in (0, 1):
            bank_free[sb] = list(sfree[sb])
        if stop_stage == 3 and h == 0:
            return _finish_early(nc, P, [(sT, sT.n), (sV, sV.n), (sS, sS.n),
                                         (dS, dS.n), (dG, dG.n)])

    t_attn_T = sT.n
    if stop_stage <= 4:
        return _finish_early(nc, P, [(sT, sT.n), (sV, sV.n), (sS, sS.n),
                                     (dS, dS.n), (dG, dG.n)])
    t_attn_V = sV.n
    t_attn_S = sS.n

    Tdrain()
    w_att = [(sT, t_attn_T), (sV, t_attn_V), (sS, t_attn_S)]
    twc = None
    for ic in range(8):
        twc = SY(lambda e, ic=ic: e.dma_start(
            out=wcT[:, ic * 1024:(ic + 1) * 1024],
            in_=wbf[3][:, ic * 128:(ic + 1) * 128], transpose=True),
            w=[(dG, tw_wbf[3])] + w_att)
    t_wcT = twc
    t_b1 = G(lambda e: e.dma_start(
        out=b1row, in_=b1d.rearrange("e h -> (e h)").unsqueeze(0)), w=w_att)
    t_b2 = G(lambda e: e.dma_start(out=b2_sb, in_=b2d), w=w_att)

    t_w1use = {}
    t_w2use = {}
    t_hm = {0: 0, 1: 0}
    t_eo_hl = {}
    t_evac = {}
    t_odma = {}
    for s_ in range(4):
        for e_ in range(NE):
            eb = e_ % 2
            m = s_ * NE + e_
            wld = [(dG, t_w1bf[e_])] + w_att
            if m >= 2:
                wld.append((sT, t_w1use[m - 2]))
            tld = SY(lambda e, i=e_, eb=eb: e.dma_start(out=w1b[eb], in_=w1bf[i]),
                     w=wld)
            gcb = 2 + eb
            tgbc = T(lambda e, i=e_, s=s_, gcb=gcb: e.matmul(
                PS[:, gcb * 512:(gcb + 1) * 512], csb["e8"][:, i * 128:(i + 1) * 128],
                g_t8[0:8, s * 512:(s + 1) * 512], start=True, stop=True,
                skip_group_check=True), w=[(sV, t_ln_done)] + bwait(gcb))
            tlast = None
            thm = None
            for hc in range(4):
                bk = (e_ * 4 + hc) % 2
                w0 = [(dS, tld)] + bwait(bk)
                for icc in range(8):
                    T(lambda e, icc=icc, hc=hc, eb=eb, s=s_, bk=bk: e.matmul(
                        PS[:, bk * 512:(bk + 1) * 512],
                        w1b[eb][:, icc * 512 + hc * 128: icc * 512 + (hc + 1) * 128],
                        xn_fm[:, icc * 2048 + s * 512: icc * 2048 + (s + 1) * 512],
                        start=(icc == 0), stop=False, skip_group_check=True),
                        w=w0, inc=False)
                tb1m = T(lambda e, i=e_, hc=hc, bk=bk: e.matmul(
                    PS[:, bk * 512:(bk + 1) * 512],
                    b1row[0:1, i * 512 + hc * 128: i * 512 + (hc + 1) * 128],
                    csb["onesrow"], start=False, stop=True, skip_group_check=True),
                    w=[(dG, t_b1)])
                tlast = tb1m
                wgl = [(sT, tb1m)]
                if t_hm[bk]:
                    wgl.append((sV, t_hm[bk]))
                tgl = S(lambda e, bk=bk: e.activation(
                    out=hgel[bk], in_=PS[:, bk * 512:(bk + 1) * 512], func=AF.Gelu),
                    w=wgl)
                bfree(bk, sS, tgl)
                whm = [(sS, tgl), (sT, tgbc)]
                if s_ > 0 and (s_ * NE + e_) in ():
                    pass
                if s_ > 0:
                    whm.append((sT, t_eo_hl[s_ - 1]))
                thm = V(lambda e, i=e_, hc=hc, bk=bk, gcb=gcb: e.tensor_tensor(
                    out=hprime[:, (i * 4 + hc) * 512:(i * 4 + hc + 1) * 512],
                    in0=hgel[bk], in1=PS[:, gcb * 512:(gcb + 1) * 512], op=ALU.mult),
                    w=whm)
                t_hm[bk] = thm
            bfree(gcb, sV, thm)
            t_w1use[m] = tlast
        treg = [0] * 8
        for e_ in range(NE):
            eb = e_ % 2
            m = s_ * NE + e_
            wld = [(dG, t_w2bf[e_])] + w_att
            if m >= 2:
                wld.append((sT, t_w2use[m - 2]))
            tld = SY(lambda e, i=e_, eb=eb: e.dma_start(out=w2b[eb], in_=w2bf[i]),
                     w=wld)
            tm = None
            for hc in range(4):
                for reg in range(8):
                    tc4, og = reg // 2, reg % 2
                    first = (e_ == 0 and hc == 0)
                    w0 = [(dS, tld)]
                    if first:
                        w0 += bwait(reg)
                    hsl = hprime[:, (e_ * 4 + hc) * 512 + tc4 * 128:
                                 (e_ * 4 + hc) * 512 + (tc4 + 1) * 128]
                    tm = T(lambda e, hs=hsl, eb=eb, hc=hc, og=og, reg=reg,
                           first=first: e.matmul(
                        PS[:, reg * 512:(reg + 1) * 512], hs,
                        w2b[eb][:, hc * 1024 + og * 512: hc * 1024 + (og + 1) * 512],
                        start=first, stop=False, skip_group_check=True),
                        w=w0, inc=(hc == 3 and reg == 7))
            t_w2use[m] = tm
        t_eo_hl[s_] = t_w2use[s_ * NE + NE - 1]
        for reg in range(8):
            tc4, og = reg // 2, reg % 2
            gtc = s_ * 4 + tc4
            T(lambda e, gtc=gtc, og=og, reg=reg: e.matmul(
                PS[:, reg * 512:(reg + 1) * 512],
                g_t8[0:8, gtc * 128:(gtc + 1) * 128],
                b2_sb[0:8, og * 512:(og + 1) * 512],
                start=False, stop=False, skip_group_check=True),
                w=[(dG, t_b2)], inc=False)
            for icc in range(8):
                treg[reg] = T(lambda e, icc=icc, gtc=gtc, og=og, reg=reg: e.matmul(
                    PS[:, reg * 512:(reg + 1) * 512],
                    y_fm[:, icc * 2048 + gtc * 128: icc * 2048 + (gtc + 1) * 128],
                    wcT[:, icc * 1024 + og * 512: icc * 1024 + (og + 1) * 512],
                    start=False, stop=(icc == 7), skip_group_check=True),
                    w=[(dS, t_wcT)], inc=(icc == 7))
        for tc4 in range(4):
            gtc = s_ * 4 + tc4
            pe = gtc % 2
            wxl = w_att + ([(sV, t_evac[gtc - 2])] if gtc >= 2 else [])
            txl = SY(lambda e, gtc=gtc, pe=pe: e.dma_start(
                out=xst2[pe], in_=x[gtc * 128:(gtc + 1) * 128, :]), w=wxl)
            tev = None
            for og in range(2):
                reg = tc4 * 2 + og
                wev = [(sT, treg[reg]), (dS, txl)]
                if gtc >= 2:
                    wev.append((sO, t_odma[gtc - 2]))
                tev = V(lambda e, pe=pe, og=og, reg=reg: e.tensor_tensor(
                    out=outst[pe][:, og * 512:(og + 1) * 512],
                    in0=PS[:, reg * 512:(reg + 1) * 512],
                    in1=xst2[pe][:, og * 512:(og + 1) * 512], op=ALU.add), w=wev)
                bfree(reg, sV, tev)
            t_evac[gtc] = tev
            t_odma[gtc] = SYo(lambda e, gtc=gtc, pe=pe: e.dma_start(
                out=out[gtc * 128:(gtc + 1) * 128, :], in_=outst[pe]),
                w=[(sV, tev)])

    if DEBUG:
        wd = [(sO, t_odma[NT - 1])]
        G(lambda e: e.dma_start(out=dbg["d_xn"], in_=xn_fm), w=wd)
        G(lambda e: e.dma_start(out=dbg["d_q"], in_=q_fm), w=wd)
        G(lambda e: e.dma_start(out=dbg["d_k"], in_=k_fm), w=wd)
        G(lambda e: e.dma_start(out=dbg["d_v"], in_=v_tm), w=wd)
        G(lambda e: e.dma_start(out=dbg["d_y"], in_=y_fm), w=wd)
        G(lambda e: e.dma_start(out=dbg["d_g"], in_=g_t8), w=wd)
        G(lambda e: e.dma_start(out=dbg["d_pcs"], in_=pcsS[1][0:32, :]), w=wd)
        G(lambda e: e.dma_start(out=dbg["d_wt"], in_=wTh[1]), w=wd)
        P.add("gpsimd", lambda e: e.nop(), w=[(dG, dG.n)])

    P.add("sync", lambda e: e.nop(), w=[(sO, 16 * NT)])

    with nc.Block() as block:
        @block.sync
        def _(e):
            P.replay("sync", e)

        @block.tensor
        def _(e):
            P.replay("tensor", e)

        @block.vector
        def _(e):
            P.replay("vector", e)

        @block.scalar
        def _(e):
            P.replay("scalar", e)

        @block.gpsimd
        def _(e):
            P.replay("gpsimd", e)

    return nc


_NC_CACHE = None


def kernel(**inputs):
    global _NC_CACHE
    xs = np.asarray(inputs["L0_attn_hidden_states"])
    base = {
        "Wq": np.ascontiguousarray(inputs["L0_attn_Wq"], np.float32),
        "Wk": np.ascontiguousarray(inputs["L0_attn_Wk"], np.float32),
        "Wv": np.ascontiguousarray(inputs["L0_attn_Wv"], np.float32),
        "Wc": np.ascontiguousarray(inputs["L0_attn_Wc"], np.float32),
        "terr": np.ascontiguousarray(inputs["L0_attn_terrace_mix"],
                                     np.float32).reshape(1, SEG),
        "idxd": np.ascontiguousarray(inputs["L0_attn_index_decays"], np.float32),
        "megad": np.ascontiguousarray(inputs["L0_attn_mega_decays"],
                                      np.float32).reshape(1, H),
        "w1": np.ascontiguousarray(inputs["L0_ffn_w1"], np.float32),
        "b1": np.ascontiguousarray(inputs["L0_ffn_b1"], np.float32),
        "w2": np.ascontiguousarray(inputs["L0_ffn_w2"], np.float32),
        "b2": np.ascontiguousarray(inputs["L0_ffn_b2"], np.float32),
        "cent": np.ascontiguousarray(inputs["L0_ffn_centroids"], np.float32),
        "wrt": np.ascontiguousarray(inputs["L0_ffn_w_route"], np.float32),
    }
    for k, v in _host_consts().items():
        base["c_" + k] = v
    if _NC_CACHE is None:
        _NC_CACHE = build()
    nc = _NC_CACHE
    in_maps = []
    for c in range(NCORE):
        m = dict(base)
        m["x"] = np.ascontiguousarray(xs[c], np.float32)
        in_maps.append(m)
    res = run_bass_kernel_spmd(nc, in_maps, core_ids=list(range(NCORE)))
    return np.stack([np.asarray(res.results[c]["out"]) for c in range(NCORE)],
                    axis=0)


# version 07
# speedup vs baseline: 1.1938x; 1.0001x over previous
import numpy as np
import ml_dtypes
import concourse.bass as bass
import concourse.mybir as mybir
from concourse.bass_utils import run_bass_kernel_spmd

F32 = mybir.dt.float32
BF16 = mybir.dt.bfloat16
U8 = mybir.dt.uint8
AF = mybir.ActivationFunctionType
ALU = mybir.AluOpType
AX = mybir.AxisListType.X

L, D, H, SEG = 2048, 1024, 16, 64
NT, NCH = 16, 16
NE = 8
NCORE = 8
LN32 = float(np.log(32.0))

DEBUG = False


def _host_consts():
    bf = ml_dtypes.bfloat16
    f = np.arange(128)
    jj, rr = np.meshgrid(np.arange(32), np.arange(32), indexing="ij")
    e8 = np.zeros((8, 1024), dtype=bf)
    for e in range(8):
        e8[e, e * 128:(e + 1) * 128] = 1
    return {
        "idnf32": np.eye(128, dtype=np.float32),
        "tri01": (f[None, :] >= f[:, None]).astype(bf),
        "strictu": (f[None, :] < f[:, None]).astype(bf),
        "negi": (-30000.0 * np.eye(128)).astype(bf),
        "rep2": (np.arange(128)[None, :] % 64 == np.arange(64)[:, None]).astype(np.float32),
        "emat": np.clip(rr - 1 - jj, 0, 31).astype(np.float32),
        "ltri": (jj < rr).astype(np.float32),
        "e8": e8,
        "onesrow": np.ones((1, 512), dtype=bf),
        "ones64": np.ones((1, 64), dtype=bf),
        "onescol": np.ones((128, 1), dtype=bf),
        "ones32f": np.ones((1, 32), dtype=np.float32),
        "onesf128": np.ones((1, 128), dtype=np.float32),
    }


class Sem:
    __slots__ = ("h", "n")

    def __init__(self, h):
        self.h = h
        self.n = 0


class Tape:
    ENGS = ("sync", "tensor", "vector", "scalar", "gpsimd")

    def __init__(self):
        self.ops = {e: [] for e in self.ENGS}
        self.waited = {e: {} for e in self.ENGS}

    def add(self, eng, fn, w=(), inc=None, amt=1):
        ws = []
        for s, v in w:
            assert v is not None and v > 0, f"bad wait {v} on {eng}"
            k = id(s)
            if self.waited[eng].get(k, 0) >= v:
                continue
            self.waited[eng][k] = v
            ws.append((s.h, v))
        pair, tick = None, None
        if inc is not None:
            inc.n += amt
            tick = inc.n
            pair = (inc.h, amt)
        self.ops[eng].append((fn, tuple(ws), pair))
        return tick

    def replay(self, eng, e):
        for fn, ws, pair in self.ops[eng]:
            for h, v in ws:
                e.wait_ge(h, v)
            inst = fn(e)
            if pair is not None:
                inst.then_inc(pair[0], pair[1])


def _finish_early(nc, P, waits):
    P.add("sync", lambda e: e.nop(), w=[(s, v) for s, v in waits if v > 0])
    with nc.Block() as block:
        @block.sync
        def _(e):
            P.replay("sync", e)

        @block.tensor
        def _(e):
            P.replay("tensor", e)

        @block.vector
        def _(e):
            P.replay("vector", e)

        @block.scalar
        def _(e):
            P.replay("scalar", e)

        @block.gpsimd
        def _(e):
            P.replay("gpsimd", e)
    return nc


def build(stop_stage=99):
    import os
    nc = bass.Bass("TRN2", target_bir_lowering=False,
                   detect_race_conditions=False)
    nc.dram_tensor("salt" + os.environ.get("KSALT", "0"), [1, 1], F32,
                   kind="Internal")
    HC = _host_consts()

    x = nc.dram_tensor("x", [L, D], F32, kind="ExternalInput").ap()
    Wmats = [nc.dram_tensor(n, [D, D], F32, kind="ExternalInput").ap()
             for n in ("Wq", "Wk", "Wv", "Wc")]
    terr = nc.dram_tensor("terr", [1, SEG], F32, kind="ExternalInput").ap()
    idxd = nc.dram_tensor("idxd", [H, SEG], F32, kind="ExternalInput").ap()
    megad = nc.dram_tensor("megad", [1, H], F32, kind="ExternalInput").ap()
    w1d = nc.dram_tensor("w1", [NE, D, 512], F32, kind="ExternalInput").ap()
    b1d = nc.dram_tensor("b1", [NE, 512], F32, kind="ExternalInput").ap()
    w2d = nc.dram_tensor("w2", [NE, 512, D], F32, kind="ExternalInput").ap()
    b2d = nc.dram_tensor("b2", [NE, D], F32, kind="ExternalInput").ap()
    cent = nc.dram_tensor("cent", [NE, D], F32, kind="ExternalInput").ap()
    wrt = nc.dram_tensor("wrt", [NE, D], F32, kind="ExternalInput").ap()
    cin = {k: nc.dram_tensor("c_" + k, list(v.shape),
                             F32 if v.dtype == np.float32 else BF16,
                             kind="ExternalInput").ap()
           for k, v in HC.items()}
    out = nc.dram_tensor("out", [L, D], F32, kind="ExternalOutput").ap()
    wbf = [nc.dram_tensor(f"wbf{i}", [D, D], BF16, kind="Internal").ap()
           for i in range(4)]
    w1bf = nc.dram_tensor("w1bf", [NE, 128, 4096], BF16, kind="Internal").ap()
    w2bf = nc.dram_tensor("w2bf", [NE, 128, 4096], BF16, kind="Internal").ap()
    dbg = {}
    if DEBUG:
        for n, shp, dt in [("d_xn", [128, 16384], BF16), ("d_q", [128, 16384], BF16),
                           ("d_k", [128, 16384], BF16), ("d_v", [128, 16384], BF16),
                           ("d_y", [128, 16384], BF16), ("d_g", [8, 2048], BF16),
                           ("d_pcs", [32, 2048], BF16), ("d_wt", [128, 512], F32)]:
            dbg[n] = nc.dram_tensor(n, shp, dt, kind="ExternalOutput").ap()

    ar_xn = nc.alloc_sbuf_tensor("ar_xn", [128, 32768], U8).ap()
    ar_q = nc.alloc_sbuf_tensor("ar_q", [128, 32768], U8).ap()
    ar_k = nc.alloc_sbuf_tensor("ar_k", [128, 32768], U8).ap()
    ar_v = nc.alloc_sbuf_tensor("ar_v", [128, 32768], U8).ap()
    ar_y = nc.alloc_sbuf_tensor("ar_y", [128, 32768], U8).ap()
    ar_t = nc.alloc_sbuf_tensor("ar_t", [128, 28672], U8).ap()

    xn_fm = ar_xn.bitcast(BF16)
    q_fm = ar_q.bitcast(BF16)
    k_fm = ar_k.bitcast(BF16)
    v_tm = ar_v.bitcast(BF16)
    y_fm = ar_y.bitcast(BF16)
    wTb = [ar_y[:, 0:16384].bitcast(BF16), ar_y[:, 16384:32768].bitcast(BF16)]
    hprime = ar_q.bitcast(BF16)
    wcT = ar_k[:, 0:16384].bitcast(BF16)
    xst2 = [ar_k[:, 16384:20480].bitcast(F32), ar_k[:, 20480:24576].bitcast(F32)]
    outst = [ar_k[:, 24576:28672].bitcast(F32), ar_k[:, 28672:32768].bitcast(F32)]
    w1b = [ar_v[:, 0:8192].bitcast(BF16), ar_v[:, 8192:16384].bitcast(BF16)]
    w2b = [ar_v[:, 16384:24576].bitcast(BF16), ar_v[:, 24576:32768].bitcast(BF16)]

    xstage = [ar_t[:, 0:4096].bitcast(F32), ar_t[:, 4096:8192].bitcast(F32)]
    xn_st = [ar_t[:, 8192:12288].bitcast(F32), ar_t[:, 12288:16384].bitcast(F32)]
    sqdump = ar_t[:, 16384:20480].bitcast(F32)
    lnS = ar_t[:, 20480:24576].bitcast(F32)
    prepS = ar_t[:, 24576:28672].bitcast(F32)
    Ptm = [ar_t[:, 0:4096].bitcast(BF16), ar_t[:, 4096:8192].bitcast(BF16)]
    stripB = Ptm
    band1 = [ar_t[:, 8192:8448].bitcast(BF16), ar_t[:, 8448:8704].bitcast(BF16),
             None]
    band2P = [nc.alloc_sbuf_tensor("band2", [128, 128], BF16).ap(),
              ar_t[:, 8704:8960].bitcast(BF16)]
    pcsS = [ar_t[:, 8960:13056].bitcast(BF16), ar_t[:, 13056:17152].bitcast(BF16)]
    wTh = [ar_t[:, 17152:19200].bitcast(F32), ar_t[:, 19200:21248].bitcast(F32)]
    atS = ar_t[:, 21248:25344].bitcast(F32)
    hgel = [ar_t[:, 0:2048].bitcast(F32), ar_t[:, 2048:4096].bitcast(F32)]
    b1row = ar_t[0:1, 4096:12288].bitcast(BF16)
    b2_sb = ar_t[0:8, 12288:14336].bitcast(BF16)

    csb = {k: nc.alloc_sbuf_tensor("s_" + k, list(HC[k].shape),
                                   F32 if HC[k].dtype == np.float32 else BF16).ap()
           for k in HC}
    g_t8 = nc.alloc_sbuf_tensor("g_t8", [8, 2048], BF16).ap()
    twrow = nc.alloc_sbuf_tensor("twrow", [1, 128], F32).ap()
    owrep2 = nc.alloc_sbuf_tensor("owrep2", [128, 128], F32).ap()
    crep = nc.alloc_sbuf_tensor("crep", [128, 16], F32).ap()
    lnmega = nc.alloc_sbuf_tensor("lnmega", [1, 16], F32).ap()
    cnrb = nc.alloc_sbuf_tensor("cnrb", [128, 128], BF16).ap()
    centF = nc.alloc_sbuf_tensor("centF", [8, 1024], F32).ap()
    routeF = nc.alloc_sbuf_tensor("routeF", [8, 1024], F32).ap()

    msum, sq2 = lnS[:, 0:2], lnS[:, 2:4]
    mmean, varv = lnS[:, 4:6], lnS[:, 6:8]
    rstd, negm = lnS[:, 8:10], lnS[:, 10:12]
    recn, tvar = lnS[:, 12:14], lnS[:, 24:26]
    m1c, m2c, nm1 = lnS[:, 14:16], lnS[:, 16:18], lnS[:, 18:20]
    gsum, grec = lnS[:, 20:22], lnS[:, 22:24]
    lgec = [lnS[:, 32:48], lnS[:, 48:64]]
    lgf = [lnS[:, 64:72], lnS[:, 72:80]]
    ge1 = [lnS[:, 96:104], lnS[:, 104:112]]
    mskd = [lnS[:, 112:120], lnS[:, 120:128]]
    kept = [lnS[:, 128:136], lnS[:, 136:144]]
    egk = [lnS[:, 144:152], lnS[:, 152:160]]
    gtm = [lnS[:, 160:168], lnS[:, 168:176]]

    cn2, lncn, rcn = prepS[0:8, 0:1], prepS[0:8, 1:2], prepS[0:8, 2:3]
    idxF = prepS[0:16, 8:72]
    c16 = prepS[0:16, 72:136]
    callT = prepS[0:64, 136:152]
    megae, megac = prepS[0:1, 152:168], prepS[0:1, 168:184]
    twe, twp, tw64 = prepS[0:1, 184:248], prepS[0:1, 248:312], prepS[0:1, 312:376]
    ow64 = prepS[0:1, 376:440]
    owrow = prepS[0:1, 440:568]
    sq8b = prepS[0:8, 504:1016]

    zparts = atS[:, 0:8]
    Zc, rZc = atS[:, 8:10], atS[:, 10:12]
    EA, ATf = atS[0:32, 16:48], atS[0:32, 48:80]
    rnb32 = atS[0:32, 80:112]
    lnm32, Pcol = atS[0:32, 112:113], atS[0:32, 113:114]
    rn0, rnorm = atS[0:32, 114:115], atS[0:32, 115:116]
    rnrow = atS[0:1, 116:148]
    ztrowP = [atS[0:1, 148:276], prepS[0:1, 0:128]]
    ztr2P = [atS[0:1, 276:404], prepS[0:1, 128:256]]
    rztw_sbP = [atS[:, 404:532], prepS[:, 320:448]]
    tcombP = [atS[:, 532:660], prepS[:, 448:576]]
    t2combP = [atS[:, 660:788], prepS[:, 576:704]]
    ATb = prepS[0:32, 708:724].bitcast(BF16)
    A2b = prepS[0:32, 724:740].bitcast(BF16)
    Pcolb = prepS[0:32, 704:705].bitcast(BF16)
    rztwbP = [prepS[0:1, 740:804].bitcast(BF16),
              ar_t[0:1, 25600:25856].bitcast(BF16)]
    cvecP = [prepS[:, 804:820].bitcast(BF16), prepS[:, 820:836].bitcast(BF16)]
    band1[2] = prepS[:, 836:900].bitcast(BF16)

    PS = nc.alloc_psum_tensor("PS", [128, 4096], F32).ap()

    sT, sV, sS = Sem(nc.alloc_semaphore("sT")), Sem(nc.alloc_semaphore("sV")), \
        Sem(nc.alloc_semaphore("sS"))
    dS, dG, sO = Sem(nc.alloc_semaphore("dS")), Sem(nc.alloc_semaphore("dG")), \
        Sem(nc.alloc_semaphore("sO"))

    P = Tape()

    def T(fn, w=(), inc=True):
        return P.add("tensor", fn, w, sT if inc else None)

    def V(fn, w=(), inc=True):
        return P.add("vector", fn, w, sV if inc else None)

    def S(fn, w=(), inc=True):
        return P.add("scalar", fn, w, sS if inc else None)

    def SY(fn, w=()):
        return P.add("sync", fn, w, dS, 16)

    def SYo(fn, w=()):
        return P.add("sync", fn, w, sO, 16)

    def G(fn, w=()):
        return P.add("gpsimd", fn, w, dG, 16)

    def Tdrain():
        P.add("tensor", lambda e: e.nop(), w=[(sT, sT.n)] if sT.n else [])

    bank_free = {b: [] for b in range(8)}

    def bfree(b, sem, tick):
        bank_free[b] = [(sem, tick)]

    def bwait(*banks):
        ws = []
        for b in banks:
            ws += bank_free[b]
        return ws

    for k in HC:
        SY(lambda e, d=csb[k], s=cin[k]: e.dma_start(out=d, in_=s))
    SY(lambda e: e.dma_start(out=idxF, in_=idxd))
    SY(lambda e: e.dma_start(out=megae, in_=megad))
    SY(lambda e: e.dma_start(out=twe, in_=terr))
    SY(lambda e: e.dma_start(out=centF, in_=cent))
    SY(lambda e: e.dma_start(out=routeF, in_=wrt))
    t_prein = dS.n
    tw_wbf = [G(lambda e, d=wbf[i], s=Wmats[i]: e.dma_start(out=d, in_=s))
              for i in range(4)]
    t_w1bf, t_w2bf = [], []
    for e_ in range(NE):
        t_w1bf.append(G(lambda e, i=e_: e.dma_start(
            out=w1bf[i], in_=w1d[i].rearrange("(ic p) h -> p ic h", p=128))))
        t_w2bf.append(G(lambda e, i=e_: e.dma_start(
            out=w2bf[i], in_=w2d[i].rearrange("(hc p) o -> p hc o", p=128))))

    wp = [(dS, t_prein)]
    a1 = S(lambda e: e.activation(out=sq8b, in_=centF[:, 0:512], func=AF.Square,
                                  accum_out=cn2), w=wp)
    a2 = S(lambda e: e.activation(out=sq8b, in_=centF[:, 512:1024], func=AF.Square,
                                  accum_out=lncn), w=[(sS, a1)])
    a3 = V(lambda e: e.tensor_tensor(out=cn2, in0=cn2, in1=lncn, op=ALU.add),
           w=[(sS, a2)])
    a4 = S(lambda e: e.activation(out=lncn, in_=cn2, func=AF.Ln), w=[(sV, a3)])
    a5 = S(lambda e: e.activation(out=rcn, in_=lncn, func=AF.Exp, scale=-0.5),
           w=[(sS, a4)])
    a6 = V(lambda e: e.tensor_scalar(out=centF, in0=centF, scalar1=rcn,
                                     scalar2=1.0 / 32.0, op0=ALU.mult,
                                     op1=ALU.mult), w=[(sS, a5)])
    tt = None
    for ic in range(8):
        T(lambda e, ic=ic: e.transpose(
            PS[:, 3584 + ic * 16:3592 + ic * 16],
            centF[0:8, ic * 128:(ic + 1) * 128], csb["idnf32"][0:8, 0:8]),
            w=[(sV, a6)], inc=False)
        tt = T(lambda e, ic=ic: e.transpose(
            PS[:, 3592 + ic * 16:3600 + ic * 16],
            routeF[0:8, ic * 128:(ic + 1) * 128], csb["idnf32"][0:8, 0:8]),
            w=[(dS, t_prein)], inc=(ic == 7))
    a7 = V(lambda e: e.tensor_copy(out=cnrb, in_=PS[:, 3584:3712]), w=[(sT, tt)])
    bfree(7, sV, a7)
    b1_ = S(lambda e: e.activation(out=c16, in_=idxF, func=AF.Exp), w=wp)
    b2_ = T(lambda e: e.transpose(PS[0:64, 3072:3088], c16[0:16, 0:64],
                                  csb["idnf32"][0:16, 0:16]), w=[(sS, b1_)])
    b3 = V(lambda e: e.tensor_copy(out=callT, in_=PS[0:64, 3072:3088]), w=[(sT, b2_)])
    b4 = T(lambda e: e.matmul(PS[:, 3088:3104], csb["rep2"], callT,
                              start=True, stop=True), w=[(sV, b3)])
    t_crep = V(lambda e: e.tensor_copy(out=crep, in_=PS[:, 3088:3104]), w=[(sT, b4)])
    c1_ = S(lambda e: e.activation(out=megac, in_=megae, func=AF.Exp, scale=64.0), w=wp)
    c2_ = V(lambda e: e.tensor_scalar_min(out=megac, in0=megac, scalar1=2.0),
            w=[(sS, c1_)])
    t_lnmega = S(lambda e: e.activation(out=lnmega, in_=megac, func=AF.Ln),
                 w=[(sV, c2_)])
    d1 = S(lambda e: e.activation(out=twp, in_=twe, func=AF.Exp, scale=-1.0), w=wp)
    d2 = V(lambda e: e.tensor_scalar_add(out=twp, in0=twp, scalar1=1.0), w=[(sS, d1)])
    d3 = V(lambda e: e.reciprocal(out=tw64, in_=twp), w=[(sV, d2)])
    d4 = V(lambda e: e.tensor_copy(out=twrow[0:1, 0:64], in_=tw64), w=[(sV, d3)])
    t_twrow = V(lambda e: e.tensor_copy(out=twrow[0:1, 64:128], in_=tw64),
                w=[(sV, d3)])
    d6 = V(lambda e: e.tensor_scalar(out=ow64, in0=tw64, scalar1=-1.0, scalar2=1.0,
                                     op0=ALU.mult, op1=ALU.add), w=[(sV, d3)])
    d7 = V(lambda e: e.tensor_copy(out=owrow[0:1, 0:64], in_=ow64), w=[(sV, d6)])
    d8 = V(lambda e: e.tensor_copy(out=owrow[0:1, 64:128], in_=ow64), w=[(sV, d6)])
    d9 = T(lambda e: e.matmul(PS[:, 3104:3232], csb["onesf128"], owrow,
                              start=True, stop=True), w=[(sV, d7), (sV, d8)])
    t_owrep = V(lambda e: e.tensor_copy(out=owrep2, in_=PS[:, 3104:3232]),
                w=[(sT, d9)])
    bfree(6, sV, t_owrep)

    t_xn = [0] * NT
    t_lnS = [0] * NT
    t_lnV = [0] * NT
    t_xtr = [0] * NT
    for tc in range(NT):
        par = tc % 2
        w_x = [(sS, t_lnS[tc - 2]), (sV, t_lnV[tc - 2])] if tc >= 2 else []
        tx = SY(lambda e, tc=tc, p=par: e.dma_start(
            out=xstage[p], in_=x[tc * 128:(tc + 1) * 128, :]), w=w_x)
        wx = [(dS, tx)]
        va = V(lambda e, p=par: e.reduce_sum(out=msum[:, p:p + 1], in_=xstage[p],
                                             axis=AX), w=wx)
        sa = S(lambda e, p=par: e.activation(out=sqdump, in_=xstage[p],
                                             func=AF.Square,
                                             accum_out=sq2[:, p:p + 1]), w=wx)
        vb = V(lambda e, p=par: e.tensor_scalar_mul(
            out=mmean[:, p:p + 1], in0=msum[:, p:p + 1], scalar1=-1.0 / D),
            w=[(sV, va)])
        vc = V(lambda e, p=par: e.tensor_scalar(
            out=tvar[:, p:p + 1], in0=mmean[:, p:p + 1], scalar1=mmean[:, p:p + 1],
            scalar2=None, op0=ALU.mult), w=[(sV, vb)])
        vd = V(lambda e, p=par: e.tensor_scalar_mul(
            out=varv[:, p:p + 1], in0=sq2[:, p:p + 1], scalar1=1.0 / D),
            w=[(sS, sa), (sV, vc)])
        ve = V(lambda e, p=par: e.tensor_tensor(
            out=varv[:, p:p + 1], in0=varv[:, p:p + 1], in1=tvar[:, p:p + 1],
            op=ALU.subtract), w=[(sV, vd)])
        vf = V(lambda e, p=par: e.tensor_scalar_add(
            out=tvar[:, p:p + 1], in0=varv[:, p:p + 1], scalar1=1e-5), w=[(sV, ve)])
        sb_ = S(lambda e, p=par: e.activation(out=rstd[:, p:p + 1],
                                              in_=tvar[:, p:p + 1], func=AF.Ln),
                w=[(sV, vf)])
        sc_ = S(lambda e, p=par: e.activation(out=rstd[:, p:p + 1],
                                              in_=rstd[:, p:p + 1], func=AF.Exp,
                                              scale=-0.5), w=[(sS, sb_)])
        vh = V(lambda e, p=par: e.tensor_scalar(
            out=recn[:, p:p + 1], in0=varv[:, p:p + 1], scalar1=rstd[:, p:p + 1],
            scalar2=rstd[:, p:p + 1], op0=ALU.mult, op1=ALU.mult), w=[(sS, sc_)])
        si = S(lambda e, p=par: e.activation(out=recn[:, p:p + 1],
                                             in_=recn[:, p:p + 1], func=AF.Ln),
               w=[(sV, vh)])
        sj = S(lambda e, p=par: e.activation(out=recn[:, p:p + 1],
                                             in_=recn[:, p:p + 1], func=AF.Exp,
                                             scale=-0.5), w=[(sS, si)])
        w_xn = [(sS, sc_)] + ([(sT, t_xtr[tc - 2])] if tc >= 2 else [])
        sk = V(lambda e, p=par: e.tensor_scalar(
            out=xn_st[p], in0=xstage[p], scalar1=mmean[:, p:p + 1],
            scalar2=rstd[:, p:p + 1], op0=ALU.add, op1=ALU.mult), w=w_xn)
        t_lnS[tc] = sa
        base = par * 1024
        wtr = [(sV, sk)] + (bwait(par * 2, par * 2 + 1) if tc >= 2 else [])
        ttr = None
        for ic in range(8):
            ttr = T(lambda e, p=par, ic=ic, b=base: e.transpose(
                PS[:, b + ic * 128: b + (ic + 1) * 128],
                xn_st[p][:, ic * 128:(ic + 1) * 128], csb["idnf32"]),
                w=wtr, inc=(ic == 7))
        t_xtr[tc] = ttr
        vk = V(lambda e, tc=tc, b=base: e.tensor_copy(
            out=xn_fm.rearrange("p (ic t) -> p ic t", ic=8)[:, :, tc * 128:(tc + 1) * 128],
            in_=PS[:, b:b + 1024].rearrange("p (ic j) -> p ic j", ic=8)),
            w=[(sT, ttr)])
        t_xn[tc] = vk
        bfree(par * 2, sV, vk)
        bfree(par * 2 + 1, sV, vk)
        rb = 2048 + par * 512
        wr = [(sV, vk), (sV, a7)] + (bwait(4 + par) if tc >= 2 else [])
        trm = None
        for ic in range(8):
            trm = T(lambda e, ic=ic, tc=tc, rb=rb: e.matmul(
                PS[:, rb:rb + 16],
                xn_fm[:, ic * 2048 + tc * 128: ic * 2048 + (tc + 1) * 128],
                cnrb[:, ic * 16:(ic + 1) * 16],
                start=(ic == 0), stop=(ic == 7), skip_group_check=True),
                w=wr, inc=(ic == 7))
        r1 = V(lambda e, p=par, rb=rb: e.tensor_copy(out=lgec[p], in_=PS[:, rb:rb + 16]),
               w=[(sT, trm)])
        bfree(4 + par, sV, r1)
        r2 = V(lambda e, p=par: e.scalar_tensor_tensor(
            out=lgf[p], in0=lgec[p][:, 0:8], scalar=recn[:, p:p + 1],
            in1=lgec[p][:, 8:16], op0=ALU.mult, op1=ALU.add),
            w=[(sV, r1), (sS, sj)])
        r3 = V(lambda e, p=par: e.reduce_max(out=m1c[:, p:p + 1], in_=lgf[p], axis=AX),
               w=[(sV, r2)])
        r4 = V(lambda e, p=par: e.tensor_scalar(
            out=ge1[p], in0=lgf[p], scalar1=m1c[:, p:p + 1], scalar2=None,
            op0=ALU.is_ge), w=[(sV, r3)])
        r5 = V(lambda e, p=par: e.scalar_tensor_tensor(
            out=mskd[p], in0=ge1[p], scalar=-1e30, in1=lgf[p],
            op0=ALU.mult, op1=ALU.add), w=[(sV, r4)])
        r6 = V(lambda e, p=par: e.reduce_max(out=m2c[:, p:p + 1], in_=mskd[p], axis=AX),
               w=[(sV, r5)])
        r7 = V(lambda e, p=par: e.tensor_scalar(
            out=kept[p], in0=lgf[p], scalar1=m2c[:, p:p + 1], scalar2=None,
            op0=ALU.is_ge), w=[(sV, r6)])
        r8 = V(lambda e, p=par: e.tensor_scalar_mul(
            out=nm1[:, p:p + 1], in0=m1c[:, p:p + 1], scalar1=-1.0), w=[(sV, r3)])
        r9 = S(lambda e, p=par: e.activation(out=egk[p], in_=lgf[p], func=AF.Exp,
                                             bias=nm1[:, p:p + 1]), w=[(sV, r8)])
        r10 = V(lambda e, p=par: e.tensor_tensor(out=egk[p], in0=egk[p], in1=kept[p],
                                                 op=ALU.mult), w=[(sS, r9), (sV, r7)])
        r11 = V(lambda e, p=par: e.reduce_sum(out=gsum[:, p:p + 1], in_=egk[p], axis=AX),
                w=[(sV, r10)])
        r12 = V(lambda e, p=par: e.reciprocal(out=grec[:, p:p + 1],
                                              in_=gsum[:, p:p + 1]), w=[(sV, r11)])
        r13 = V(lambda e, p=par: e.tensor_scalar_mul(
            out=gtm[p], in0=egk[p], scalar1=grec[:, p:p + 1]), w=[(sV, r12)])
        gb = 3072 + par * 128
        r14 = T(lambda e, p=par, gb=gb: e.transpose(
            PS[0:8, gb:gb + 128], gtm[p], csb["idnf32"]), w=[(sV, r13)] + bwait(6))
        r15 = V(lambda e, tc=tc, gb=gb: e.tensor_copy(
            out=g_t8[0:8, tc * 128:(tc + 1) * 128], in_=PS[0:8, gb:gb + 128]),
            w=[(sT, r14)])
        t_lnV[tc] = r15
        bfree(6, sV, r15)
    t_ln_done = t_lnV[NT - 1]
    t_xn_all = t_xn[NT - 1]
    if stop_stage <= 1:
        return _finish_early(nc, P, [(sT, sT.n), (sV, sV.n), (sS, sS.n),
                                     (dS, dS.n), (dG, dG.n)])

    t_wT = [0, 0, 0]
    for m in range(2):
        tk = None
        for ic in range(8):
            tk = SY(lambda e, m=m, ic=ic: e.dma_start(
                out=wTb[m][:, ic * 1024:(ic + 1) * 1024],
                in_=wbf[m][:, ic * 128:(ic + 1) * 128], transpose=True),
                w=[(dG, tw_wbf[m])])
        t_wT[m] = tk

    def mm_proj(dst_fm, wsrc, t_wready):
        tick_last = None
        for oc in range(8):
            for tg in range(4):
                bk = (oc * 4 + tg) % 2
                w0 = [(dS, t_wready), (sV, t_xn_all)] + bwait(bk)
                tm = None
                for ic in range(8):
                    tm = T(lambda e, oc=oc, tg=tg, ic=ic, bk=bk, ws=wsrc: e.matmul(
                        PS[:, bk * 512:(bk + 1) * 512],
                        ws[:, ic * 1024 + oc * 128: ic * 1024 + (oc + 1) * 128],
                        xn_fm[:, ic * 2048 + tg * 512: ic * 2048 + (tg + 1) * 512],
                        start=(ic == 0), stop=(ic == 7), skip_group_check=True),
                        w=w0, inc=(ic == 7))
                tv_ = V(lambda e, oc=oc, tg=tg, bk=bk, d=dst_fm: e.tensor_copy(
                    out=d[:, oc * 2048 + tg * 512: oc * 2048 + (tg + 1) * 512],
                    in_=PS[:, bk * 512:(bk + 1) * 512]), w=[(sT, tm)])
                bfree(bk, sV, tv_)
                tick_last = tm
        return tick_last

    t_qmm = mm_proj(q_fm, wTb[0], t_wT[0])
    mm_proj(k_fm, wTb[1], t_wT[1])
    tk = None
    for ic in range(8):
        tk = SY(lambda e, ic=ic: e.dma_start(
            out=wTb[0][:, ic * 1024:(ic + 1) * 1024],
            in_=wbf[2][:, ic * 128:(ic + 1) * 128], transpose=True),
            w=[(dG, tw_wbf[2]), (sT, t_qmm)])
    t_wT[2] = tk
    for tc in range(NT):
        for og in range(2):
            bk = (tc * 2 + og) % 2
            w0 = [(dS, t_wT[2])] + bwait(bk)
            tm = None
            for ic in range(8):
                tm = T(lambda e, tc=tc, og=og, ic=ic, bk=bk: e.matmul(
                    PS[:, bk * 512:(bk + 1) * 512],
                    xn_fm[:, ic * 2048 + tc * 128: ic * 2048 + (tc + 1) * 128],
                    wTb[0][:, ic * 1024 + og * 512: ic * 1024 + (og + 1) * 512],
                    start=(ic == 0), stop=(ic == 7), skip_group_check=True),
                    w=w0, inc=(ic == 7))
            tv_ = V(lambda e, tc=tc, og=og, bk=bk: e.tensor_copy(
                out=v_tm[:, tc * 1024 + og * 512: tc * 1024 + (og + 1) * 512],
                in_=PS[:, bk * 512:(bk + 1) * 512]), w=[(sT, tm)])
            bfree(bk, sV, tv_)
    t_qkv_T = sT.n
    if stop_stage <= 2:
        return _finish_early(nc, P, [(sT, sT.n), (sV, sV.n), (sS, sS.n),
                                     (dS, dS.n), (dG, dG.n)])

    war_pt = [[], []]
    war_cvec = []
    war_pcs = [[], []]
    war_wTh = [[], []]
    war_band1 = [[], []]
    war_band2 = [[], []]
    war_zt = [[], []]
    war_rz = [[], []]
    war_bandacc = [[], []]
    t_comb_prev = 0

    tbz0 = V(lambda e: e.memset(band1[0], 0.0), w=[(sT, t_qkv_T)])
    tbz1 = V(lambda e: e.memset(band1[1], 0.0), w=[(sT, t_qkv_T)])
    tbz2 = V(lambda e: e.memset(band1[2], 0.0), w=[(sT, t_qkv_T)])
    war_band1 = [[(sV, tbz0)], [(sV, tbz1)], [(sV, tbz2)]]

    for h in range(H):
        hp = h % 2
        hb = hp * 64
        icb = (h // 2) * 2048
        tp = (0, 64) if hp else None
        qsl_all = q_fm[hb:hb + 64, icb:icb + 2048]
        ksl_all = k_fm[hb:hb + 64, icb:icb + 2048]

        import os as _os2
        _kmin = _os2.environ.get("KMIN", "")
        if _kmin:
            t_wTr = V(lambda e, hp=hp: e.memset(wTh[hp], 0.001),
                      w=[(sT, t_qkv_T)])
            t_pcs = [sT.n if sT.n else 1] * NT
            t_pcsev = t_wTr
            twt = sT.n if sT.n else 1
        _kmin_noA = _kmin in ("1", "3")
        _kmin_noScan = _kmin in ("1", "2")
        Tdrain()
        if _kmin_noA if _kmin else False:
            tcv = None
        else:
            V(lambda e: e.memset(cvecP[0], 0.0), w=war_cvec, inc=False)
            tcv = V(lambda e: e.memset(cvecP[1], 0.0), w=war_cvec)
        t_pcs = [0] * NT
        t_exp = {}
        t_cv = [0] * NT

        def emit_pcs(i):
            W = 128 * (i + 1)
            ngrp = (W + 511) // 512
            Tdrain()
            tpc = None
            for Gi in range(ngrp):
                n0, n1 = 512 * Gi, min(W, 512 * Gi + 512)
                first = (i == 4 * Gi)
                wpc = [(sV, t_cv[i]), (sS, t_exp[(i, Gi)])]
                if first and n1 < 512 * (Gi + 1):
                    tz = V(lambda e, p=i % 2, n1=n1, Gi=Gi: e.memset(
                        Ptm[p][:, n1:512 * (Gi + 1)], 0.0),
                        w=[(sT, t_pcs[i - 2])] if i >= 2 else [])
                    wpc.append((sV, tz))
                    n1 = 512 * (Gi + 1)
                if first:
                    wpc += bwait(4 + Gi) + war_pcs[hp]
                tpc = T(lambda e, n0=n0, n1=n1, first=first, i=i: e.matmul(
                    PS[0:32, 2048 + n0: 2048 + n1], cvecP[i % 2],
                    Ptm[i % 2][:, n0:n1],
                    start=first, stop=(i == NT - 1), skip_group_check=True),
                    w=wpc, inc=(Gi == ngrp - 1))
            t_pcs[i] = tpc

        for i in ([] if (_kmin and _kmin_noA) else range(NT)):
            W = 128 * (i + 1)
            par = i % 2
            ngrp = (W + 511) // 512
            qsl = qsl_all[:, i * 128:(i + 1) * 128]
            tsm_l = {}
            for Gi in range(ngrp):
                n0, n1 = 512 * Gi, min(W, 512 * Gi + 512)
                w0 = bwait(Gi)
                if h == 0 and i == 0 and Gi == 0:
                    w0 = w0 + [(sT, t_qkv_T)]
                lastg = (Gi == ngrp - 1)
                if not lastg:
                    tsm = T(lambda e, n0=n0, n1=n1, q=qsl, ks=ksl_all: e.matmul(
                        PS[:, n0:n1], q, ks[:, n0:n1],
                        start=True, stop=True, skip_group_check=True), w=w0)
                else:
                    T(lambda e, n0=n0, n1=n1, q=qsl, ks=ksl_all: e.matmul(
                        PS[:, n0:n1], q, ks[:, n0:n1],
                        start=True, stop=False, skip_group_check=True),
                        w=w0, inc=False)
                    tsm = T(lambda e, i=i: e.matmul(
                        PS[:, i * 128:(i + 1) * 128], csb["strictu"], csb["negi"],
                        start=False, stop=True, skip_group_check=True))
                tsm_l[Gi] = tsm
            for Gi in range(ngrp):
                n0, n1 = 512 * Gi, min(W, 512 * Gi + 512)
                wx = [(sT, tsm_l[Gi])]
                if i >= 2:
                    wx.append((sT, t_pcs[i - 2]))
                wx += war_pt[par]
                te = S(lambda e, p=par, n0=n0, n1=n1, g=Gi: e.activation(
                    out=Ptm[p][:, n0:n1], in_=PS[:, n0:n1], func=AF.Exp,
                    scale=0.125, accum_out=zparts[:, p * 4 + g: p * 4 + g + 1]),
                    w=wx)
                t_exp[(i, Gi)] = te
                bfree(Gi, sS, te)
            war_pt[par] = []
            z1 = V(lambda e, p=par, ng=ngrp: e.reduce_sum(
                out=Zc[:, p:p + 1], in_=zparts[:, p * 4: p * 4 + ng], axis=AX),
                w=[(sS, t_exp[(i, ngrp - 1)])])
            z2 = V(lambda e, p=par: e.reciprocal(out=rZc[:, p:p + 1],
                                                 in_=Zc[:, p:p + 1]), w=[(sV, z1)])
            wc = [(sV, z2), (sV, tcv), (sV, t_crep)]
            if i >= 2:
                V(lambda e, i=i, p=par: e.memset(
                    cvecP[p][:, 2 * i - 4: 2 * i - 2], 0.0),
                    w=[(sT, t_pcs[i - 2])], inc=False)
            cv1 = V(lambda e, i=i, h=h, p=par: e.tensor_tensor(
                out=cvecP[p][0:64, 2 * i: 2 * i + 1], in0=crep[0:64, h:h + 1],
                in1=rZc[0:64, p:p + 1], op=ALU.mult), w=wc)
            cv2 = V(lambda e, i=i, h=h, p=par: e.tensor_tensor(
                out=cvecP[p][64:128, 2 * i + 1: 2 * i + 2], in0=crep[64:128, h:h + 1],
                in1=rZc[64:128, p:p + 1], op=ALU.mult), w=wc)
            t_cv[i] = cv2
            emit_pcs(i)
        if _kmin and _kmin_noA:
            pass
        else:
            war_cvec = [(sT, t_pcs[NT - 1])]
        if (not _kmin) or not _kmin_noScan:
            if stop_stage == 21 and h == 0:
                return _finish_early(nc, P, [(sT, sT.n), (sV, sV.n), (sS, sS.n),
                                             (dS, dS.n), (dG, dG.n)])
            tevp = None
            for g in range(4):
                tevp = V(lambda e, g=g, hp=hp: e.tensor_copy(
                    out=pcsS[hp][0:32, 512 * g: 512 * (g + 1)],
                    in_=PS[0:32, 2048 + 512 * g: 2048 + 512 * (g + 1)]),
                    w=[(sT, t_pcs[NT - 1])] + war_pcs[hp])
                bfree(4 + g, sV, tevp)
            t_pcsev = tevp
            war_pcs[hp] = []

            s1_ = T(lambda e, h=h: e.matmul(PS[0:32, 3584:3585], csb["ones32f"],
                                            lnmega[0:1, h:h + 1], start=True, stop=True,
                                            skip_group_check=True),
                    w=[(sS, t_lnmega)] + bwait(7))
            s2_ = V(lambda e: e.tensor_copy(out=lnm32, in_=PS[0:32, 3584:3585]),
                    w=[(sT, s1_)])
            s3_ = V(lambda e: e.tensor_scalar_mul(out=EA, in0=csb["emat"], scalar1=lnm32),
                    w=[(sV, s2_)])
            s4_ = S(lambda e: e.activation(out=EA, in_=EA, func=AF.Exp), w=[(sV, s3_)])
            s5_ = V(lambda e: e.tensor_tensor(out=ATf, in0=EA, in1=csb["ltri"],
                                              op=ALU.mult), w=[(sS, s4_)])
            s6_ = V(lambda e: e.tensor_copy(out=ATb, in_=ATf), w=[(sV, s5_)])
            s7_ = V(lambda e, hp=hp: e.reduce_sum(out=Pcol, in_=pcsS[hp][0:32, :], axis=AX),
                    w=[(sV, t_pcsev)])
            s8_ = V(lambda e: e.tensor_copy(out=Pcolb[0:32, 0:1], in_=Pcol), w=[(sV, s7_)])
            s9_ = T(lambda e: e.matmul(PS[0:32, 3585:3586], ATb, Pcolb[0:32, 0:1],
                                       start=True, stop=True, skip_group_check=True),
                    w=[(sV, s6_), (sV, s8_)])
            s10_ = V(lambda e: e.tensor_scalar_add(out=rn0, in0=PS[0:32, 3585:3586],
                                                   scalar1=1e-5), w=[(sT, s9_)])
            s11_ = V(lambda e: e.reciprocal(out=rnorm, in_=rn0), w=[(sV, s10_)])
            s12_ = T(lambda e: e.transpose(PS[0:1, 3586:3618], rnorm,
                                           csb["idnf32"][0:32, 0:32]), w=[(sV, s11_)])
            s13_ = V(lambda e: e.tensor_copy(out=rnrow, in_=PS[0:1, 3586:3618]),
                     w=[(sT, s12_)])
            s14_ = T(lambda e: e.matmul(PS[0:32, 3618:3650], csb["ones32f"], rnrow,
                                        start=True, stop=True, skip_group_check=True),
                     w=[(sV, s13_)])
            s15_ = V(lambda e: e.tensor_copy(out=rnb32, in_=PS[0:32, 3618:3650]),
                     w=[(sT, s14_)])
            s16_ = V(lambda e: e.tensor_tensor(out=A2b, in0=ATf, in1=rnb32, op=ALU.mult),
                     w=[(sV, s15_)])
            bfree(7, sV, s15_)
            twt = None
            for c in range(NCH):
                twt = T(lambda e, c=c, hp=hp: e.matmul(
                    PS[:, 3072 + 32 * c: 3072 + 32 * (c + 1)],
                    pcsS[hp][0:32, 128 * c: 128 * (c + 1)], A2b,
                    start=True, stop=True, skip_group_check=True),
                    w=[(sV, s16_)] + (bwait(6) if c == 0 else []), inc=(c == NCH - 1))
            war_pcs[hp] = [(sT, twt)]
            t_wTr = V(lambda e, hp=hp: e.tensor_copy(out=wTh[hp], in_=PS[:, 3072:3584]),
                      w=[(sT, twt)] + war_wTh[hp])
            if stop_stage == 22 and h == 0:
                return _finish_early(nc, P, [(sT, sT.n), (sV, sV.n), (sS, sS.n),
                                             (dS, dS.n), (dG, dG.n)])

        Tdrain()
        sfree = {0: list(bank_free[0]), 1: list(bank_free[1])}
        t_stm = {}
        t_wg = {}
        t_pv = [0] * NCH
        t_bmm = [0] * NCH
        t_b2m = [0] * NCH
        t_zt = [0] * NCH
        t_ztread = [0] * NCH
        t_rzev = [0] * NCH
        t_cmbm = [0] * NCH
        t_cmb = [0] * NCH
        t_b1w = [0] * NCH
        def emit_fin(c):
            bac = 3072 + (c % 2) * 128
            wbm = [(sV, t_b2m[c])]
            if c > 1:
                wbm += [(sV, t_cmbm[c - 2])]
            else:
                wbm += war_bandacc[c % 2] + bwait(6)
            if c > 0:
                wbm += [(sS, t_b1w[c])]
                vsl1 = v_tm[:, (c - 1) * 1024 + h * 64: (c - 1) * 1024 + (h + 1) * 64]
                T(lambda e, c=c, vs=vsl1, hb=hb, tp=tp, bac=bac: e.matmul(
                    PS[hb:hb + 64, bac:bac + 128], vs, band1[c % 3],
                    start=True, stop=False, skip_group_check=True,
                    tile_position=tp), w=wbm, inc=False)
            vsl = v_tm[:, c * 1024 + h * 64: c * 1024 + (h + 1) * 64]
            t_bmm[c] = T(lambda e, c=c, vs=vsl, hb=hb, tp=tp, bac=bac: e.matmul(
                PS[hb:hb + 64, bac:bac + 128], vs, band2P[c % 2],
                start=(c == 0), stop=True, skip_group_check=True,
                tile_position=tp), w=wbm)
            if stop_stage == 53 and h == 0 and c == 0:
                return _finish_early(nc, P, [(sT, sT.n), (sV, sV.n), (sS, sS.n),
                                             (dS, dS.n), (dG, dG.n)])
            Tdrain()
            ztc = 3712 + (c % 2) * 128
            wzt = [(sV, t_b2m[c])]
            if c > 1:
                wzt += [(sV, t_ztread[c - 2])]
            else:
                wzt += war_zt[c % 2]
            if c > 0:
                wzt += [(sS, t_b1w[c])]
                T(lambda e, c=c, ztc=ztc: e.matmul(
                    PS[0:1, ztc:ztc + 128], csb["onescol"], band1[c % 3],
                    start=True, stop=False, skip_group_check=True), w=wzt, inc=False)
            t_zt[c] = T(lambda e, c=c, ztc=ztc: e.matmul(
                PS[0:1, ztc:ztc + 128], csb["onescol"], band2P[c % 2],
                start=(c == 0), stop=True, skip_group_check=True), w=wzt)
            if stop_stage == 54 and h == 0 and c == 0:
                return _finish_early(nc, P, [(sT, sT.n), (sV, sV.n), (sS, sS.n),
                                             (dS, dS.n), (dG, dG.n)])
            pr2 = c % 2
            u1 = V(lambda e, pr2=pr2, ztc=ztc: e.tensor_copy(
                out=ztrowP[pr2], in_=PS[0:1, ztc:ztc + 128]), w=[(sT, t_zt[c])])
            t_ztread[c] = u1
            u2 = V(lambda e, pr2=pr2: e.reciprocal(out=ztr2P[pr2],
                                                   in_=ztrowP[pr2]), w=[(sV, u1)])
            u3 = V(lambda e, pr2=pr2: e.tensor_tensor(
                out=rztwbP[pr2], in0=ztr2P[pr2], in1=twrow, op=ALU.mult),
                w=[(sV, u2), (sV, t_twrow)])
            rzc = 3328 + pr2 * 128
            wrz = [(sV, u3)]
            if c > 1:
                wrz.append((sV, t_rzev[c - 2]))
            else:
                wrz += war_rz[pr2]
            Tdrain()
            trz = T(lambda e, hb=hb, tp=tp, pr2=pr2, rzc=rzc: e.matmul(
                PS[hb:hb + 64, rzc:rzc + 128], csb["ones64"], rztwbP[pr2],
                start=True, stop=True, skip_group_check=True, tile_position=tp),
                w=wrz)
            if stop_stage == 55 and h == 0 and c == 0:
                return _finish_early(nc, P, [(sT, sT.n), (sV, sV.n), (sS, sS.n),
                                             (dS, dS.n), (dG, dG.n)])
            u4 = V(lambda e, hb=hb, pr2=pr2, rzc=rzc: e.tensor_copy(
                out=rztw_sbP[pr2][hb:hb + 64, :],
                in_=PS[hb:hb + 64, rzc:rzc + 128]), w=[(sT, trz)])
            t_rzev[c] = u4
            u5 = V(lambda e, hb=hb, pr2=pr2, bac=bac: e.tensor_tensor(
                out=tcombP[pr2][hb:hb + 64, :], in0=PS[hb:hb + 64, bac:bac + 128],
                in1=rztw_sbP[pr2][hb:hb + 64, :], op=ALU.mult),
                w=[(sV, u4), (sT, t_bmm[c])])
            t_cmbm[c] = u5
            qc0 = 1024 + 512 * (c // 4) + (c % 4) * 128
            u6 = V(lambda e, qc0=qc0, hb=hb, pr2=pr2: e.tensor_tensor(
                out=t2combP[pr2][hb:hb + 64, :], in0=PS[hb:hb + 64, qc0:qc0 + 128],
                in1=owrep2[hb:hb + 64, :], op=ALU.mult),
                w=[(sT, t_pv[c]), (sV, t_owrep)])
            u7 = V(lambda e, c=c, hb=hb, icb=icb, pr2=pr2: e.tensor_tensor(
                out=y_fm[hb:hb + 64, icb + c * 128: icb + (c + 1) * 128],
                in0=tcombP[pr2][hb:hb + 64, :], in1=t2combP[pr2][hb:hb + 64, :],
                op=ALU.add), w=[(sV, u5), (sV, u6)])
            t_cmb[c] = u7

        for c in range(NCH):
            Gmin = c // 4
            Tdrain()
            ksl = ksl_all[:, c * 128:(c + 1) * 128]
            G2b = Gmin
            G1b = (128 * c + 128) // 512 if c < NCH - 1 else -1
            off0 = 128 * c - 512 * G2b
            sb0 = G2b % 2
            tb2 = None
            tb1 = None
            for Gi in range(Gmin, 4):
                n0, n1 = max(128 * c, 512 * Gi), 512 * (Gi + 1)
                sb = Gi % 2
                off = n0 - 512 * Gi
                w0 = list(sfree[sb])
                sfree[sb] = []
                tsm = T(lambda e, n0=n0, n1=n1, sb=sb, off=off, k=ksl,
                        qs=qsl_all: e.matmul(
                    PS[:, sb * 512 + off: sb * 512 + off + (n1 - n0)],
                    k, qs[:, n0:n1], start=True, stop=True, skip_group_check=True),
                    w=w0)
                t_stm[(c, Gi)] = tsm
                wact = []
                if Gi == G2b:
                    wb2 = [(sT, tsm)]
                    if c > 1:
                        wb2 += [(sT, t_bmm[c - 2]), (sT, t_zt[c - 2])]
                    else:
                        wb2 += war_band2[c % 2]
                    tb2 = S(lambda e, sb0=sb0, off0=off0, c=c: e.activation(
                        out=band2P[c % 2],
                        in_=PS[:, sb0 * 512 + off0: sb0 * 512 + off0 + 128],
                        func=AF.Exp, scale=0.125), w=wb2)
                    sfree[sb0].append((sS, tb2))
                    wact.append((sS, tb2))
                if Gi == G1b:
                    colB1 = 128 * c + 128
                    off1 = colB1 - 512 * G1b
                    sb1 = G1b % 2
                    wb1 = [(sT, tsm)]
                    if c > 1:
                        wb1 += [(sT, t_bmm[c - 2]), (sT, t_zt[c - 2])]
                    else:
                        wb1 += war_band1[(c + 1) % 3]
                    tb1 = S(lambda e, sb1=sb1, off1=off1, c=c: e.activation(
                        out=band1[(c + 1) % 3][64:128, 0:64],
                        in_=PS[64:128, sb1 * 512 + off1: sb1 * 512 + off1 + 64],
                        func=AF.Exp, scale=0.125), w=wb1)
                    t_b1w[c + 1] = tb1
                    sfree[sb1].append((sS, tb1))
                    wact.append((sS, tb1))
                nr = (n1 - n0) // 64
                r0 = n0 // 64
                wwg = [(sT, tsm), (sV, t_wTr)] + wact
                if c >= 2:
                    wwg.append((sT, t_pv[c - 2]))
                wwg += war_pt[c % 2]
                twg = V(lambda e, n0=n0, n1=n1, sb=sb, off=off, nr=nr, r0=r0,
                        c=c, hp=hp: e.tensor_tensor(
                    out=stripB[c % 2][:, n0:n1].rearrange("p (r z) -> p r z", z=64),
                    in0=PS[:, sb * 512 + off: sb * 512 + off + (n1 - n0)]
                        .rearrange("p (r z) -> p r z", z=64),
                    in1=wTh[hp][:, 32 * c + r0: 32 * c + r0 + nr]
                        .unsqueeze(2).broadcast_to([128, nr, 64]),
                    op=ALU.mult), w=wwg)
                t_wg[(c, Gi)] = twg
                sfree[sb].append((sV, twg))
            war_pt[c % 2] = []
            t_b2m[c] = V(lambda e, c=c: e.tensor_tensor(out=band2P[c % 2],
                                               in0=band2P[c % 2],
                                               in1=csb["tri01"],
                                               op=ALU.mult), w=[(sS, tb2)])
            if stop_stage == 51 and h == 0 and c == 0:
                return _finish_early(nc, P, [(sT, sT.n), (sV, sV.n), (sS, sS.n),
                                             (dS, dS.n), (dG, dG.n)])
            tpv = None
            for Qd in range(Gmin, 4):
                n0q, n1q = max(512 * Qd, 128 * c), 512 * (Qd + 1)
                qoff = n0q - 512 * Qd
                wpv = [(sV, t_wg[(c, Qd)])]
                if c == 0:
                    wpv += bwait(2 + Qd)
                    if t_comb_prev:
                        wpv.append((sV, t_comb_prev))
                vsl = v_tm[:, c * 1024 + h * 64: c * 1024 + (h + 1) * 64]
                tpv = T(lambda e, Qd=Qd, n0q=n0q, n1q=n1q, qoff=qoff, c=c,
                        vs=vsl, hb=hb, tp=tp: e.matmul(
                    PS[hb:hb + 64,
                       1024 + 512 * Qd + qoff: 1024 + 512 * Qd + qoff + (n1q - n0q)],
                    vs, stripB[c % 2][:, n0q:n1q],
                    start=(c == 0), stop=(c == 4 * Qd + 3), skip_group_check=True,
                    tile_position=tp), w=wpv, inc=(Qd == 3))
            t_pv[c] = tpv
            if stop_stage == 52 and h == 0 and c == 0:
                return _finish_early(nc, P, [(sT, sT.n), (sV, sV.n), (sS, sS.n),
                                             (dS, dS.n), (dG, dG.n)])
            emit_fin(c)

            if stop_stage == 23 + c and h == 0:
                return _finish_early(nc, P, [(sT, sT.n), (sV, sV.n), (sS, sS.n),
                                             (dS, dS.n), (dG, dG.n)])
        t_comb_prev = t_cmb[NCH - 1]
        war_pt[0] = [(sT, t_pv[NCH - 2])]
        war_pt[1] = [(sT, t_pv[NCH - 1])]
        war_wTh[hp] = [(sV, t_wg[(NCH - 1, 3)])]
        war_band1 = [[(sT, t_bmm[15]), (sT, t_zt[15])],
                     [(sT, t_bmm[13]), (sT, t_zt[13])],
                     [(sT, t_bmm[14]), (sT, t_zt[14])]]
        war_band2 = [[(sT, t_bmm[NCH - 2]), (sT, t_zt[NCH - 2])],
                     [(sT, t_bmm[NCH - 1]), (sT, t_zt[NCH - 1])]]
        war_zt = [[(sV, t_ztread[NCH - 2])], [(sV, t_ztread[NCH - 1])]]
        war_rz = [[(sV, t_rzev[NCH - 2])], [(sV, t_rzev[NCH - 1])]]
        war_bandacc = [[(sV, t_cmbm[NCH - 2])], [(sV, t_cmbm[NCH - 1])]]
        bank_free[6] = [(sV, t_cmbm[NCH - 1])]
        bank_free[7] = [(sV, t_ztread[NCH - 1])]
        for qd in range(4):
            bank_free[2 + qd] = [(sV, t_cmb[NCH - 1])]
        for sb in (0, 1):
            bank_free[sb] = list(sfree[sb])
        if stop_stage == 3 and h == 0:
            return _finish_early(nc, P, [(sT, sT.n), (sV, sV.n), (sS, sS.n),
                                         (dS, dS.n), (dG, dG.n)])

    t_attn_T = sT.n
    if stop_stage <= 4:
        return _finish_early(nc, P, [(sT, sT.n), (sV, sV.n), (sS, sS.n),
                                     (dS, dS.n), (dG, dG.n)])
    t_attn_V = sV.n
    t_attn_S = sS.n

    Tdrain()
    w_att = [(sT, t_attn_T), (sV, t_attn_V), (sS, t_attn_S)]
    twc = None
    for ic in range(8):
        twc = SY(lambda e, ic=ic: e.dma_start(
            out=wcT[:, ic * 1024:(ic + 1) * 1024],
            in_=wbf[3][:, ic * 128:(ic + 1) * 128], transpose=True),
            w=[(dG, tw_wbf[3])] + w_att)
    t_wcT = twc
    t_b1 = G(lambda e: e.dma_start(
        out=b1row, in_=b1d.rearrange("e h -> (e h)").unsqueeze(0)), w=w_att)
    t_b2 = G(lambda e: e.dma_start(out=b2_sb, in_=b2d), w=w_att)

    t_w1use = {}
    t_w2use = {}
    t_hm = {0: 0, 1: 0}
    t_eo_hl = {}
    t_evac = {}
    t_odma = {}
    for s_ in range(4):
        for e_ in range(NE):
            eb = e_ % 2
            m = s_ * NE + e_
            wld = [(dG, t_w1bf[e_])] + w_att
            if m >= 2:
                wld.append((sT, t_w1use[m - 2]))
            tld = SY(lambda e, i=e_, eb=eb: e.dma_start(out=w1b[eb], in_=w1bf[i]),
                     w=wld)
            gcb = 2 + eb
            tgbc = T(lambda e, i=e_, s=s_, gcb=gcb: e.matmul(
                PS[:, gcb * 512:(gcb + 1) * 512], csb["e8"][:, i * 128:(i + 1) * 128],
                g_t8[0:8, s * 512:(s + 1) * 512], start=True, stop=True,
                skip_group_check=True), w=[(sV, t_ln_done)] + bwait(gcb))
            tlast = None
            thm = None
            for hc in range(4):
                bk = (e_ * 4 + hc) % 2
                w0 = [(dS, tld)] + bwait(bk)
                for icc in range(8):
                    T(lambda e, icc=icc, hc=hc, eb=eb, s=s_, bk=bk: e.matmul(
                        PS[:, bk * 512:(bk + 1) * 512],
                        w1b[eb][:, icc * 512 + hc * 128: icc * 512 + (hc + 1) * 128],
                        xn_fm[:, icc * 2048 + s * 512: icc * 2048 + (s + 1) * 512],
                        start=(icc == 0), stop=False, skip_group_check=True),
                        w=w0, inc=False)
                tb1m = T(lambda e, i=e_, hc=hc, bk=bk: e.matmul(
                    PS[:, bk * 512:(bk + 1) * 512],
                    b1row[0:1, i * 512 + hc * 128: i * 512 + (hc + 1) * 128],
                    csb["onesrow"], start=False, stop=True, skip_group_check=True),
                    w=[(dG, t_b1)])
                tlast = tb1m
                wgl = [(sT, tb1m)]
                if t_hm[bk]:
                    wgl.append((sV, t_hm[bk]))
                tgl = S(lambda e, bk=bk: e.activation(
                    out=hgel[bk], in_=PS[:, bk * 512:(bk + 1) * 512], func=AF.Gelu),
                    w=wgl)
                bfree(bk, sS, tgl)
                whm = [(sS, tgl), (sT, tgbc)]
                if s_ > 0 and (s_ * NE + e_) in ():
                    pass
                if s_ > 0:
                    whm.append((sT, t_eo_hl[s_ - 1]))
                thm = V(lambda e, i=e_, hc=hc, bk=bk, gcb=gcb: e.tensor_tensor(
                    out=hprime[:, (i * 4 + hc) * 512:(i * 4 + hc + 1) * 512],
                    in0=hgel[bk], in1=PS[:, gcb * 512:(gcb + 1) * 512], op=ALU.mult),
                    w=whm)
                t_hm[bk] = thm
            bfree(gcb, sV, thm)
            t_w1use[m] = tlast
        treg = [0] * 8
        for e_ in range(NE):
            eb = e_ % 2
            m = s_ * NE + e_
            wld = [(dG, t_w2bf[e_])] + w_att
            if m >= 2:
                wld.append((sT, t_w2use[m - 2]))
            tld = SY(lambda e, i=e_, eb=eb: e.dma_start(out=w2b[eb], in_=w2bf[i]),
                     w=wld)
            tm = None
            for hc in range(4):
                for reg in range(8):
                    tc4, og = reg // 2, reg % 2
                    first = (e_ == 0 and hc == 0)
                    w0 = [(dS, tld)]
                    if first:
                        w0 += bwait(reg)
                    hsl = hprime[:, (e_ * 4 + hc) * 512 + tc4 * 128:
                                 (e_ * 4 + hc) * 512 + (tc4 + 1) * 128]
                    tm = T(lambda e, hs=hsl, eb=eb, hc=hc, og=og, reg=reg,
                           first=first: e.matmul(
                        PS[:, reg * 512:(reg + 1) * 512], hs,
                        w2b[eb][:, hc * 1024 + og * 512: hc * 1024 + (og + 1) * 512],
                        start=first, stop=False, skip_group_check=True),
                        w=w0, inc=(hc == 3 and reg == 7))
            t_w2use[m] = tm
        t_eo_hl[s_] = t_w2use[s_ * NE + NE - 1]
        for reg in range(8):
            tc4, og = reg // 2, reg % 2
            gtc = s_ * 4 + tc4
            T(lambda e, gtc=gtc, og=og, reg=reg: e.matmul(
                PS[:, reg * 512:(reg + 1) * 512],
                g_t8[0:8, gtc * 128:(gtc + 1) * 128],
                b2_sb[0:8, og * 512:(og + 1) * 512],
                start=False, stop=False, skip_group_check=True),
                w=[(dG, t_b2)], inc=False)
            for icc in range(8):
                treg[reg] = T(lambda e, icc=icc, gtc=gtc, og=og, reg=reg: e.matmul(
                    PS[:, reg * 512:(reg + 1) * 512],
                    y_fm[:, icc * 2048 + gtc * 128: icc * 2048 + (gtc + 1) * 128],
                    wcT[:, icc * 1024 + og * 512: icc * 1024 + (og + 1) * 512],
                    start=False, stop=(icc == 7), skip_group_check=True),
                    w=[(dS, t_wcT)], inc=(icc == 7))
        for tc4 in range(4):
            gtc = s_ * 4 + tc4
            pe = gtc % 2
            wxl = w_att + ([(sV, t_evac[gtc - 2])] if gtc >= 2 else [])
            txl = SY(lambda e, gtc=gtc, pe=pe: e.dma_start(
                out=xst2[pe], in_=x[gtc * 128:(gtc + 1) * 128, :]), w=wxl)
            tev = None
            for og in range(2):
                reg = tc4 * 2 + og
                wev = [(sT, treg[reg]), (dS, txl)]
                if gtc >= 2:
                    wev.append((sO, t_odma[gtc - 2]))
                tev = V(lambda e, pe=pe, og=og, reg=reg: e.tensor_tensor(
                    out=outst[pe][:, og * 512:(og + 1) * 512],
                    in0=PS[:, reg * 512:(reg + 1) * 512],
                    in1=xst2[pe][:, og * 512:(og + 1) * 512], op=ALU.add), w=wev)
                bfree(reg, sV, tev)
            t_evac[gtc] = tev
            t_odma[gtc] = SYo(lambda e, gtc=gtc, pe=pe: e.dma_start(
                out=out[gtc * 128:(gtc + 1) * 128, :], in_=outst[pe]),
                w=[(sV, tev)])

    if DEBUG:
        wd = [(sO, t_odma[NT - 1])]
        G(lambda e: e.dma_start(out=dbg["d_xn"], in_=xn_fm), w=wd)
        G(lambda e: e.dma_start(out=dbg["d_q"], in_=q_fm), w=wd)
        G(lambda e: e.dma_start(out=dbg["d_k"], in_=k_fm), w=wd)
        G(lambda e: e.dma_start(out=dbg["d_v"], in_=v_tm), w=wd)
        G(lambda e: e.dma_start(out=dbg["d_y"], in_=y_fm), w=wd)
        G(lambda e: e.dma_start(out=dbg["d_g"], in_=g_t8), w=wd)
        G(lambda e: e.dma_start(out=dbg["d_pcs"], in_=pcsS[1][0:32, :]), w=wd)
        G(lambda e: e.dma_start(out=dbg["d_wt"], in_=wTh[1]), w=wd)
        P.add("gpsimd", lambda e: e.nop(), w=[(dG, dG.n)])

    P.add("sync", lambda e: e.nop(), w=[(sO, 16 * NT)])

    with nc.Block() as block:
        @block.sync
        def _(e):
            P.replay("sync", e)

        @block.tensor
        def _(e):
            P.replay("tensor", e)

        @block.vector
        def _(e):
            P.replay("vector", e)

        @block.scalar
        def _(e):
            P.replay("scalar", e)

        @block.gpsimd
        def _(e):
            P.replay("gpsimd", e)

    return nc


_NC_CACHE = None


def kernel(**inputs):
    global _NC_CACHE
    xs = np.asarray(inputs["L0_attn_hidden_states"])
    base = {
        "Wq": np.ascontiguousarray(inputs["L0_attn_Wq"], np.float32),
        "Wk": np.ascontiguousarray(inputs["L0_attn_Wk"], np.float32),
        "Wv": np.ascontiguousarray(inputs["L0_attn_Wv"], np.float32),
        "Wc": np.ascontiguousarray(inputs["L0_attn_Wc"], np.float32),
        "terr": np.ascontiguousarray(inputs["L0_attn_terrace_mix"],
                                     np.float32).reshape(1, SEG),
        "idxd": np.ascontiguousarray(inputs["L0_attn_index_decays"], np.float32),
        "megad": np.ascontiguousarray(inputs["L0_attn_mega_decays"],
                                      np.float32).reshape(1, H),
        "w1": np.ascontiguousarray(inputs["L0_ffn_w1"], np.float32),
        "b1": np.ascontiguousarray(inputs["L0_ffn_b1"], np.float32),
        "w2": np.ascontiguousarray(inputs["L0_ffn_w2"], np.float32),
        "b2": np.ascontiguousarray(inputs["L0_ffn_b2"], np.float32),
        "cent": np.ascontiguousarray(inputs["L0_ffn_centroids"], np.float32),
        "wrt": np.ascontiguousarray(inputs["L0_ffn_w_route"], np.float32),
    }
    for k, v in _host_consts().items():
        base["c_" + k] = v
    if _NC_CACHE is None:
        _NC_CACHE = build()
    nc = _NC_CACHE
    in_maps = []
    for c in range(NCORE):
        m = dict(base)
        m["x"] = np.ascontiguousarray(xs[c], np.float32)
        in_maps.append(m)
    res = run_bass_kernel_spmd(nc, in_maps, core_ids=list(range(NCORE)))
    return np.stack([np.asarray(res.results[c]["out"]) for c in range(NCORE)],
                    axis=0)


# version 08
# speedup vs baseline: 1.3547x; 1.1347x over previous
import numpy as np
import ml_dtypes
import concourse.bass as bass
import concourse.mybir as mybir
from concourse.bass_utils import run_bass_kernel_spmd

F32 = mybir.dt.float32
BF16 = mybir.dt.bfloat16
U8 = mybir.dt.uint8
AF = mybir.ActivationFunctionType
ALU = mybir.AluOpType
AX = mybir.AxisListType.X

L, D, H, SEG = 2048, 1024, 16, 64
NT, NCH = 16, 16
NE = 8
NCORE = 8
LN32 = float(np.log(32.0))

DEBUG = False


def _host_consts():
    bf = ml_dtypes.bfloat16
    f = np.arange(128)
    jj, rr = np.meshgrid(np.arange(32), np.arange(32), indexing="ij")
    e8 = np.zeros((8, 1024), dtype=bf)
    for e in range(8):
        e8[e, e * 128:(e + 1) * 128] = 1
    return {
        "idnf32": np.eye(128, dtype=np.float32),
        "tri01": (f[None, :] >= f[:, None]).astype(bf),
        "strictu": (f[None, :] < f[:, None]).astype(bf),
        "negi": (-30000.0 * np.eye(128)).astype(bf),
        "rep2": (np.arange(128)[None, :] % 64 == np.arange(64)[:, None]).astype(np.float32),
        "emat": np.clip(rr - 1 - jj, 0, 31).astype(np.float32),
        "ltri": (jj < rr).astype(np.float32),
        "e8": e8,
        "onesrow": np.ones((1, 512), dtype=bf),
        "ones64": np.ones((1, 64), dtype=bf),
        "onescol": np.ones((128, 1), dtype=bf),
        "ones32f": np.ones((1, 32), dtype=np.float32),
        "onesf128": np.ones((1, 128), dtype=np.float32),
    }


class Sem:
    __slots__ = ("h", "n")

    def __init__(self, h):
        self.h = h
        self.n = 0


class Tape:
    ENGS = ("sync", "tensor", "vector", "scalar", "gpsimd")

    def __init__(self):
        self.ops = {e: [] for e in self.ENGS}
        self.waited = {e: {} for e in self.ENGS}

    def add(self, eng, fn, w=(), inc=None, amt=1):
        ws = []
        for s, v in w:
            assert v is not None and v > 0, f"bad wait {v} on {eng}"
            k = id(s)
            if self.waited[eng].get(k, 0) >= v:
                continue
            self.waited[eng][k] = v
            ws.append((s.h, v))
        pair, tick = None, None
        if inc is not None:
            inc.n += amt
            tick = inc.n
            pair = (inc.h, amt)
        self.ops[eng].append((fn, tuple(ws), pair))
        return tick

    def replay(self, eng, e):
        for fn, ws, pair in self.ops[eng]:
            for h, v in ws:
                e.wait_ge(h, v)
            inst = fn(e)
            if pair is not None:
                inst.then_inc(pair[0], pair[1])


def _finish_early(nc, P, waits):
    P.add("sync", lambda e: e.nop(), w=[(s, v) for s, v in waits if v > 0])
    with nc.Block() as block:
        @block.sync
        def _(e):
            P.replay("sync", e)

        @block.tensor
        def _(e):
            P.replay("tensor", e)

        @block.vector
        def _(e):
            P.replay("vector", e)

        @block.scalar
        def _(e):
            P.replay("scalar", e)

        @block.gpsimd
        def _(e):
            P.replay("gpsimd", e)
    return nc


def build(stop_stage=99):
    import os
    nc = bass.Bass("TRN2", target_bir_lowering=False,
                   detect_race_conditions=False)
    nc.dram_tensor("salt" + os.environ.get("KSALT", "0"), [1, 1], F32,
                   kind="Internal")
    HC = _host_consts()

    x = nc.dram_tensor("x", [L, D], F32, kind="ExternalInput").ap()
    Wmats = [nc.dram_tensor(n, [D, D], F32, kind="ExternalInput").ap()
             for n in ("Wq", "Wk", "Wv", "Wc")]
    terr = nc.dram_tensor("terr", [1, SEG], F32, kind="ExternalInput").ap()
    idxd = nc.dram_tensor("idxd", [H, SEG], F32, kind="ExternalInput").ap()
    megad = nc.dram_tensor("megad", [1, H], F32, kind="ExternalInput").ap()
    w1d = nc.dram_tensor("w1", [NE, D, 512], F32, kind="ExternalInput").ap()
    b1d = nc.dram_tensor("b1", [NE, 512], F32, kind="ExternalInput").ap()
    w2d = nc.dram_tensor("w2", [NE, 512, D], F32, kind="ExternalInput").ap()
    b2d = nc.dram_tensor("b2", [NE, D], F32, kind="ExternalInput").ap()
    cent = nc.dram_tensor("cent", [NE, D], F32, kind="ExternalInput").ap()
    wrt = nc.dram_tensor("wrt", [NE, D], F32, kind="ExternalInput").ap()
    cin = {k: nc.dram_tensor("c_" + k, list(v.shape),
                             F32 if v.dtype == np.float32 else BF16,
                             kind="ExternalInput").ap()
           for k, v in HC.items()}
    out = nc.dram_tensor("out", [L, D], F32, kind="ExternalOutput").ap()
    wbf = [nc.dram_tensor(f"wbf{i}", [D, D], BF16, kind="Internal").ap()
           for i in range(4)]
    w1bf = nc.dram_tensor("w1bf", [NE, 128, 4096], BF16, kind="Internal").ap()
    w2bf = nc.dram_tensor("w2bf", [NE, 128, 4096], BF16, kind="Internal").ap()
    dbg = {}
    if DEBUG:
        for n, shp, dt in [("d_xn", [128, 16384], BF16), ("d_q", [128, 16384], BF16),
                           ("d_k", [128, 16384], BF16), ("d_v", [128, 16384], BF16),
                           ("d_y", [128, 16384], BF16), ("d_g", [8, 2048], BF16),
                           ("d_pcs", [32, 2048], BF16), ("d_wt", [128, 512], F32)]:
            dbg[n] = nc.dram_tensor(n, shp, dt, kind="ExternalOutput").ap()

    ar_xn = nc.alloc_sbuf_tensor("ar_xn", [128, 32768], U8).ap()
    ar_q = nc.alloc_sbuf_tensor("ar_q", [128, 32768], U8).ap()
    ar_k = nc.alloc_sbuf_tensor("ar_k", [128, 32768], U8).ap()
    ar_v = nc.alloc_sbuf_tensor("ar_v", [128, 32768], U8).ap()
    ar_y = nc.alloc_sbuf_tensor("ar_y", [128, 32768], U8).ap()
    ar_t = nc.alloc_sbuf_tensor("ar_t", [128, 28672], U8).ap()

    xn_fm = ar_xn.bitcast(BF16)
    q_fm = ar_q.bitcast(BF16)
    k_fm = ar_k.bitcast(BF16)
    v_tm = ar_v.bitcast(BF16)
    y_fm = ar_y.bitcast(BF16)
    wTb = [ar_y[:, 0:16384].bitcast(BF16), ar_y[:, 16384:32768].bitcast(BF16)]
    hprime = ar_q.bitcast(BF16)
    wcT = ar_k[:, 0:16384].bitcast(BF16)
    xst2 = [ar_k[:, 16384:20480].bitcast(F32), ar_k[:, 20480:24576].bitcast(F32)]
    outst = [ar_k[:, 24576:28672].bitcast(F32), ar_k[:, 28672:32768].bitcast(F32)]
    w1b = [ar_v[:, 0:8192].bitcast(BF16), ar_v[:, 8192:16384].bitcast(BF16)]
    w2b = [ar_v[:, 16384:24576].bitcast(BF16), ar_v[:, 24576:32768].bitcast(BF16)]

    xstage = [ar_t[:, 0:4096].bitcast(F32), ar_t[:, 4096:8192].bitcast(F32)]
    xn_st = [ar_t[:, 8192:12288].bitcast(F32), ar_t[:, 12288:16384].bitcast(F32)]
    sqdump = ar_t[:, 16384:20480].bitcast(F32)
    lnS = ar_t[:, 20480:24576].bitcast(F32)
    prepS = ar_t[:, 24576:28672].bitcast(F32)
    Ptm = [ar_t[:, 0:4096].bitcast(BF16), ar_t[:, 4096:8192].bitcast(BF16)]
    stripB = Ptm
    band1 = [ar_t[:, 8192:8448].bitcast(BF16), ar_t[:, 8448:8704].bitcast(BF16),
             None]
    band2P = [nc.alloc_sbuf_tensor("band2", [128, 128], BF16).ap(),
              ar_t[:, 8704:8960].bitcast(BF16)]
    pcsS = [ar_t[:, 8960:13056].bitcast(BF16), ar_t[:, 13056:17152].bitcast(BF16)]
    wTh = [ar_t[:, 17152:19200].bitcast(F32), ar_t[:, 19200:21248].bitcast(F32)]
    atS = ar_t[:, 21248:25344].bitcast(F32)
    hgel = [ar_t[:, 0:2048].bitcast(F32), ar_t[:, 2048:4096].bitcast(F32)]
    b1row = ar_t[0:1, 4096:12288].bitcast(BF16)
    b2_sb = ar_t[0:8, 12288:14336].bitcast(BF16)

    csb = {k: nc.alloc_sbuf_tensor("s_" + k, list(HC[k].shape),
                                   F32 if HC[k].dtype == np.float32 else BF16).ap()
           for k in HC}
    g_t8 = nc.alloc_sbuf_tensor("g_t8", [8, 2048], BF16).ap()
    twrow = nc.alloc_sbuf_tensor("twrow", [1, 128], F32).ap()
    owrep2 = nc.alloc_sbuf_tensor("owrep2", [128, 128], F32).ap()
    crep = nc.alloc_sbuf_tensor("crep", [128, 16], F32).ap()
    lnmega = nc.alloc_sbuf_tensor("lnmega", [1, 16], F32).ap()
    cnrb = nc.alloc_sbuf_tensor("cnrb", [128, 128], BF16).ap()
    centF = nc.alloc_sbuf_tensor("centF", [8, 1024], F32).ap()
    routeF = nc.alloc_sbuf_tensor("routeF", [8, 1024], F32).ap()

    msum, sq2 = lnS[:, 0:2], lnS[:, 2:4]
    mmean, varv = lnS[:, 4:6], lnS[:, 6:8]
    rstd, negm = lnS[:, 8:10], lnS[:, 10:12]
    recn, tvar = lnS[:, 12:14], lnS[:, 24:26]
    m1c, m2c, nm1 = lnS[:, 14:16], lnS[:, 16:18], lnS[:, 18:20]
    gsum, grec = lnS[:, 20:22], lnS[:, 22:24]
    lgec = [lnS[:, 32:48], lnS[:, 48:64]]
    lgf = [lnS[:, 64:72], lnS[:, 72:80]]
    ge1 = [lnS[:, 96:104], lnS[:, 104:112]]
    mskd = [lnS[:, 112:120], lnS[:, 120:128]]
    kept = [lnS[:, 128:136], lnS[:, 136:144]]
    egk = [lnS[:, 144:152], lnS[:, 152:160]]
    gtm = [lnS[:, 160:168], lnS[:, 168:176]]

    cn2, lncn, rcn = prepS[0:8, 0:1], prepS[0:8, 1:2], prepS[0:8, 2:3]
    idxF = prepS[0:16, 8:72]
    c16 = prepS[0:16, 72:136]
    callT = prepS[0:64, 136:152]
    megae, megac = prepS[0:1, 152:168], prepS[0:1, 168:184]
    twe, twp, tw64 = prepS[0:1, 184:248], prepS[0:1, 248:312], prepS[0:1, 312:376]
    ow64 = prepS[0:1, 376:440]
    owrow = prepS[0:1, 440:568]
    sq8b = prepS[0:8, 504:1016]

    zparts = atS[:, 0:8]
    Zc, rZc = atS[:, 8:10], atS[:, 10:12]
    EA, ATf = atS[0:32, 16:48], atS[0:32, 48:80]
    rnb32 = atS[0:32, 80:112]
    lnm32, Pcol = atS[0:32, 112:113], atS[0:32, 113:114]
    rn0, rnorm = atS[0:32, 114:115], atS[0:32, 115:116]
    rnrow = atS[0:1, 116:148]
    ztrowP = [atS[0:1, 148:276], prepS[0:1, 0:128]]
    ztr2P = [atS[0:1, 276:404], prepS[0:1, 128:256]]
    rztw_sbP = [atS[:, 404:532], prepS[:, 320:448]]
    tcombP = [atS[:, 532:660], prepS[:, 448:576]]
    t2combP = [atS[:, 660:788], prepS[:, 576:704]]
    ATb = prepS[0:32, 708:724].bitcast(BF16)
    A2b = prepS[0:32, 724:740].bitcast(BF16)
    Pcolb = prepS[0:32, 704:705].bitcast(BF16)
    rztwbP = [prepS[0:1, 740:804].bitcast(BF16),
              ar_t[0:1, 25600:25856].bitcast(BF16)]
    cvecP = [prepS[:, 804:820].bitcast(BF16), prepS[:, 820:836].bitcast(BF16)]
    band1[2] = prepS[:, 836:900].bitcast(BF16)

    PS = nc.alloc_psum_tensor("PS", [128, 4096], F32).ap()

    sT, sV, sS = Sem(nc.alloc_semaphore("sT")), Sem(nc.alloc_semaphore("sV")), \
        Sem(nc.alloc_semaphore("sS"))
    dS, dG, sO = Sem(nc.alloc_semaphore("dS")), Sem(nc.alloc_semaphore("dG")), \
        Sem(nc.alloc_semaphore("sO"))

    P = Tape()

    def T(fn, w=(), inc=True):
        return P.add("tensor", fn, w, sT if inc else None)

    def V(fn, w=(), inc=True):
        return P.add("vector", fn, w, sV if inc else None)

    def S(fn, w=(), inc=True):
        return P.add("scalar", fn, w, sS if inc else None)

    def SY(fn, w=()):
        return P.add("sync", fn, w, dS, 16)

    def SYo(fn, w=()):
        return P.add("sync", fn, w, sO, 16)

    def G(fn, w=()):
        return P.add("gpsimd", fn, w, dG, 16)

    def Tdrain():
        P.add("tensor", lambda e: e.nop(), w=[(sT, sT.n)] if sT.n else [])

    bank_free = {b: [] for b in range(8)}

    def bfree(b, sem, tick):
        bank_free[b] = [(sem, tick)]

    def bwait(*banks):
        ws = []
        for b in banks:
            ws += bank_free[b]
        return ws

    for k in HC:
        SY(lambda e, d=csb[k], s=cin[k]: e.dma_start(out=d, in_=s))
    SY(lambda e: e.dma_start(out=idxF, in_=idxd))
    SY(lambda e: e.dma_start(out=megae, in_=megad))
    SY(lambda e: e.dma_start(out=twe, in_=terr))
    SY(lambda e: e.dma_start(out=centF, in_=cent))
    SY(lambda e: e.dma_start(out=routeF, in_=wrt))
    t_prein = dS.n
    tw_wbf = [G(lambda e, d=wbf[i], s=Wmats[i]: e.dma_start(out=d, in_=s))
              for i in range(4)]
    t_w1bf, t_w2bf = [], []
    for e_ in range(NE):
        t_w1bf.append(G(lambda e, i=e_: e.dma_start(
            out=w1bf[i], in_=w1d[i].rearrange("(ic p) h -> p ic h", p=128))))
        t_w2bf.append(G(lambda e, i=e_: e.dma_start(
            out=w2bf[i], in_=w2d[i].rearrange("(hc p) o -> p hc o", p=128))))

    wp = [(dS, t_prein)]
    a1 = S(lambda e: e.activation(out=sq8b, in_=centF[:, 0:512], func=AF.Square,
                                  accum_out=cn2), w=wp)
    a2 = S(lambda e: e.activation(out=sq8b, in_=centF[:, 512:1024], func=AF.Square,
                                  accum_out=lncn), w=[(sS, a1)])
    a3 = V(lambda e: e.tensor_tensor(out=cn2, in0=cn2, in1=lncn, op=ALU.add),
           w=[(sS, a2)])
    a4 = S(lambda e: e.activation(out=lncn, in_=cn2, func=AF.Ln), w=[(sV, a3)])
    a5 = S(lambda e: e.activation(out=rcn, in_=lncn, func=AF.Exp, scale=-0.5),
           w=[(sS, a4)])
    a6 = V(lambda e: e.tensor_scalar(out=centF, in0=centF, scalar1=rcn,
                                     scalar2=1.0 / 32.0, op0=ALU.mult,
                                     op1=ALU.mult), w=[(sS, a5)])
    tt = None
    for ic in range(8):
        T(lambda e, ic=ic: e.transpose(
            PS[:, 3584 + ic * 16:3592 + ic * 16],
            centF[0:8, ic * 128:(ic + 1) * 128], csb["idnf32"][0:8, 0:8]),
            w=[(sV, a6)], inc=False)
        tt = T(lambda e, ic=ic: e.transpose(
            PS[:, 3592 + ic * 16:3600 + ic * 16],
            routeF[0:8, ic * 128:(ic + 1) * 128], csb["idnf32"][0:8, 0:8]),
            w=[(dS, t_prein)], inc=(ic == 7))
    a7 = V(lambda e: e.tensor_copy(out=cnrb, in_=PS[:, 3584:3712]), w=[(sT, tt)])
    bfree(7, sV, a7)
    b1_ = S(lambda e: e.activation(out=c16, in_=idxF, func=AF.Exp), w=wp)
    b2_ = T(lambda e: e.transpose(PS[0:64, 3072:3088], c16[0:16, 0:64],
                                  csb["idnf32"][0:16, 0:16]), w=[(sS, b1_)])
    b3 = V(lambda e: e.tensor_copy(out=callT, in_=PS[0:64, 3072:3088]), w=[(sT, b2_)])
    b4 = T(lambda e: e.matmul(PS[:, 3088:3104], csb["rep2"], callT,
                              start=True, stop=True), w=[(sV, b3)])
    t_crep = V(lambda e: e.tensor_copy(out=crep, in_=PS[:, 3088:3104]), w=[(sT, b4)])
    c1_ = S(lambda e: e.activation(out=megac, in_=megae, func=AF.Exp, scale=64.0), w=wp)
    c2_ = V(lambda e: e.tensor_scalar_min(out=megac, in0=megac, scalar1=2.0),
            w=[(sS, c1_)])
    t_lnmega = S(lambda e: e.activation(out=lnmega, in_=megac, func=AF.Ln),
                 w=[(sV, c2_)])
    d1 = S(lambda e: e.activation(out=twp, in_=twe, func=AF.Exp, scale=-1.0), w=wp)
    d2 = V(lambda e: e.tensor_scalar_add(out=twp, in0=twp, scalar1=1.0), w=[(sS, d1)])
    d3 = V(lambda e: e.reciprocal(out=tw64, in_=twp), w=[(sV, d2)])
    d4 = V(lambda e: e.tensor_copy(out=twrow[0:1, 0:64], in_=tw64), w=[(sV, d3)])
    t_twrow = V(lambda e: e.tensor_copy(out=twrow[0:1, 64:128], in_=tw64),
                w=[(sV, d3)])
    d6 = V(lambda e: e.tensor_scalar(out=ow64, in0=tw64, scalar1=-1.0, scalar2=1.0,
                                     op0=ALU.mult, op1=ALU.add), w=[(sV, d3)])
    d7 = V(lambda e: e.tensor_copy(out=owrow[0:1, 0:64], in_=ow64), w=[(sV, d6)])
    d8 = V(lambda e: e.tensor_copy(out=owrow[0:1, 64:128], in_=ow64), w=[(sV, d6)])
    d9 = T(lambda e: e.matmul(PS[:, 3104:3232], csb["onesf128"], owrow,
                              start=True, stop=True), w=[(sV, d7), (sV, d8)])
    t_owrep = V(lambda e: e.tensor_copy(out=owrep2, in_=PS[:, 3104:3232]),
                w=[(sT, d9)])
    bfree(6, sV, t_owrep)

    t_xn = [0] * NT
    t_lnS = [0] * NT
    t_lnV = [0] * NT
    t_xtr = [0] * NT
    for tc in range(NT):
        par = tc % 2
        w_x = [(sS, t_lnS[tc - 2]), (sV, t_lnV[tc - 2])] if tc >= 2 else []
        tx = SY(lambda e, tc=tc, p=par: e.dma_start(
            out=xstage[p], in_=x[tc * 128:(tc + 1) * 128, :]), w=w_x)
        wx = [(dS, tx)]
        va = V(lambda e, p=par: e.reduce_sum(out=msum[:, p:p + 1], in_=xstage[p],
                                             axis=AX), w=wx)
        sa = S(lambda e, p=par: e.activation(out=sqdump, in_=xstage[p],
                                             func=AF.Square,
                                             accum_out=sq2[:, p:p + 1]), w=wx)
        vb = V(lambda e, p=par: e.tensor_scalar_mul(
            out=mmean[:, p:p + 1], in0=msum[:, p:p + 1], scalar1=-1.0 / D),
            w=[(sV, va)])
        vc = V(lambda e, p=par: e.tensor_scalar(
            out=tvar[:, p:p + 1], in0=mmean[:, p:p + 1], scalar1=mmean[:, p:p + 1],
            scalar2=None, op0=ALU.mult), w=[(sV, vb)])
        vd = V(lambda e, p=par: e.tensor_scalar_mul(
            out=varv[:, p:p + 1], in0=sq2[:, p:p + 1], scalar1=1.0 / D),
            w=[(sS, sa), (sV, vc)])
        ve = V(lambda e, p=par: e.tensor_tensor(
            out=varv[:, p:p + 1], in0=varv[:, p:p + 1], in1=tvar[:, p:p + 1],
            op=ALU.subtract), w=[(sV, vd)])
        vf = V(lambda e, p=par: e.tensor_scalar_add(
            out=tvar[:, p:p + 1], in0=varv[:, p:p + 1], scalar1=1e-5), w=[(sV, ve)])
        sb_ = S(lambda e, p=par: e.activation(out=rstd[:, p:p + 1],
                                              in_=tvar[:, p:p + 1], func=AF.Ln),
                w=[(sV, vf)])
        sc_ = S(lambda e, p=par: e.activation(out=rstd[:, p:p + 1],
                                              in_=rstd[:, p:p + 1], func=AF.Exp,
                                              scale=-0.5), w=[(sS, sb_)])
        vh = V(lambda e, p=par: e.tensor_scalar(
            out=recn[:, p:p + 1], in0=varv[:, p:p + 1], scalar1=rstd[:, p:p + 1],
            scalar2=rstd[:, p:p + 1], op0=ALU.mult, op1=ALU.mult), w=[(sS, sc_)])
        si = S(lambda e, p=par: e.activation(out=recn[:, p:p + 1],
                                             in_=recn[:, p:p + 1], func=AF.Ln),
               w=[(sV, vh)])
        sj = S(lambda e, p=par: e.activation(out=recn[:, p:p + 1],
                                             in_=recn[:, p:p + 1], func=AF.Exp,
                                             scale=-0.5), w=[(sS, si)])
        w_xn = [(sS, sc_)] + ([(sT, t_xtr[tc - 2])] if tc >= 2 else [])
        sk = V(lambda e, p=par: e.tensor_scalar(
            out=xn_st[p], in0=xstage[p], scalar1=mmean[:, p:p + 1],
            scalar2=rstd[:, p:p + 1], op0=ALU.add, op1=ALU.mult), w=w_xn)
        t_lnS[tc] = sa
        base = par * 1024
        wtr = [(sV, sk)] + (bwait(par * 2, par * 2 + 1) if tc >= 2 else [])
        ttr = None
        for ic in range(8):
            ttr = T(lambda e, p=par, ic=ic, b=base: e.transpose(
                PS[:, b + ic * 128: b + (ic + 1) * 128],
                xn_st[p][:, ic * 128:(ic + 1) * 128], csb["idnf32"]),
                w=wtr, inc=(ic == 7))
        t_xtr[tc] = ttr
        vk = V(lambda e, tc=tc, b=base: e.tensor_copy(
            out=xn_fm.rearrange("p (ic t) -> p ic t", ic=8)[:, :, tc * 128:(tc + 1) * 128],
            in_=PS[:, b:b + 1024].rearrange("p (ic j) -> p ic j", ic=8)),
            w=[(sT, ttr)])
        t_xn[tc] = vk
        bfree(par * 2, sV, vk)
        bfree(par * 2 + 1, sV, vk)
        rb = 2048 + par * 512
        wr = [(sV, vk), (sV, a7)] + (bwait(4 + par) if tc >= 2 else [])
        trm = None
        for ic in range(8):
            trm = T(lambda e, ic=ic, tc=tc, rb=rb: e.matmul(
                PS[:, rb:rb + 16],
                xn_fm[:, ic * 2048 + tc * 128: ic * 2048 + (tc + 1) * 128],
                cnrb[:, ic * 16:(ic + 1) * 16],
                start=(ic == 0), stop=(ic == 7), skip_group_check=True),
                w=wr, inc=(ic == 7))
        r1 = V(lambda e, p=par, rb=rb: e.tensor_copy(out=lgec[p], in_=PS[:, rb:rb + 16]),
               w=[(sT, trm)])
        bfree(4 + par, sV, r1)
        r2 = V(lambda e, p=par: e.scalar_tensor_tensor(
            out=lgf[p], in0=lgec[p][:, 0:8], scalar=recn[:, p:p + 1],
            in1=lgec[p][:, 8:16], op0=ALU.mult, op1=ALU.add),
            w=[(sV, r1), (sS, sj)])
        r3 = V(lambda e, p=par: e.reduce_max(out=m1c[:, p:p + 1], in_=lgf[p], axis=AX),
               w=[(sV, r2)])
        r4 = V(lambda e, p=par: e.tensor_scalar(
            out=ge1[p], in0=lgf[p], scalar1=m1c[:, p:p + 1], scalar2=None,
            op0=ALU.is_ge), w=[(sV, r3)])
        r5 = V(lambda e, p=par: e.scalar_tensor_tensor(
            out=mskd[p], in0=ge1[p], scalar=-1e30, in1=lgf[p],
            op0=ALU.mult, op1=ALU.add), w=[(sV, r4)])
        r6 = V(lambda e, p=par: e.reduce_max(out=m2c[:, p:p + 1], in_=mskd[p], axis=AX),
               w=[(sV, r5)])
        r7 = V(lambda e, p=par: e.tensor_scalar(
            out=kept[p], in0=lgf[p], scalar1=m2c[:, p:p + 1], scalar2=None,
            op0=ALU.is_ge), w=[(sV, r6)])
        r8 = V(lambda e, p=par: e.tensor_scalar_mul(
            out=nm1[:, p:p + 1], in0=m1c[:, p:p + 1], scalar1=-1.0), w=[(sV, r3)])
        r9 = S(lambda e, p=par: e.activation(out=egk[p], in_=lgf[p], func=AF.Exp,
                                             bias=nm1[:, p:p + 1]), w=[(sV, r8)])
        r10 = V(lambda e, p=par: e.tensor_tensor(out=egk[p], in0=egk[p], in1=kept[p],
                                                 op=ALU.mult), w=[(sS, r9), (sV, r7)])
        r11 = V(lambda e, p=par: e.reduce_sum(out=gsum[:, p:p + 1], in_=egk[p], axis=AX),
                w=[(sV, r10)])
        r12 = V(lambda e, p=par: e.reciprocal(out=grec[:, p:p + 1],
                                              in_=gsum[:, p:p + 1]), w=[(sV, r11)])
        r13 = V(lambda e, p=par: e.tensor_scalar_mul(
            out=gtm[p], in0=egk[p], scalar1=grec[:, p:p + 1]), w=[(sV, r12)])
        gb = 3072 + par * 128
        r14 = T(lambda e, p=par, gb=gb: e.transpose(
            PS[0:8, gb:gb + 128], gtm[p], csb["idnf32"]), w=[(sV, r13)] + bwait(6))
        r15 = V(lambda e, tc=tc, gb=gb: e.tensor_copy(
            out=g_t8[0:8, tc * 128:(tc + 1) * 128], in_=PS[0:8, gb:gb + 128]),
            w=[(sT, r14)])
        t_lnV[tc] = r15
        bfree(6, sV, r15)
    t_ln_done = t_lnV[NT - 1]
    t_xn_all = t_xn[NT - 1]
    if stop_stage <= 1:
        return _finish_early(nc, P, [(sT, sT.n), (sV, sV.n), (sS, sS.n),
                                     (dS, dS.n), (dG, dG.n)])

    t_wT = [0, 0, 0]
    for m in range(2):
        tk = None
        for ic in range(8):
            tk = SY(lambda e, m=m, ic=ic: e.dma_start(
                out=wTb[m][:, ic * 1024:(ic + 1) * 1024],
                in_=wbf[m][:, ic * 128:(ic + 1) * 128], transpose=True),
                w=[(dG, tw_wbf[m])])
        t_wT[m] = tk

    def mm_proj(dst_fm, wsrc, t_wready):
        tick_last = None
        for oc in range(8):
            for tg in range(4):
                bk = (oc * 4 + tg) % 2
                w0 = [(dS, t_wready), (sV, t_xn_all)] + bwait(bk)
                tm = None
                for ic in range(8):
                    tm = T(lambda e, oc=oc, tg=tg, ic=ic, bk=bk, ws=wsrc: e.matmul(
                        PS[:, bk * 512:(bk + 1) * 512],
                        ws[:, ic * 1024 + oc * 128: ic * 1024 + (oc + 1) * 128],
                        xn_fm[:, ic * 2048 + tg * 512: ic * 2048 + (tg + 1) * 512],
                        start=(ic == 0), stop=(ic == 7), skip_group_check=True),
                        w=w0, inc=(ic == 7))
                tv_ = V(lambda e, oc=oc, tg=tg, bk=bk, d=dst_fm: e.tensor_copy(
                    out=d[:, oc * 2048 + tg * 512: oc * 2048 + (tg + 1) * 512],
                    in_=PS[:, bk * 512:(bk + 1) * 512]), w=[(sT, tm)])
                bfree(bk, sV, tv_)
                tick_last = tm
        return tick_last

    t_qmm = mm_proj(q_fm, wTb[0], t_wT[0])
    mm_proj(k_fm, wTb[1], t_wT[1])
    tk = None
    for ic in range(8):
        tk = SY(lambda e, ic=ic: e.dma_start(
            out=wTb[0][:, ic * 1024:(ic + 1) * 1024],
            in_=wbf[2][:, ic * 128:(ic + 1) * 128], transpose=True),
            w=[(dG, tw_wbf[2]), (sT, t_qmm)])
    t_wT[2] = tk
    for tc in range(NT):
        for og in range(2):
            bk = (tc * 2 + og) % 2
            w0 = [(dS, t_wT[2])] + bwait(bk)
            tm = None
            for ic in range(8):
                tm = T(lambda e, tc=tc, og=og, ic=ic, bk=bk: e.matmul(
                    PS[:, bk * 512:(bk + 1) * 512],
                    xn_fm[:, ic * 2048 + tc * 128: ic * 2048 + (tc + 1) * 128],
                    wTb[0][:, ic * 1024 + og * 512: ic * 1024 + (og + 1) * 512],
                    start=(ic == 0), stop=(ic == 7), skip_group_check=True),
                    w=w0, inc=(ic == 7))
            tv_ = V(lambda e, tc=tc, og=og, bk=bk: e.tensor_copy(
                out=v_tm[:, tc * 1024 + og * 512: tc * 1024 + (og + 1) * 512],
                in_=PS[:, bk * 512:(bk + 1) * 512]), w=[(sT, tm)])
            bfree(bk, sV, tv_)
    t_qkv_T = sT.n
    if stop_stage <= 2:
        return _finish_early(nc, P, [(sT, sT.n), (sV, sV.n), (sS, sS.n),
                                     (dS, dS.n), (dG, dG.n)])

    war_pt = [[], []]
    war_cvec = []
    war_pcs = [[], []]
    war_wTh = [[], []]
    war_band1 = [[], []]
    war_band2 = [[], []]
    war_zt = [[], []]
    war_rz = [[], []]
    war_bandacc = [[], []]
    t_comb_prev = 0

    tbz0 = V(lambda e: e.memset(band1[0], 0.0), w=[(sT, t_qkv_T)])
    tbz1 = V(lambda e: e.memset(band1[1], 0.0), w=[(sT, t_qkv_T)])
    tbz2 = V(lambda e: e.memset(band1[2], 0.0), w=[(sT, t_qkv_T)])
    war_band1 = [[(sV, tbz0)], [(sV, tbz1)], [(sV, tbz2)]]

    for h in range(H):
        hp = h % 2
        hb = hp * 64
        icb = (h // 2) * 2048
        tp = (0, 64) if hp else None
        qsl_all = q_fm[hb:hb + 64, icb:icb + 2048]
        ksl_all = k_fm[hb:hb + 64, icb:icb + 2048]

        import os as _os2
        _kmin = _os2.environ.get("KMIN", "")
        if _kmin:
            t_wTr = V(lambda e, hp=hp: e.memset(wTh[hp], 0.001),
                      w=[(sT, t_qkv_T)])
            t_pcs = [sT.n if sT.n else 1] * NT
            t_pcsev = t_wTr
            twt = sT.n if sT.n else 1
        _kmin_noA = _kmin in ("1", "3")
        _kmin_noScan = _kmin in ("1", "2")
        Tdrain()
        if _kmin_noA if _kmin else False:
            tcv = None
        else:
            V(lambda e: e.memset(cvecP[0], 0.0), w=war_cvec, inc=False)
            tcv = V(lambda e: e.memset(cvecP[1], 0.0), w=war_cvec)
        t_pcs = [0] * NT
        t_exp = {}
        t_cv = [0] * NT

        def emit_pcs(i):
            W = 128 * (i + 1)
            ngrp = (W + 511) // 512
            Tdrain()
            tpc = None
            for Gi in range(ngrp):
                n0, n1 = 512 * Gi, min(W, 512 * Gi + 512)
                first = (i == 4 * Gi)
                wpc = [(sV, t_cv[i]), (sS, t_exp[(i, Gi)])]
                if first and n1 < 512 * (Gi + 1):
                    tz = V(lambda e, p=i % 2, n1=n1, Gi=Gi: e.memset(
                        Ptm[p][:, n1:512 * (Gi + 1)], 0.0),
                        w=[(sT, t_pcs[i - 2])] if i >= 2 else [])
                    wpc.append((sV, tz))
                    n1 = 512 * (Gi + 1)
                if first:
                    wpc += bwait(4 + Gi) + war_pcs[hp]
                tpc = T(lambda e, n0=n0, n1=n1, first=first, i=i: e.matmul(
                    PS[0:32, 2048 + n0: 2048 + n1], cvecP[i % 2],
                    Ptm[i % 2][:, n0:n1],
                    start=first, stop=(i == NT - 1), skip_group_check=True),
                    w=wpc, inc=(Gi == ngrp - 1))
            t_pcs[i] = tpc

        for i in ([] if (_kmin and _kmin_noA) else range(NT)):
            W = 128 * (i + 1)
            par = i % 2
            ngrp = (W + 511) // 512
            qsl = qsl_all[:, i * 128:(i + 1) * 128]
            tsm_l = {}
            for Gi in range(ngrp):
                n0, n1 = 512 * Gi, min(W, 512 * Gi + 512)
                w0 = bwait(Gi)
                if h == 0 and i == 0 and Gi == 0:
                    w0 = w0 + [(sT, t_qkv_T)]
                lastg = (Gi == ngrp - 1)
                if not lastg:
                    tsm = T(lambda e, n0=n0, n1=n1, q=qsl, ks=ksl_all: e.matmul(
                        PS[:, n0:n1], q, ks[:, n0:n1],
                        start=True, stop=True, skip_group_check=True), w=w0)
                else:
                    T(lambda e, n0=n0, n1=n1, q=qsl, ks=ksl_all: e.matmul(
                        PS[:, n0:n1], q, ks[:, n0:n1],
                        start=True, stop=False, skip_group_check=True),
                        w=w0, inc=False)
                    tsm = T(lambda e, i=i: e.matmul(
                        PS[:, i * 128:(i + 1) * 128], csb["strictu"], csb["negi"],
                        start=False, stop=True, skip_group_check=True))
                tsm_l[Gi] = tsm
            if i >= 1:
                emit_pcs(i - 1)
            for Gi in range(ngrp):
                n0, n1 = 512 * Gi, min(W, 512 * Gi + 512)
                wx = [(sT, tsm_l[Gi])]
                if i >= 2:
                    wx.append((sT, t_pcs[i - 2]))
                wx += war_pt[par]
                te = S(lambda e, p=par, n0=n0, n1=n1, g=Gi: e.activation(
                    out=Ptm[p][:, n0:n1], in_=PS[:, n0:n1], func=AF.Exp,
                    scale=0.125, accum_out=zparts[:, p * 4 + g: p * 4 + g + 1]),
                    w=wx)
                t_exp[(i, Gi)] = te
                bfree(Gi, sS, te)
            war_pt[par] = []
            z1 = V(lambda e, p=par, ng=ngrp: e.reduce_sum(
                out=Zc[:, p:p + 1], in_=zparts[:, p * 4: p * 4 + ng], axis=AX),
                w=[(sS, t_exp[(i, ngrp - 1)])])
            z2 = V(lambda e, p=par: e.reciprocal(out=rZc[:, p:p + 1],
                                                 in_=Zc[:, p:p + 1]), w=[(sV, z1)])
            wc = [(sV, z2), (sV, tcv), (sV, t_crep)]
            if i >= 2:
                V(lambda e, i=i, p=par: e.memset(
                    cvecP[p][:, 2 * i - 4: 2 * i - 2], 0.0),
                    w=[(sT, t_pcs[i - 2])], inc=False)
            cv1 = V(lambda e, i=i, h=h, p=par: e.tensor_tensor(
                out=cvecP[p][0:64, 2 * i: 2 * i + 1], in0=crep[0:64, h:h + 1],
                in1=rZc[0:64, p:p + 1], op=ALU.mult), w=wc)
            cv2 = V(lambda e, i=i, h=h, p=par: e.tensor_tensor(
                out=cvecP[p][64:128, 2 * i + 1: 2 * i + 2], in0=crep[64:128, h:h + 1],
                in1=rZc[64:128, p:p + 1], op=ALU.mult), w=wc)
            t_cv[i] = cv2
        if not (_kmin and _kmin_noA):
            emit_pcs(NT - 1)
        if _kmin and _kmin_noA:
            pass
        else:
            war_cvec = [(sT, t_pcs[NT - 1])]
        if (not _kmin) or not _kmin_noScan:
            if stop_stage == 21 and h == 0:
                return _finish_early(nc, P, [(sT, sT.n), (sV, sV.n), (sS, sS.n),
                                             (dS, dS.n), (dG, dG.n)])
            tevp = None
            for g in range(4):
                tevp = V(lambda e, g=g, hp=hp: e.tensor_copy(
                    out=pcsS[hp][0:32, 512 * g: 512 * (g + 1)],
                    in_=PS[0:32, 2048 + 512 * g: 2048 + 512 * (g + 1)]),
                    w=[(sT, t_pcs[NT - 1])] + war_pcs[hp])
                bfree(4 + g, sV, tevp)
            t_pcsev = tevp
            war_pcs[hp] = []

            s1_ = T(lambda e, h=h: e.matmul(PS[0:32, 3584:3585], csb["ones32f"],
                                            lnmega[0:1, h:h + 1], start=True, stop=True,
                                            skip_group_check=True),
                    w=[(sS, t_lnmega)] + bwait(7))
            s2_ = V(lambda e: e.tensor_copy(out=lnm32, in_=PS[0:32, 3584:3585]),
                    w=[(sT, s1_)])
            s3_ = V(lambda e: e.tensor_scalar_mul(out=EA, in0=csb["emat"], scalar1=lnm32),
                    w=[(sV, s2_)])
            s4_ = S(lambda e: e.activation(out=EA, in_=EA, func=AF.Exp), w=[(sV, s3_)])
            s5_ = V(lambda e: e.tensor_tensor(out=ATf, in0=EA, in1=csb["ltri"],
                                              op=ALU.mult), w=[(sS, s4_)])
            s6_ = V(lambda e: e.tensor_copy(out=ATb, in_=ATf), w=[(sV, s5_)])
            s7_ = V(lambda e, hp=hp: e.reduce_sum(out=Pcol, in_=pcsS[hp][0:32, :], axis=AX),
                    w=[(sV, t_pcsev)])
            s8_ = V(lambda e: e.tensor_copy(out=Pcolb[0:32, 0:1], in_=Pcol), w=[(sV, s7_)])
            s9_ = T(lambda e: e.matmul(PS[0:32, 3585:3586], ATb, Pcolb[0:32, 0:1],
                                       start=True, stop=True, skip_group_check=True),
                    w=[(sV, s6_), (sV, s8_)])
            s10_ = V(lambda e: e.tensor_scalar_add(out=rn0, in0=PS[0:32, 3585:3586],
                                                   scalar1=1e-5), w=[(sT, s9_)])
            s11_ = V(lambda e: e.reciprocal(out=rnorm, in_=rn0), w=[(sV, s10_)])
            s12_ = T(lambda e: e.transpose(PS[0:1, 3586:3618], rnorm,
                                           csb["idnf32"][0:32, 0:32]), w=[(sV, s11_)])
            s13_ = V(lambda e: e.tensor_copy(out=rnrow, in_=PS[0:1, 3586:3618]),
                     w=[(sT, s12_)])
            s14_ = T(lambda e: e.matmul(PS[0:32, 3618:3650], csb["ones32f"], rnrow,
                                        start=True, stop=True, skip_group_check=True),
                     w=[(sV, s13_)])
            s15_ = V(lambda e: e.tensor_copy(out=rnb32, in_=PS[0:32, 3618:3650]),
                     w=[(sT, s14_)])
            s16_ = V(lambda e: e.tensor_tensor(out=A2b, in0=ATf, in1=rnb32, op=ALU.mult),
                     w=[(sV, s15_)])
            bfree(7, sV, s15_)
            twt = None
            for c in range(NCH):
                twt = T(lambda e, c=c, hp=hp: e.matmul(
                    PS[:, 3072 + 32 * c: 3072 + 32 * (c + 1)],
                    pcsS[hp][0:32, 128 * c: 128 * (c + 1)], A2b,
                    start=True, stop=True, skip_group_check=True),
                    w=[(sV, s16_)] + (bwait(6) if c == 0 else []), inc=(c == NCH - 1))
            war_pcs[hp] = [(sT, twt)]
            t_wTr = V(lambda e, hp=hp: e.tensor_copy(out=wTh[hp], in_=PS[:, 3072:3584]),
                      w=[(sT, twt)] + war_wTh[hp])
            if stop_stage == 22 and h == 0:
                return _finish_early(nc, P, [(sT, sT.n), (sV, sV.n), (sS, sS.n),
                                             (dS, dS.n), (dG, dG.n)])

        Tdrain()
        sfree = {0: list(bank_free[0]), 1: list(bank_free[1])}
        t_stm = {}
        t_wg = {}
        t_pv = [0] * NCH
        t_bmm = [0] * NCH
        t_b2m = [0] * NCH
        t_zt = [0] * NCH
        t_ztread = [0] * NCH
        t_rzev = [0] * NCH
        t_cmbm = [0] * NCH
        t_cmb = [0] * NCH
        t_b1w = [0] * NCH
        def emit_fin(c):
            bac = 3072 + (c % 2) * 128
            wbm = [(sV, t_b2m[c])]
            if c > 1:
                wbm += [(sV, t_cmbm[c - 2])]
            else:
                wbm += war_bandacc[c % 2] + bwait(6)
            if c > 0:
                wbm += [(sS, t_b1w[c])]
                vsl1 = v_tm[:, (c - 1) * 1024 + h * 64: (c - 1) * 1024 + (h + 1) * 64]
                T(lambda e, c=c, vs=vsl1, hb=hb, tp=tp, bac=bac: e.matmul(
                    PS[hb:hb + 64, bac:bac + 128], vs, band1[c % 3],
                    start=True, stop=False, skip_group_check=True,
                    tile_position=tp), w=wbm, inc=False)
            vsl = v_tm[:, c * 1024 + h * 64: c * 1024 + (h + 1) * 64]
            t_bmm[c] = T(lambda e, c=c, vs=vsl, hb=hb, tp=tp, bac=bac: e.matmul(
                PS[hb:hb + 64, bac:bac + 128], vs, band2P[c % 2],
                start=(c == 0), stop=True, skip_group_check=True,
                tile_position=tp), w=wbm)
            if stop_stage == 53 and h == 0 and c == 0:
                return _finish_early(nc, P, [(sT, sT.n), (sV, sV.n), (sS, sS.n),
                                             (dS, dS.n), (dG, dG.n)])
            Tdrain()
            ztc = 3712 + (c % 2) * 128
            wzt = [(sV, t_b2m[c])]
            if c > 1:
                wzt += [(sV, t_ztread[c - 2])]
            else:
                wzt += war_zt[c % 2]
            if c > 0:
                wzt += [(sS, t_b1w[c])]
                T(lambda e, c=c, ztc=ztc: e.matmul(
                    PS[0:1, ztc:ztc + 128], csb["onescol"], band1[c % 3],
                    start=True, stop=False, skip_group_check=True), w=wzt, inc=False)
            t_zt[c] = T(lambda e, c=c, ztc=ztc: e.matmul(
                PS[0:1, ztc:ztc + 128], csb["onescol"], band2P[c % 2],
                start=(c == 0), stop=True, skip_group_check=True), w=wzt)
            if stop_stage == 54 and h == 0 and c == 0:
                return _finish_early(nc, P, [(sT, sT.n), (sV, sV.n), (sS, sS.n),
                                             (dS, dS.n), (dG, dG.n)])
            pr2 = c % 2
            u1 = V(lambda e, pr2=pr2, ztc=ztc: e.tensor_copy(
                out=ztrowP[pr2], in_=PS[0:1, ztc:ztc + 128]), w=[(sT, t_zt[c])])
            t_ztread[c] = u1
            u2 = V(lambda e, pr2=pr2: e.reciprocal(out=ztr2P[pr2],
                                                   in_=ztrowP[pr2]), w=[(sV, u1)])
            u3 = V(lambda e, pr2=pr2: e.tensor_tensor(
                out=rztwbP[pr2], in0=ztr2P[pr2], in1=twrow, op=ALU.mult),
                w=[(sV, u2), (sV, t_twrow)])
            rzc = 3328 + pr2 * 128
            wrz = [(sV, u3)]
            if c > 1:
                wrz.append((sV, t_rzev[c - 2]))
            else:
                wrz += war_rz[pr2]
            Tdrain()
            trz = T(lambda e, hb=hb, tp=tp, pr2=pr2, rzc=rzc: e.matmul(
                PS[hb:hb + 64, rzc:rzc + 128], csb["ones64"], rztwbP[pr2],
                start=True, stop=True, skip_group_check=True, tile_position=tp),
                w=wrz)
            if stop_stage == 55 and h == 0 and c == 0:
                return _finish_early(nc, P, [(sT, sT.n), (sV, sV.n), (sS, sS.n),
                                             (dS, dS.n), (dG, dG.n)])
            u4 = V(lambda e, hb=hb, pr2=pr2, rzc=rzc: e.tensor_copy(
                out=rztw_sbP[pr2][hb:hb + 64, :],
                in_=PS[hb:hb + 64, rzc:rzc + 128]), w=[(sT, trz)])
            t_rzev[c] = u4
            u5 = V(lambda e, hb=hb, pr2=pr2, bac=bac: e.tensor_tensor(
                out=tcombP[pr2][hb:hb + 64, :], in0=PS[hb:hb + 64, bac:bac + 128],
                in1=rztw_sbP[pr2][hb:hb + 64, :], op=ALU.mult),
                w=[(sV, u4), (sT, t_bmm[c])])
            t_cmbm[c] = u5
            qc0 = 1024 + 512 * (c // 4) + (c % 4) * 128
            u6 = V(lambda e, qc0=qc0, hb=hb, pr2=pr2: e.tensor_tensor(
                out=t2combP[pr2][hb:hb + 64, :], in0=PS[hb:hb + 64, qc0:qc0 + 128],
                in1=owrep2[hb:hb + 64, :], op=ALU.mult),
                w=[(sT, t_pv[c]), (sV, t_owrep)])
            u7 = V(lambda e, c=c, hb=hb, icb=icb, pr2=pr2: e.tensor_tensor(
                out=y_fm[hb:hb + 64, icb + c * 128: icb + (c + 1) * 128],
                in0=tcombP[pr2][hb:hb + 64, :], in1=t2combP[pr2][hb:hb + 64, :],
                op=ALU.add), w=[(sV, u5), (sV, u6)])
            t_cmb[c] = u7

        for c in range(NCH):
            Gmin = c // 4
            Tdrain()
            ksl = ksl_all[:, c * 128:(c + 1) * 128]
            G2b = Gmin
            G1b = (128 * c + 128) // 512 if c < NCH - 1 else -1
            off0 = 128 * c - 512 * G2b
            sb0 = G2b % 2
            tb2 = None
            tb1 = None
            for Gi in range(Gmin, 4):
                n0, n1 = max(128 * c, 512 * Gi), 512 * (Gi + 1)
                sb = Gi % 2
                off = n0 - 512 * Gi
                w0 = list(sfree[sb])
                sfree[sb] = []
                tsm = T(lambda e, n0=n0, n1=n1, sb=sb, off=off, k=ksl,
                        qs=qsl_all: e.matmul(
                    PS[:, sb * 512 + off: sb * 512 + off + (n1 - n0)],
                    k, qs[:, n0:n1], start=True, stop=True, skip_group_check=True),
                    w=w0)
                t_stm[(c, Gi)] = tsm
                wact = []
                if Gi == G2b:
                    wb2 = [(sT, tsm)]
                    if c > 1:
                        wb2 += [(sT, t_bmm[c - 2]), (sT, t_zt[c - 2])]
                    else:
                        wb2 += war_band2[c % 2]
                    tb2 = S(lambda e, sb0=sb0, off0=off0, c=c: e.activation(
                        out=band2P[c % 2],
                        in_=PS[:, sb0 * 512 + off0: sb0 * 512 + off0 + 128],
                        func=AF.Exp, scale=0.125), w=wb2)
                    sfree[sb0].append((sS, tb2))
                    wact.append((sS, tb2))
                if Gi == G1b:
                    colB1 = 128 * c + 128
                    off1 = colB1 - 512 * G1b
                    sb1 = G1b % 2
                    wb1 = [(sT, tsm)]
                    if c > 1:
                        wb1 += [(sT, t_bmm[c - 2]), (sT, t_zt[c - 2])]
                    else:
                        wb1 += war_band1[(c + 1) % 3]
                    tb1 = S(lambda e, sb1=sb1, off1=off1, c=c: e.activation(
                        out=band1[(c + 1) % 3][64:128, 0:64],
                        in_=PS[64:128, sb1 * 512 + off1: sb1 * 512 + off1 + 64],
                        func=AF.Exp, scale=0.125), w=wb1)
                    t_b1w[c + 1] = tb1
                    sfree[sb1].append((sS, tb1))
                    wact.append((sS, tb1))
                nr = (n1 - n0) // 64
                r0 = n0 // 64
                wwg = [(sT, tsm), (sV, t_wTr)] + wact
                if c >= 2:
                    wwg.append((sT, t_pv[c - 2]))
                wwg += war_pt[c % 2]
                twg = V(lambda e, n0=n0, n1=n1, sb=sb, off=off, nr=nr, r0=r0,
                        c=c, hp=hp: e.tensor_tensor(
                    out=stripB[c % 2][:, n0:n1].rearrange("p (r z) -> p r z", z=64),
                    in0=PS[:, sb * 512 + off: sb * 512 + off + (n1 - n0)]
                        .rearrange("p (r z) -> p r z", z=64),
                    in1=wTh[hp][:, 32 * c + r0: 32 * c + r0 + nr]
                        .unsqueeze(2).broadcast_to([128, nr, 64]),
                    op=ALU.mult), w=wwg)
                t_wg[(c, Gi)] = twg
                sfree[sb].append((sV, twg))
            war_pt[c % 2] = []
            t_b2m[c] = V(lambda e, c=c: e.tensor_tensor(out=band2P[c % 2],
                                               in0=band2P[c % 2],
                                               in1=csb["tri01"],
                                               op=ALU.mult), w=[(sS, tb2)])
            if stop_stage == 51 and h == 0 and c == 0:
                return _finish_early(nc, P, [(sT, sT.n), (sV, sV.n), (sS, sS.n),
                                             (dS, dS.n), (dG, dG.n)])
            tpv = None
            for Qd in range(Gmin, 4):
                n0q, n1q = max(512 * Qd, 128 * c), 512 * (Qd + 1)
                qoff = n0q - 512 * Qd
                wpv = [(sV, t_wg[(c, Qd)])]
                if c == 0:
                    wpv += bwait(2 + Qd)
                    if t_comb_prev:
                        wpv.append((sV, t_comb_prev))
                vsl = v_tm[:, c * 1024 + h * 64: c * 1024 + (h + 1) * 64]
                tpv = T(lambda e, Qd=Qd, n0q=n0q, n1q=n1q, qoff=qoff, c=c,
                        vs=vsl, hb=hb, tp=tp: e.matmul(
                    PS[hb:hb + 64,
                       1024 + 512 * Qd + qoff: 1024 + 512 * Qd + qoff + (n1q - n0q)],
                    vs, stripB[c % 2][:, n0q:n1q],
                    start=(c == 0), stop=(c == 4 * Qd + 3), skip_group_check=True,
                    tile_position=tp), w=wpv, inc=(Qd == 3))
            t_pv[c] = tpv
            if stop_stage == 52 and h == 0 and c == 0:
                return _finish_early(nc, P, [(sT, sT.n), (sV, sV.n), (sS, sS.n),
                                             (dS, dS.n), (dG, dG.n)])
            emit_fin(c)

            if stop_stage == 23 + c and h == 0:
                return _finish_early(nc, P, [(sT, sT.n), (sV, sV.n), (sS, sS.n),
                                             (dS, dS.n), (dG, dG.n)])
        t_comb_prev = t_cmb[NCH - 1]
        war_pt[0] = [(sT, t_pv[NCH - 2])]
        war_pt[1] = [(sT, t_pv[NCH - 1])]
        war_wTh[hp] = [(sV, t_wg[(NCH - 1, 3)])]
        war_band1 = [[(sT, t_bmm[15]), (sT, t_zt[15])],
                     [(sT, t_bmm[13]), (sT, t_zt[13])],
                     [(sT, t_bmm[14]), (sT, t_zt[14])]]
        war_band2 = [[(sT, t_bmm[NCH - 2]), (sT, t_zt[NCH - 2])],
                     [(sT, t_bmm[NCH - 1]), (sT, t_zt[NCH - 1])]]
        war_zt = [[(sV, t_ztread[NCH - 2])], [(sV, t_ztread[NCH - 1])]]
        war_rz = [[(sV, t_rzev[NCH - 2])], [(sV, t_rzev[NCH - 1])]]
        war_bandacc = [[(sV, t_cmbm[NCH - 2])], [(sV, t_cmbm[NCH - 1])]]
        bank_free[6] = [(sV, t_cmbm[NCH - 1])]
        bank_free[7] = [(sV, t_ztread[NCH - 1])]
        for qd in range(4):
            bank_free[2 + qd] = [(sV, t_cmb[NCH - 1])]
        for sb in (0, 1):
            bank_free[sb] = list(sfree[sb])
        if stop_stage == 3 and h == 0:
            return _finish_early(nc, P, [(sT, sT.n), (sV, sV.n), (sS, sS.n),
                                         (dS, dS.n), (dG, dG.n)])

    t_attn_T = sT.n
    if stop_stage <= 4:
        return _finish_early(nc, P, [(sT, sT.n), (sV, sV.n), (sS, sS.n),
                                     (dS, dS.n), (dG, dG.n)])
    t_attn_V = sV.n
    t_attn_S = sS.n

    Tdrain()
    w_att = [(sT, t_attn_T), (sV, t_attn_V), (sS, t_attn_S)]
    twc = None
    for ic in range(8):
        twc = SY(lambda e, ic=ic: e.dma_start(
            out=wcT[:, ic * 1024:(ic + 1) * 1024],
            in_=wbf[3][:, ic * 128:(ic + 1) * 128], transpose=True),
            w=[(dG, tw_wbf[3])] + w_att)
    t_wcT = twc
    t_b1 = G(lambda e: e.dma_start(
        out=b1row, in_=b1d.rearrange("e h -> (e h)").unsqueeze(0)), w=w_att)
    t_b2 = G(lambda e: e.dma_start(out=b2_sb, in_=b2d), w=w_att)

    t_w1use = {}
    t_w2use = {}
    t_hm = {0: 0, 1: 0}
    t_eo_hl = {}
    t_evac = {}
    t_odma = {}
    for s_ in range(4):
        for e_ in range(NE):
            eb = e_ % 2
            m = s_ * NE + e_
            wld = [(dG, t_w1bf[e_])] + w_att
            if m >= 2:
                wld.append((sT, t_w1use[m - 2]))
            tld = SY(lambda e, i=e_, eb=eb: e.dma_start(out=w1b[eb], in_=w1bf[i]),
                     w=wld)
            gcb = 2 + eb
            tgbc = T(lambda e, i=e_, s=s_, gcb=gcb: e.matmul(
                PS[:, gcb * 512:(gcb + 1) * 512], csb["e8"][:, i * 128:(i + 1) * 128],
                g_t8[0:8, s * 512:(s + 1) * 512], start=True, stop=True,
                skip_group_check=True), w=[(sV, t_ln_done)] + bwait(gcb))
            tlast = None
            thm = None
            for hc in range(4):
                bk = (e_ * 4 + hc) % 2
                w0 = [(dS, tld)] + bwait(bk)
                for icc in range(8):
                    T(lambda e, icc=icc, hc=hc, eb=eb, s=s_, bk=bk: e.matmul(
                        PS[:, bk * 512:(bk + 1) * 512],
                        w1b[eb][:, icc * 512 + hc * 128: icc * 512 + (hc + 1) * 128],
                        xn_fm[:, icc * 2048 + s * 512: icc * 2048 + (s + 1) * 512],
                        start=(icc == 0), stop=False, skip_group_check=True),
                        w=w0, inc=False)
                tb1m = T(lambda e, i=e_, hc=hc, bk=bk: e.matmul(
                    PS[:, bk * 512:(bk + 1) * 512],
                    b1row[0:1, i * 512 + hc * 128: i * 512 + (hc + 1) * 128],
                    csb["onesrow"], start=False, stop=True, skip_group_check=True),
                    w=[(dG, t_b1)])
                tlast = tb1m
                wgl = [(sT, tb1m)]
                if t_hm[bk]:
                    wgl.append((sV, t_hm[bk]))
                tgl = S(lambda e, bk=bk: e.activation(
                    out=hgel[bk], in_=PS[:, bk * 512:(bk + 1) * 512], func=AF.Gelu),
                    w=wgl)
                bfree(bk, sS, tgl)
                whm = [(sS, tgl), (sT, tgbc)]
                if s_ > 0 and (s_ * NE + e_) in ():
                    pass
                if s_ > 0:
                    whm.append((sT, t_eo_hl[s_ - 1]))
                thm = V(lambda e, i=e_, hc=hc, bk=bk, gcb=gcb: e.tensor_tensor(
                    out=hprime[:, (i * 4 + hc) * 512:(i * 4 + hc + 1) * 512],
                    in0=hgel[bk], in1=PS[:, gcb * 512:(gcb + 1) * 512], op=ALU.mult),
                    w=whm)
                t_hm[bk] = thm
            bfree(gcb, sV, thm)
            t_w1use[m] = tlast
        treg = [0] * 8
        for e_ in range(NE):
            eb = e_ % 2
            m = s_ * NE + e_
            wld = [(dG, t_w2bf[e_])] + w_att
            if m >= 2:
                wld.append((sT, t_w2use[m - 2]))
            tld = SY(lambda e, i=e_, eb=eb: e.dma_start(out=w2b[eb], in_=w2bf[i]),
                     w=wld)
            tm = None
            for hc in range(4):
                for reg in range(8):
                    tc4, og = reg // 2, reg % 2
                    first = (e_ == 0 and hc == 0)
                    w0 = [(dS, tld)]
                    if first:
                        w0 += bwait(reg)
                    hsl = hprime[:, (e_ * 4 + hc) * 512 + tc4 * 128:
                                 (e_ * 4 + hc) * 512 + (tc4 + 1) * 128]
                    tm = T(lambda e, hs=hsl, eb=eb, hc=hc, og=og, reg=reg,
                           first=first: e.matmul(
                        PS[:, reg * 512:(reg + 1) * 512], hs,
                        w2b[eb][:, hc * 1024 + og * 512: hc * 1024 + (og + 1) * 512],
                        start=first, stop=False, skip_group_check=True),
                        w=w0, inc=(hc == 3 and reg == 7))
            t_w2use[m] = tm
        t_eo_hl[s_] = t_w2use[s_ * NE + NE - 1]
        for reg in range(8):
            tc4, og = reg // 2, reg % 2
            gtc = s_ * 4 + tc4
            T(lambda e, gtc=gtc, og=og, reg=reg: e.matmul(
                PS[:, reg * 512:(reg + 1) * 512],
                g_t8[0:8, gtc * 128:(gtc + 1) * 128],
                b2_sb[0:8, og * 512:(og + 1) * 512],
                start=False, stop=False, skip_group_check=True),
                w=[(dG, t_b2)], inc=False)
            for icc in range(8):
                treg[reg] = T(lambda e, icc=icc, gtc=gtc, og=og, reg=reg: e.matmul(
                    PS[:, reg * 512:(reg + 1) * 512],
                    y_fm[:, icc * 2048 + gtc * 128: icc * 2048 + (gtc + 1) * 128],
                    wcT[:, icc * 1024 + og * 512: icc * 1024 + (og + 1) * 512],
                    start=False, stop=(icc == 7), skip_group_check=True),
                    w=[(dS, t_wcT)], inc=(icc == 7))
        for tc4 in range(4):
            gtc = s_ * 4 + tc4
            pe = gtc % 2
            wxl = w_att + ([(sV, t_evac[gtc - 2])] if gtc >= 2 else [])
            txl = SY(lambda e, gtc=gtc, pe=pe: e.dma_start(
                out=xst2[pe], in_=x[gtc * 128:(gtc + 1) * 128, :]), w=wxl)
            tev = None
            for og in range(2):
                reg = tc4 * 2 + og
                wev = [(sT, treg[reg]), (dS, txl)]
                if gtc >= 2:
                    wev.append((sO, t_odma[gtc - 2]))
                tev = V(lambda e, pe=pe, og=og, reg=reg: e.tensor_tensor(
                    out=outst[pe][:, og * 512:(og + 1) * 512],
                    in0=PS[:, reg * 512:(reg + 1) * 512],
                    in1=xst2[pe][:, og * 512:(og + 1) * 512], op=ALU.add), w=wev)
                bfree(reg, sV, tev)
            t_evac[gtc] = tev
            t_odma[gtc] = SYo(lambda e, gtc=gtc, pe=pe: e.dma_start(
                out=out[gtc * 128:(gtc + 1) * 128, :], in_=outst[pe]),
                w=[(sV, tev)])

    if DEBUG:
        wd = [(sO, t_odma[NT - 1])]
        G(lambda e: e.dma_start(out=dbg["d_xn"], in_=xn_fm), w=wd)
        G(lambda e: e.dma_start(out=dbg["d_q"], in_=q_fm), w=wd)
        G(lambda e: e.dma_start(out=dbg["d_k"], in_=k_fm), w=wd)
        G(lambda e: e.dma_start(out=dbg["d_v"], in_=v_tm), w=wd)
        G(lambda e: e.dma_start(out=dbg["d_y"], in_=y_fm), w=wd)
        G(lambda e: e.dma_start(out=dbg["d_g"], in_=g_t8), w=wd)
        G(lambda e: e.dma_start(out=dbg["d_pcs"], in_=pcsS[1][0:32, :]), w=wd)
        G(lambda e: e.dma_start(out=dbg["d_wt"], in_=wTh[1]), w=wd)
        P.add("gpsimd", lambda e: e.nop(), w=[(dG, dG.n)])

    P.add("sync", lambda e: e.nop(), w=[(sO, 16 * NT)])

    with nc.Block() as block:
        @block.sync
        def _(e):
            P.replay("sync", e)

        @block.tensor
        def _(e):
            P.replay("tensor", e)

        @block.vector
        def _(e):
            P.replay("vector", e)

        @block.scalar
        def _(e):
            P.replay("scalar", e)

        @block.gpsimd
        def _(e):
            P.replay("gpsimd", e)

    return nc


_NC_CACHE = None


def kernel(**inputs):
    global _NC_CACHE
    xs = np.asarray(inputs["L0_attn_hidden_states"])
    base = {
        "Wq": np.ascontiguousarray(inputs["L0_attn_Wq"], np.float32),
        "Wk": np.ascontiguousarray(inputs["L0_attn_Wk"], np.float32),
        "Wv": np.ascontiguousarray(inputs["L0_attn_Wv"], np.float32),
        "Wc": np.ascontiguousarray(inputs["L0_attn_Wc"], np.float32),
        "terr": np.ascontiguousarray(inputs["L0_attn_terrace_mix"],
                                     np.float32).reshape(1, SEG),
        "idxd": np.ascontiguousarray(inputs["L0_attn_index_decays"], np.float32),
        "megad": np.ascontiguousarray(inputs["L0_attn_mega_decays"],
                                      np.float32).reshape(1, H),
        "w1": np.ascontiguousarray(inputs["L0_ffn_w1"], np.float32),
        "b1": np.ascontiguousarray(inputs["L0_ffn_b1"], np.float32),
        "w2": np.ascontiguousarray(inputs["L0_ffn_w2"], np.float32),
        "b2": np.ascontiguousarray(inputs["L0_ffn_b2"], np.float32),
        "cent": np.ascontiguousarray(inputs["L0_ffn_centroids"], np.float32),
        "wrt": np.ascontiguousarray(inputs["L0_ffn_w_route"], np.float32),
    }
    for k, v in _host_consts().items():
        base["c_" + k] = v
    if _NC_CACHE is None:
        _NC_CACHE = build()
    nc = _NC_CACHE
    in_maps = []
    for c in range(NCORE):
        m = dict(base)
        m["x"] = np.ascontiguousarray(xs[c], np.float32)
        in_maps.append(m)
    res = run_bass_kernel_spmd(nc, in_maps, core_ids=list(range(NCORE)))
    return np.stack([np.asarray(res.results[c]["out"]) for c in range(NCORE)],
                    axis=0)


# version 14
# speedup vs baseline: 1.5505x; 1.1446x over previous
import numpy as np
import ml_dtypes
import concourse.bass as bass
import concourse.mybir as mybir
from concourse.bass_utils import run_bass_kernel_spmd

F32 = mybir.dt.float32
BF16 = mybir.dt.bfloat16
U8 = mybir.dt.uint8
AF = mybir.ActivationFunctionType
ALU = mybir.AluOpType
AX = mybir.AxisListType.X

L, D, H, SEG = 2048, 1024, 16, 64
NT, NCH = 16, 16
NE = 8
NCORE = 8
LN32 = float(np.log(32.0))

DEBUG = False


def _host_consts():
    bf = ml_dtypes.bfloat16
    f = np.arange(128)
    jj, rr = np.meshgrid(np.arange(32), np.arange(32), indexing="ij")
    e8 = np.zeros((8, 1024), dtype=bf)
    for e in range(8):
        e8[e, e * 128:(e + 1) * 128] = 1
    return {
        "idnf32": np.eye(128, dtype=np.float32),
        "tri01": (f[None, :] >= f[:, None]).astype(bf),
        "strictu": (f[None, :] < f[:, None]).astype(bf),
        "negi": (-30000.0 * np.eye(128)).astype(bf),
        "rep2": (np.arange(128)[None, :] % 64 == np.arange(64)[:, None]).astype(np.float32),
        "emat": np.clip(rr - 1 - jj, 0, 31).astype(np.float32),
        "ltri": (jj < rr).astype(np.float32),
        "e8": e8,
        "onesrow": np.ones((1, 512), dtype=bf),
        "e16": (np.arange(128)[:, None] == (np.arange(1024)[None, :] // 64)).astype(bf),
        "ones64": np.ones((1, 64), dtype=bf),
        "onescol": np.ones((128, 1), dtype=bf),
        "ones32f": np.ones((1, 32), dtype=np.float32),
        "onesf128": np.ones((1, 128), dtype=np.float32),
    }


class Sem:
    __slots__ = ("h", "n")

    def __init__(self, h):
        self.h = h
        self.n = 0


class Tape:
    ENGS = ("sync", "tensor", "vector", "scalar", "gpsimd")

    def __init__(self):
        self.ops = {e: [] for e in self.ENGS}
        self.waited = {e: {} for e in self.ENGS}

    def add(self, eng, fn, w=(), inc=None, amt=1):
        ws = []
        for s, v in w:
            assert v is not None and v > 0, f"bad wait {v} on {eng}"
            k = id(s)
            if self.waited[eng].get(k, 0) >= v:
                continue
            self.waited[eng][k] = v
            ws.append((s.h, v))
        pair, tick = None, None
        if inc is not None:
            inc.n += amt
            tick = inc.n
            pair = (inc.h, amt)
        self.ops[eng].append((fn, tuple(ws), pair))
        return tick

    def replay(self, eng, e):
        for fn, ws, pair in self.ops[eng]:
            for h, v in ws:
                e.wait_ge(h, v)
            inst = fn(e)
            if pair is not None:
                inst.then_inc(pair[0], pair[1])


def _finish_early(nc, P, waits):
    P.add("sync", lambda e: e.nop(), w=[(s, v) for s, v in waits if v > 0])
    with nc.Block() as block:
        @block.sync
        def _(e):
            P.replay("sync", e)

        @block.tensor
        def _(e):
            P.replay("tensor", e)

        @block.vector
        def _(e):
            P.replay("vector", e)

        @block.scalar
        def _(e):
            P.replay("scalar", e)

        @block.gpsimd
        def _(e):
            P.replay("gpsimd", e)
    return nc


def build(stop_stage=99):
    import os
    nc = bass.Bass("TRN2", target_bir_lowering=False,
                   detect_race_conditions=False)
    nc.dram_tensor("salt" + os.environ.get("KSALT", "0"), [1, 1], F32,
                   kind="Internal")
    HC = _host_consts()

    x = nc.dram_tensor("x", [L, D], F32, kind="ExternalInput").ap()
    Wmats = [nc.dram_tensor(n, [D, D], F32, kind="ExternalInput").ap()
             for n in ("Wq", "Wk", "Wv", "Wc")]
    terr = nc.dram_tensor("terr", [1, SEG], F32, kind="ExternalInput").ap()
    idxd = nc.dram_tensor("idxd", [H, SEG], F32, kind="ExternalInput").ap()
    megad = nc.dram_tensor("megad", [1, H], F32, kind="ExternalInput").ap()
    w1d = nc.dram_tensor("w1", [NE, D, 512], F32, kind="ExternalInput").ap()
    b1d = nc.dram_tensor("b1", [NE, 512], F32, kind="ExternalInput").ap()
    w2d = nc.dram_tensor("w2", [NE, 512, D], F32, kind="ExternalInput").ap()
    b2d = nc.dram_tensor("b2", [NE, D], F32, kind="ExternalInput").ap()
    cent = nc.dram_tensor("cent", [NE, D], F32, kind="ExternalInput").ap()
    wrt = nc.dram_tensor("wrt", [NE, D], F32, kind="ExternalInput").ap()
    cin = {k: nc.dram_tensor("c_" + k, list(v.shape),
                             F32 if v.dtype == np.float32 else BF16,
                             kind="ExternalInput").ap()
           for k, v in HC.items()}
    out = nc.dram_tensor("out", [L, D], F32, kind="ExternalOutput").ap()
    wbf = [nc.dram_tensor(f"wbf{i}", [D, D], BF16, kind="Internal").ap()
           for i in range(4)]
    w1bf = nc.dram_tensor("w1bf", [NE, 128, 4096], BF16, kind="Internal").ap()
    w2bf = nc.dram_tensor("w2bf", [NE, 128, 4096], BF16, kind="Internal").ap()
    dbg = {}
    if DEBUG:
        for n, shp, dt in [("d_xn", [128, 16384], BF16), ("d_q", [128, 16384], BF16),
                           ("d_k", [128, 16384], BF16), ("d_v", [128, 16384], BF16),
                           ("d_y", [128, 16384], BF16), ("d_g", [8, 2048], BF16),
                           ("d_pcs", [32, 2048], BF16), ("d_wt", [128, 512], F32)]:
            dbg[n] = nc.dram_tensor(n, shp, dt, kind="ExternalOutput").ap()

    ar_xn = nc.alloc_sbuf_tensor("ar_xn", [128, 32768], U8).ap()
    ar_q = nc.alloc_sbuf_tensor("ar_q", [128, 32768], U8).ap()
    ar_k = nc.alloc_sbuf_tensor("ar_k", [128, 32768], U8).ap()
    ar_v = nc.alloc_sbuf_tensor("ar_v", [128, 32768], U8).ap()
    ar_y = nc.alloc_sbuf_tensor("ar_y", [128, 32768], U8).ap()
    ar_t = nc.alloc_sbuf_tensor("ar_t", [128, 28672], U8).ap()

    xn_fm = ar_xn.bitcast(BF16)
    q_fm = ar_q.bitcast(BF16)
    k_fm = ar_k.bitcast(BF16)
    v_tm = ar_v.bitcast(BF16)
    y_fm = ar_y.bitcast(BF16)
    wTb = [ar_y[:, 0:16384].bitcast(BF16), ar_y[:, 16384:32768].bitcast(BF16)]
    hprime = ar_q.bitcast(BF16)
    wcT = ar_k[:, 0:16384].bitcast(BF16)
    xst2 = [ar_k[:, 16384:20480].bitcast(F32), ar_k[:, 20480:24576].bitcast(F32)]
    outst = [ar_k[:, 24576:28672].bitcast(F32), ar_k[:, 28672:32768].bitcast(F32)]
    w1b = [ar_v[:, 0:8192].bitcast(BF16), ar_v[:, 8192:16384].bitcast(BF16)]
    w2b = [ar_v[:, 16384:24576].bitcast(BF16), ar_v[:, 24576:32768].bitcast(BF16)]

    xstage = [ar_t[:, 0:4096].bitcast(F32), ar_t[:, 4096:8192].bitcast(F32)]
    xn_st = [ar_t[:, 8192:12288].bitcast(F32), ar_t[:, 12288:16384].bitcast(F32)]
    sqdump = ar_t[:, 16384:20480].bitcast(F32)
    lnS = ar_t[:, 20480:24576].bitcast(F32)
    prepS = ar_t[:, 24576:28672].bitcast(F32)
    Ptm = [ar_t[:, 0:4096].bitcast(BF16), ar_t[:, 4096:8192].bitcast(BF16)]
    stripB = Ptm
    band1 = [ar_t[:, 8192:8448].bitcast(BF16), ar_t[:, 8448:8704].bitcast(BF16),
             None]
    band2P = [nc.alloc_sbuf_tensor("band2", [128, 128], BF16).ap(),
              ar_t[:, 8704:8960].bitcast(BF16)]
    pcsS = [ar_t[:, 8960:13056].bitcast(BF16), ar_t[:, 13056:17152].bitcast(BF16)]
    wTh = [ar_t[:, 17152:19200].bitcast(F32), ar_t[:, 19200:21248].bitcast(F32)]
    atS = ar_t[:, 21248:25344].bitcast(F32)
    hgel = [ar_t[:, 0:2048].bitcast(F32), ar_t[:, 2048:4096].bitcast(F32)]
    b1row = ar_t[0:1, 4096:12288].bitcast(BF16)
    b2_sb = ar_t[0:8, 12288:14336].bitcast(BF16)

    csb = {k: nc.alloc_sbuf_tensor("s_" + k, list(HC[k].shape),
                                   F32 if HC[k].dtype == np.float32 else BF16).ap()
           for k in HC if k != "e8"}
    csb["e8"] = ar_t[0:8, 14336:16384].bitcast(BF16)
    g_t8 = nc.alloc_sbuf_tensor("g_t8", [8, 2048], BF16).ap()
    twrow = nc.alloc_sbuf_tensor("twrow", [1, 128], F32).ap()
    owrep2 = nc.alloc_sbuf_tensor("owrep2", [128, 128], F32).ap()
    crep = nc.alloc_sbuf_tensor("crep", [128, 16], F32).ap()
    lnmega = nc.alloc_sbuf_tensor("lnmega", [1, 16], F32).ap()
    cnrb = nc.alloc_sbuf_tensor("cnrb", [128, 128], BF16).ap()
    centF = nc.alloc_sbuf_tensor("centF", [8, 1024], F32).ap()
    routeF = nc.alloc_sbuf_tensor("routeF", [8, 1024], F32).ap()

    msum, sq2 = lnS[:, 0:2], lnS[:, 2:4]
    mmean, varv = lnS[:, 4:6], lnS[:, 6:8]
    rstd, negm = lnS[:, 8:10], lnS[:, 10:12]
    recn, tvar = lnS[:, 12:14], lnS[:, 24:26]
    m1c, m2c, nm1 = lnS[:, 14:16], lnS[:, 16:18], lnS[:, 18:20]
    gsum, grec = lnS[:, 20:22], lnS[:, 22:24]
    lgec = [lnS[:, 32:48], lnS[:, 48:64]]
    lgf = [lnS[:, 64:72], lnS[:, 72:80]]
    ge1 = [lnS[:, 96:104], lnS[:, 104:112]]
    mskd = [lnS[:, 112:120], lnS[:, 120:128]]
    kept = [lnS[:, 128:136], lnS[:, 136:144]]
    egk = [lnS[:, 144:152], lnS[:, 152:160]]
    gtm = [lnS[:, 160:168], lnS[:, 168:176]]

    cn2, lncn, rcn = prepS[0:8, 0:1], prepS[0:8, 1:2], prepS[0:8, 2:3]
    idxF = prepS[0:16, 8:72]
    c16 = prepS[0:16, 72:136]
    callT = prepS[0:64, 136:152]
    megae, megac = prepS[0:1, 152:168], prepS[0:1, 168:184]
    twe, twp, tw64 = prepS[0:1, 184:248], prepS[0:1, 248:312], prepS[0:1, 312:376]
    ow64 = prepS[0:1, 376:440]
    owrow = prepS[0:1, 440:568]
    sq8b = prepS[0:8, 504:1016]

    zparts = atS[:, 0:8]
    Zc, rZc = atS[:, 8:10], atS[:, 10:12]
    EA, ATf = atS[0:32, 16:48], atS[0:32, 48:80]
    rnb32 = atS[0:32, 80:112]
    lnm32, Pcol = atS[0:32, 112:113], atS[0:32, 113:114]
    rn0, rnorm = atS[0:32, 114:115], atS[0:32, 115:116]
    rnrow = atS[0:1, 116:148]
    ztAll = atS[:, 148:164]
    ztT = atS[0:16, 164:292]
    ztTb = atS[:, 292:356].bitcast(BF16)
    twcol = atS[:, 356:357]
    rztw_sbP = [atS[:, 404:532], prepS[:, 320:448]]
    tcombP = [atS[:, 532:660], prepS[:, 448:576]]
    t2combP = [atS[:, 660:788], prepS[:, 576:704]]
    ATb = prepS[0:32, 708:724].bitcast(BF16)
    A2b = prepS[0:32, 724:740].bitcast(BF16)
    Pcolb = prepS[0:32, 704:705].bitcast(BF16)
    rztwbP = [prepS[0:1, 740:804].bitcast(BF16),
              ar_t[0:1, 25600:25856].bitcast(BF16)]
    cvecP = [prepS[:, 804:820].bitcast(BF16), prepS[:, 820:836].bitcast(BF16)]
    band1[2] = prepS[:, 836:900].bitcast(BF16)

    PS = nc.alloc_psum_tensor("PS", [128, 4096], F32).ap()

    sT, sV, sS = Sem(nc.alloc_semaphore("sT")), Sem(nc.alloc_semaphore("sV")), \
        Sem(nc.alloc_semaphore("sS"))
    dS, dG, sO = Sem(nc.alloc_semaphore("dS")), Sem(nc.alloc_semaphore("dG")), \
        Sem(nc.alloc_semaphore("sO"))

    P = Tape()

    def T(fn, w=(), inc=True):
        return P.add("tensor", fn, w, sT if inc else None)

    def V(fn, w=(), inc=True):
        return P.add("vector", fn, w, sV if inc else None)

    def S(fn, w=(), inc=True):
        return P.add("scalar", fn, w, sS if inc else None)

    def SY(fn, w=()):
        return P.add("sync", fn, w, dS, 16)

    def SYo(fn, w=()):
        return P.add("sync", fn, w, sO, 16)

    def G(fn, w=()):
        return P.add("gpsimd", fn, w, dG, 16)

    def Tdrain():
        P.add("tensor", lambda e: e.nop(), w=[(sT, sT.n)] if sT.n else [])

    bank_free = {b: [] for b in range(8)}

    def bfree(b, sem, tick):
        bank_free[b] = [(sem, tick)]

    def bwait(*banks):
        ws = []
        for b in banks:
            ws += bank_free[b]
        return ws

    for k in HC:
        if k == "e8":
            continue
        SY(lambda e, d=csb[k], s=cin[k]: e.dma_start(out=d, in_=s))
    SY(lambda e: e.dma_start(out=idxF, in_=idxd))
    SY(lambda e: e.dma_start(out=megae, in_=megad))
    SY(lambda e: e.dma_start(out=twe, in_=terr))
    SY(lambda e: e.dma_start(out=centF, in_=cent))
    SY(lambda e: e.dma_start(out=routeF, in_=wrt))
    t_prein = dS.n
    tw_wbf = [G(lambda e, d=wbf[i], s=Wmats[i]: e.dma_start(out=d, in_=s))
              for i in range(4)]
    t_w1bf, t_w2bf = [], []
    for e_ in range(NE):
        t_w1bf.append(G(lambda e, i=e_: e.dma_start(
            out=w1bf[i], in_=w1d[i].rearrange("(ic p) h -> p ic h", p=128))))
        t_w2bf.append(G(lambda e, i=e_: e.dma_start(
            out=w2bf[i], in_=w2d[i].rearrange("(hc p) o -> p hc o", p=128))))

    wp = [(dS, t_prein)]
    a1 = S(lambda e: e.activation(out=sq8b, in_=centF[:, 0:512], func=AF.Square,
                                  accum_out=cn2), w=wp)
    a2 = S(lambda e: e.activation(out=sq8b, in_=centF[:, 512:1024], func=AF.Square,
                                  accum_out=lncn), w=[(sS, a1)])
    a3 = V(lambda e: e.tensor_tensor(out=cn2, in0=cn2, in1=lncn, op=ALU.add),
           w=[(sS, a2)])
    a4 = S(lambda e: e.activation(out=lncn, in_=cn2, func=AF.Ln), w=[(sV, a3)])
    a5 = S(lambda e: e.activation(out=rcn, in_=lncn, func=AF.Exp, scale=-0.5),
           w=[(sS, a4)])
    a6 = V(lambda e: e.tensor_scalar(out=centF, in0=centF, scalar1=rcn,
                                     scalar2=1.0 / 32.0, op0=ALU.mult,
                                     op1=ALU.mult), w=[(sS, a5)])
    tt = None
    for ic in range(8):
        T(lambda e, ic=ic: e.transpose(
            PS[:, 3584 + ic * 16:3592 + ic * 16],
            centF[0:8, ic * 128:(ic + 1) * 128], csb["idnf32"][0:8, 0:8]),
            w=[(sV, a6)], inc=False)
        tt = T(lambda e, ic=ic: e.transpose(
            PS[:, 3592 + ic * 16:3600 + ic * 16],
            routeF[0:8, ic * 128:(ic + 1) * 128], csb["idnf32"][0:8, 0:8]),
            w=[(dS, t_prein)], inc=(ic == 7))
    a7 = V(lambda e: e.tensor_copy(out=cnrb, in_=PS[:, 3584:3712]), w=[(sT, tt)])
    bfree(7, sV, a7)
    b1_ = S(lambda e: e.activation(out=c16, in_=idxF, func=AF.Exp), w=wp)
    b2_ = T(lambda e: e.transpose(PS[0:64, 3072:3088], c16[0:16, 0:64],
                                  csb["idnf32"][0:16, 0:16]), w=[(sS, b1_)])
    b3 = V(lambda e: e.tensor_copy(out=callT, in_=PS[0:64, 3072:3088]), w=[(sT, b2_)])
    b4 = T(lambda e: e.matmul(PS[:, 3088:3104], csb["rep2"], callT,
                              start=True, stop=True), w=[(sV, b3)])
    t_crep = V(lambda e: e.tensor_copy(out=crep, in_=PS[:, 3088:3104]), w=[(sT, b4)])
    c1_ = S(lambda e: e.activation(out=megac, in_=megae, func=AF.Exp, scale=64.0), w=wp)
    c2_ = V(lambda e: e.tensor_scalar_min(out=megac, in0=megac, scalar1=2.0),
            w=[(sS, c1_)])
    t_lnmega = S(lambda e: e.activation(out=lnmega, in_=megac, func=AF.Ln),
                 w=[(sV, c2_)])
    d1 = S(lambda e: e.activation(out=twp, in_=twe, func=AF.Exp, scale=-1.0), w=wp)
    d2 = V(lambda e: e.tensor_scalar_add(out=twp, in0=twp, scalar1=1.0), w=[(sS, d1)])
    d3 = V(lambda e: e.reciprocal(out=tw64, in_=twp), w=[(sV, d2)])
    d4 = V(lambda e: e.tensor_copy(out=twrow[0:1, 0:64], in_=tw64), w=[(sV, d3)])
    t_twrow = V(lambda e: e.tensor_copy(out=twrow[0:1, 64:128], in_=tw64),
                w=[(sV, d3)])
    d6 = V(lambda e: e.tensor_scalar(out=ow64, in0=tw64, scalar1=-1.0, scalar2=1.0,
                                     op0=ALU.mult, op1=ALU.add), w=[(sV, d3)])
    d7 = V(lambda e: e.tensor_copy(out=owrow[0:1, 0:64], in_=ow64), w=[(sV, d6)])
    d8 = V(lambda e: e.tensor_copy(out=owrow[0:1, 64:128], in_=ow64), w=[(sV, d6)])
    d9 = T(lambda e: e.matmul(PS[:, 3104:3232], csb["onesf128"], owrow,
                              start=True, stop=True), w=[(sV, d7), (sV, d8)])
    t_owrep = V(lambda e: e.tensor_copy(out=owrep2, in_=PS[:, 3104:3232]),
                w=[(sT, d9)])
    tD = G(lambda e: e.dma_start(out=twcol, in_=terr.broadcast_to([2, 64])))
    e1_ = S(lambda e: e.activation(out=twcol, in_=twcol, func=AF.Exp, scale=-1.0),
            w=[(dG, tD)])
    e2_ = V(lambda e: e.tensor_scalar_add(out=twcol, in0=twcol, scalar1=1.0),
            w=[(sS, e1_)])
    t_twcol = V(lambda e: e.reciprocal(out=twcol, in_=twcol), w=[(sV, e2_)])
    bfree(6, sV, t_owrep)

    t_xn = [0] * NT
    t_lnS = [0] * NT
    t_lnV = [0] * NT
    t_xtr = [0] * NT
    for tc in range(NT):
        par = tc % 2
        w_x = [(sS, t_lnS[tc - 2]), (sV, t_lnV[tc - 2])] if tc >= 2 else []
        tx = SY(lambda e, tc=tc, p=par: e.dma_start(
            out=xstage[p], in_=x[tc * 128:(tc + 1) * 128, :]), w=w_x)
        wx = [(dS, tx)]
        va = V(lambda e, p=par: e.reduce_sum(out=msum[:, p:p + 1], in_=xstage[p],
                                             axis=AX), w=wx)
        sa = S(lambda e, p=par: e.activation(out=sqdump, in_=xstage[p],
                                             func=AF.Square,
                                             accum_out=sq2[:, p:p + 1]), w=wx)
        vb = V(lambda e, p=par: e.tensor_scalar_mul(
            out=mmean[:, p:p + 1], in0=msum[:, p:p + 1], scalar1=-1.0 / D),
            w=[(sV, va)])
        vc = V(lambda e, p=par: e.tensor_scalar(
            out=tvar[:, p:p + 1], in0=mmean[:, p:p + 1], scalar1=mmean[:, p:p + 1],
            scalar2=None, op0=ALU.mult), w=[(sV, vb)])
        vd = V(lambda e, p=par: e.tensor_scalar_mul(
            out=varv[:, p:p + 1], in0=sq2[:, p:p + 1], scalar1=1.0 / D),
            w=[(sS, sa), (sV, vc)])
        ve = V(lambda e, p=par: e.tensor_tensor(
            out=varv[:, p:p + 1], in0=varv[:, p:p + 1], in1=tvar[:, p:p + 1],
            op=ALU.subtract), w=[(sV, vd)])
        vf = V(lambda e, p=par: e.tensor_scalar_add(
            out=tvar[:, p:p + 1], in0=varv[:, p:p + 1], scalar1=1e-5), w=[(sV, ve)])
        sb_ = S(lambda e, p=par: e.activation(out=rstd[:, p:p + 1],
                                              in_=tvar[:, p:p + 1], func=AF.Ln),
                w=[(sV, vf)])
        sc_ = S(lambda e, p=par: e.activation(out=rstd[:, p:p + 1],
                                              in_=rstd[:, p:p + 1], func=AF.Exp,
                                              scale=-0.5), w=[(sS, sb_)])
        vh = V(lambda e, p=par: e.tensor_scalar(
            out=recn[:, p:p + 1], in0=varv[:, p:p + 1], scalar1=rstd[:, p:p + 1],
            scalar2=rstd[:, p:p + 1], op0=ALU.mult, op1=ALU.mult), w=[(sS, sc_)])
        si = S(lambda e, p=par: e.activation(out=recn[:, p:p + 1],
                                             in_=recn[:, p:p + 1], func=AF.Ln),
               w=[(sV, vh)])
        sj = S(lambda e, p=par: e.activation(out=recn[:, p:p + 1],
                                             in_=recn[:, p:p + 1], func=AF.Exp,
                                             scale=-0.5), w=[(sS, si)])
        w_xn = [(sS, sc_)] + ([(sT, t_xtr[tc - 2])] if tc >= 2 else [])
        sk = V(lambda e, p=par: e.tensor_scalar(
            out=xn_st[p], in0=xstage[p], scalar1=mmean[:, p:p + 1],
            scalar2=rstd[:, p:p + 1], op0=ALU.add, op1=ALU.mult), w=w_xn)
        t_lnS[tc] = sa
        base = par * 1024
        wtr = [(sV, sk)] + (bwait(par * 2, par * 2 + 1) if tc >= 2 else [])
        ttr = None
        for ic in range(8):
            ttr = T(lambda e, p=par, ic=ic, b=base: e.transpose(
                PS[:, b + ic * 128: b + (ic + 1) * 128],
                xn_st[p][:, ic * 128:(ic + 1) * 128], csb["idnf32"]),
                w=wtr, inc=(ic == 7))
        t_xtr[tc] = ttr
        vk = V(lambda e, tc=tc, b=base: e.tensor_copy(
            out=xn_fm.rearrange("p (ic t) -> p ic t", ic=8)[:, :, tc * 128:(tc + 1) * 128],
            in_=PS[:, b:b + 1024].rearrange("p (ic j) -> p ic j", ic=8)),
            w=[(sT, ttr)])
        t_xn[tc] = vk
        bfree(par * 2, sV, vk)
        bfree(par * 2 + 1, sV, vk)
        rb = 2048 + par * 512
        wr = [(sV, vk), (sV, a7)] + (bwait(4 + par) if tc >= 2 else [])
        trm = None
        for ic in range(8):
            trm = T(lambda e, ic=ic, tc=tc, rb=rb: e.matmul(
                PS[:, rb:rb + 16],
                xn_fm[:, ic * 2048 + tc * 128: ic * 2048 + (tc + 1) * 128],
                cnrb[:, ic * 16:(ic + 1) * 16],
                start=(ic == 0), stop=(ic == 7), skip_group_check=True),
                w=wr, inc=(ic == 7))
        r1 = V(lambda e, p=par, rb=rb: e.tensor_copy(out=lgec[p], in_=PS[:, rb:rb + 16]),
               w=[(sT, trm)])
        bfree(4 + par, sV, r1)
        r2 = V(lambda e, p=par: e.scalar_tensor_tensor(
            out=lgf[p], in0=lgec[p][:, 0:8], scalar=recn[:, p:p + 1],
            in1=lgec[p][:, 8:16], op0=ALU.mult, op1=ALU.add),
            w=[(sV, r1), (sS, sj)])
        r3 = V(lambda e, p=par: e.reduce_max(out=m1c[:, p:p + 1], in_=lgf[p], axis=AX),
               w=[(sV, r2)])
        r4 = V(lambda e, p=par: e.tensor_scalar(
            out=ge1[p], in0=lgf[p], scalar1=m1c[:, p:p + 1], scalar2=None,
            op0=ALU.is_ge), w=[(sV, r3)])
        r5 = V(lambda e, p=par: e.scalar_tensor_tensor(
            out=mskd[p], in0=ge1[p], scalar=-1e30, in1=lgf[p],
            op0=ALU.mult, op1=ALU.add), w=[(sV, r4)])
        r6 = V(lambda e, p=par: e.reduce_max(out=m2c[:, p:p + 1], in_=mskd[p], axis=AX),
               w=[(sV, r5)])
        r7 = V(lambda e, p=par: e.tensor_scalar(
            out=kept[p], in0=lgf[p], scalar1=m2c[:, p:p + 1], scalar2=None,
            op0=ALU.is_ge), w=[(sV, r6)])
        r8 = V(lambda e, p=par: e.tensor_scalar_mul(
            out=nm1[:, p:p + 1], in0=m1c[:, p:p + 1], scalar1=-1.0), w=[(sV, r3)])
        r9 = S(lambda e, p=par: e.activation(out=egk[p], in_=lgf[p], func=AF.Exp,
                                             bias=nm1[:, p:p + 1]), w=[(sV, r8)])
        r10 = V(lambda e, p=par: e.tensor_tensor(out=egk[p], in0=egk[p], in1=kept[p],
                                                 op=ALU.mult), w=[(sS, r9), (sV, r7)])
        r11 = V(lambda e, p=par: e.reduce_sum(out=gsum[:, p:p + 1], in_=egk[p], axis=AX),
                w=[(sV, r10)])
        r12 = V(lambda e, p=par: e.reciprocal(out=grec[:, p:p + 1],
                                              in_=gsum[:, p:p + 1]), w=[(sV, r11)])
        r13 = V(lambda e, p=par: e.tensor_scalar_mul(
            out=gtm[p], in0=egk[p], scalar1=grec[:, p:p + 1]), w=[(sV, r12)])
        gb = 3072 + par * 128
        r14 = T(lambda e, p=par, gb=gb: e.transpose(
            PS[0:8, gb:gb + 128], gtm[p], csb["idnf32"]), w=[(sV, r13)] + bwait(6))
        r15 = V(lambda e, tc=tc, gb=gb: e.tensor_copy(
            out=g_t8[0:8, tc * 128:(tc + 1) * 128], in_=PS[0:8, gb:gb + 128]),
            w=[(sT, r14)])
        t_lnV[tc] = r15
        bfree(6, sV, r15)
    t_ln_done = t_lnV[NT - 1]
    t_xn_all = t_xn[NT - 1]
    if stop_stage <= 1:
        return _finish_early(nc, P, [(sT, sT.n), (sV, sV.n), (sS, sS.n),
                                     (dS, dS.n), (dG, dG.n)])

    t_wT = [0, 0, 0]
    for m in range(2):
        tk = None
        for ic in range(8):
            tk = SY(lambda e, m=m, ic=ic: e.dma_start(
                out=wTb[m][:, ic * 1024:(ic + 1) * 1024],
                in_=wbf[m][:, ic * 128:(ic + 1) * 128], transpose=True),
                w=[(dG, tw_wbf[m])])
        t_wT[m] = tk

    def mm_proj(dst_fm, wsrc, t_wready):
        tick_last = None
        for oc in range(8):
            for tg in range(4):
                bk = (oc * 4 + tg) % 2
                w0 = [(dS, t_wready), (sV, t_xn_all)] + bwait(bk)
                tm = None
                for ic in range(8):
                    tm = T(lambda e, oc=oc, tg=tg, ic=ic, bk=bk, ws=wsrc: e.matmul(
                        PS[:, bk * 512:(bk + 1) * 512],
                        ws[:, ic * 1024 + oc * 128: ic * 1024 + (oc + 1) * 128],
                        xn_fm[:, ic * 2048 + tg * 512: ic * 2048 + (tg + 1) * 512],
                        start=(ic == 0), stop=(ic == 7), skip_group_check=True),
                        w=w0, inc=(ic == 7))
                tv_ = V(lambda e, oc=oc, tg=tg, bk=bk, d=dst_fm: e.tensor_copy(
                    out=d[:, oc * 2048 + tg * 512: oc * 2048 + (tg + 1) * 512],
                    in_=PS[:, bk * 512:(bk + 1) * 512]), w=[(sT, tm)])
                bfree(bk, sV, tv_)
                tick_last = tm
        return tick_last

    t_qmm = mm_proj(q_fm, wTb[0], t_wT[0])
    mm_proj(k_fm, wTb[1], t_wT[1])
    tk = None
    for ic in range(8):
        tk = SY(lambda e, ic=ic: e.dma_start(
            out=wTb[0][:, ic * 1024:(ic + 1) * 1024],
            in_=wbf[2][:, ic * 128:(ic + 1) * 128], transpose=True),
            w=[(dG, tw_wbf[2]), (sT, t_qmm)])
    t_wT[2] = tk
    for tc in range(NT):
        for og in range(2):
            bk = (tc * 2 + og) % 2
            w0 = [(dS, t_wT[2])] + bwait(bk)
            tm = None
            for ic in range(8):
                tm = T(lambda e, tc=tc, og=og, ic=ic, bk=bk: e.matmul(
                    PS[:, bk * 512:(bk + 1) * 512],
                    xn_fm[:, ic * 2048 + tc * 128: ic * 2048 + (tc + 1) * 128],
                    wTb[0][:, ic * 1024 + og * 512: ic * 1024 + (og + 1) * 512],
                    start=(ic == 0), stop=(ic == 7), skip_group_check=True),
                    w=w0, inc=(ic == 7))
            tv_ = V(lambda e, tc=tc, og=og, bk=bk: e.tensor_copy(
                out=v_tm[:, tc * 1024 + og * 512: tc * 1024 + (og + 1) * 512],
                in_=PS[:, bk * 512:(bk + 1) * 512]), w=[(sT, tm)])
            bfree(bk, sV, tv_)
    t_qkv_T = sT.n
    if stop_stage <= 2:
        return _finish_early(nc, P, [(sT, sT.n), (sV, sV.n), (sS, sS.n),
                                     (dS, dS.n), (dG, dG.n)])

    war_pt = [[], []]
    war_cvec = []
    war_pcs = [[], []]
    war_wTh = [[], []]
    war_band1 = [[], []]
    war_band2 = [[], []]
    war_fin = [[], []]
    t_comb_prev = 0

    tbz0 = V(lambda e: e.memset(band1[0], 0.0), w=[(sT, t_qkv_T)])
    tbz1 = V(lambda e: e.memset(band1[1], 0.0), w=[(sT, t_qkv_T)])
    tbz2 = V(lambda e: e.memset(band1[2], 0.0), w=[(sT, t_qkv_T)])
    V(lambda e: e.memset(ztTb, 0.0), w=[(sT, t_qkv_T)], inc=False)
    war_band1 = [[(sV, tbz0)], [(sV, tbz1)], [(sV, tbz2)]]

    for h in range(H):
        hp = h % 2
        hb = hp * 64
        icb = (h // 2) * 2048
        tp = (0, 64) if hp else None
        qsl_all = q_fm[hb:hb + 64, icb:icb + 2048]
        ksl_all = k_fm[hb:hb + 64, icb:icb + 2048]

        import os as _os2
        _kmin = _os2.environ.get("KMIN", "")
        if _kmin:
            t_wTr = V(lambda e, hp=hp: e.memset(wTh[hp], 0.001),
                      w=[(sT, t_qkv_T)])
            t_pcs = [sT.n if sT.n else 1] * NT
            t_pcsev = t_wTr
            twt = sT.n if sT.n else 1
        _kmin_noA = _kmin in ("1", "3")
        _kmin_noScan = _kmin in ("1", "2")
        Tdrain()
        if _kmin_noA if _kmin else False:
            tcv = None
        else:
            V(lambda e: e.memset(cvecP[0], 0.0), w=war_cvec, inc=False)
            tcv = V(lambda e: e.memset(cvecP[1], 0.0), w=war_cvec)
        t_pcs = [0] * NT
        t_exp = {}
        t_cv = [0] * NT
        t_ztred = [0] * NT

        def emit_pcs(i):
            W = 128 * (i + 1)
            ngrp = (W + 511) // 512
            Tdrain()
            tpc = None
            for Gi in range(ngrp):
                n0, n1 = 512 * Gi, min(W, 512 * Gi + 512)
                first = (i == 4 * Gi)
                wpc = [(sV, t_cv[i]), (sS, t_exp[(i, Gi)])]
                if first and n1 < 512 * (Gi + 1):
                    tz = V(lambda e, p=i % 2, n1=n1, Gi=Gi: e.memset(
                        Ptm[p][:, n1:512 * (Gi + 1)], 0.0),
                        w=[(sT, t_pcs[i - 2])] if i >= 2 else [])
                    wpc.append((sV, tz))
                    n1 = 512 * (Gi + 1)
                if first:
                    wpc += bwait(4 + Gi) + war_pcs[hp]
                tpc = T(lambda e, n0=n0, n1=n1, first=first, i=i: e.matmul(
                    PS[0:32, 2048 + n0: 2048 + n1], cvecP[i % 2],
                    Ptm[i % 2][:, n0:n1],
                    start=first, stop=(i == NT - 1), skip_group_check=True),
                    w=wpc, inc=(Gi == ngrp - 1))
            t_pcs[i] = tpc

        for i in ([] if (_kmin and _kmin_noA) else range(NT)):
            W = 128 * (i + 1)
            par = i % 2
            ngrp = (W + 511) // 512
            qsl = qsl_all[:, i * 128:(i + 1) * 128]
            tsm_l = {}
            for Gi in range(ngrp):
                n0, n1 = 512 * Gi, min(W, 512 * Gi + 512)
                w0 = bwait(Gi)
                if h == 0 and i == 0 and Gi == 0:
                    w0 = w0 + [(sT, t_qkv_T)]
                lastg = (Gi == ngrp - 1)
                if not lastg:
                    tsm = T(lambda e, n0=n0, n1=n1, q=qsl, ks=ksl_all: e.matmul(
                        PS[:, n0:n1], q, ks[:, n0:n1],
                        start=True, stop=True, skip_group_check=True), w=w0)
                else:
                    T(lambda e, n0=n0, n1=n1, q=qsl, ks=ksl_all: e.matmul(
                        PS[:, n0:n1], q, ks[:, n0:n1],
                        start=True, stop=False, skip_group_check=True),
                        w=w0, inc=False)
                    tsm = T(lambda e, i=i: e.matmul(
                        PS[:, i * 128:(i + 1) * 128], csb["strictu"], csb["negi"],
                        start=False, stop=True, skip_group_check=True))
                tsm_l[Gi] = tsm
            if i >= 1:
                emit_pcs(i - 1)
            for Gi in range(ngrp):
                n0, n1 = 512 * Gi, min(W, 512 * Gi + 512)
                wx = [(sT, tsm_l[Gi])]
                if i >= 2:
                    wx.append((sT, t_pcs[i - 2]))
                wx += war_pt[par]
                te = S(lambda e, p=par, n0=n0, n1=n1, g=Gi: e.activation(
                    out=Ptm[p][:, n0:n1], in_=PS[:, n0:n1], func=AF.Exp,
                    scale=0.125, accum_out=zparts[:, p * 4 + g: p * 4 + g + 1]),
                    w=wx)
                t_exp[(i, Gi)] = te
                bfree(Gi, sS, te)
            war_pt[par] = []
            z1 = V(lambda e, p=par, ng=ngrp: e.reduce_sum(
                out=Zc[:, p:p + 1], in_=zparts[:, p * 4: p * 4 + ng], axis=AX),
                w=[(sS, t_exp[(i, ngrp - 1)])])
            z2 = V(lambda e, p=par: e.reciprocal(out=rZc[:, p:p + 1],
                                                 in_=Zc[:, p:p + 1]), w=[(sV, z1)])
            wc = [(sV, z2), (sV, tcv), (sV, t_crep)]
            if i >= 2:
                V(lambda e, i=i, p=par: e.memset(
                    cvecP[p][:, 2 * i - 4: 2 * i - 2], 0.0),
                    w=[(sT, t_pcs[i - 2])], inc=False)
            cv1 = V(lambda e, i=i, h=h, p=par: e.tensor_tensor(
                out=cvecP[p][0:64, 2 * i: 2 * i + 1], in0=crep[0:64, h:h + 1],
                in1=rZc[0:64, p:p + 1], op=ALU.mult), w=wc)
            cv2 = V(lambda e, i=i, h=h, p=par: e.tensor_tensor(
                out=cvecP[p][64:128, 2 * i + 1: 2 * i + 2], in0=crep[64:128, h:h + 1],
                in1=rZc[64:128, p:p + 1], op=ALU.mult), w=wc)
            t_cv[i] = cv2
            c0a, c1a = max(0, 128 * i - 64), 128 * i + 64
            c0b, c1b = 128 * i, W
            wzr = [(sS, t_exp[(i, g)]) for g in
                   sorted({c0a // 512, (c1a - 1) // 512, c0b // 512, (c1b - 1) // 512})]
            if h > 0:
                wzr.append((sT, t_ztT_read))
            V(lambda e, i=i, p=par, c0a=c0a, c1a=c1a: e.reduce_sum(
                out=ztAll[0:64, i:i + 1], in_=Ptm[p][0:64, c0a:c1a], axis=AX),
                w=wzr, inc=False)
            tzr = V(lambda e, i=i, p=par, c0b=c0b, c1b=c1b: e.reduce_sum(
                out=ztAll[64:128, i:i + 1], in_=Ptm[p][64:128, c0b:c1b], axis=AX),
                w=wzr)
            t_ztred[i] = tzr
        if not (_kmin and _kmin_noA):
            emit_pcs(NT - 1)
        if _kmin and _kmin_noA:
            pass
        else:
            war_cvec = [(sT, t_pcs[NT - 1])]
        if (not _kmin) or not _kmin_noScan:
            if stop_stage == 21 and h == 0:
                return _finish_early(nc, P, [(sT, sT.n), (sV, sV.n), (sS, sS.n),
                                             (dS, dS.n), (dG, dG.n)])
            tevp = None
            for g in range(4):
                tevp = V(lambda e, g=g, hp=hp: e.tensor_copy(
                    out=pcsS[hp][0:32, 512 * g: 512 * (g + 1)],
                    in_=PS[0:32, 2048 + 512 * g: 2048 + 512 * (g + 1)]),
                    w=[(sT, t_pcs[NT - 1])] + war_pcs[hp])
                bfree(4 + g, sV, tevp)
            t_pcsev = tevp
            war_pcs[hp] = []

            s1_ = T(lambda e, h=h: e.matmul(PS[0:32, 3584:3585], csb["ones32f"],
                                            lnmega[0:1, h:h + 1], start=True, stop=True,
                                            skip_group_check=True),
                    w=[(sS, t_lnmega)] + bwait(7))
            s2_ = V(lambda e: e.tensor_copy(out=lnm32, in_=PS[0:32, 3584:3585]),
                    w=[(sT, s1_)])
            s3_ = V(lambda e: e.tensor_scalar_mul(out=EA, in0=csb["emat"], scalar1=lnm32),
                    w=[(sV, s2_)])
            s4_ = S(lambda e: e.activation(out=EA, in_=EA, func=AF.Exp), w=[(sV, s3_)])
            s5_ = V(lambda e: e.tensor_tensor(out=ATf, in0=EA, in1=csb["ltri"],
                                              op=ALU.mult), w=[(sS, s4_)])
            s6_ = V(lambda e: e.tensor_copy(out=ATb, in_=ATf), w=[(sV, s5_)])
            s7_ = V(lambda e, hp=hp: e.reduce_sum(out=Pcol, in_=pcsS[hp][0:32, :], axis=AX),
                    w=[(sV, t_pcsev)])
            s8_ = V(lambda e: e.tensor_copy(out=Pcolb[0:32, 0:1], in_=Pcol), w=[(sV, s7_)])
            s9_ = T(lambda e: e.matmul(PS[0:32, 3585:3586], ATb, Pcolb[0:32, 0:1],
                                       start=True, stop=True, skip_group_check=True),
                    w=[(sV, s6_), (sV, s8_)])
            s10_ = V(lambda e: e.tensor_scalar_add(out=rn0, in0=PS[0:32, 3585:3586],
                                                   scalar1=1e-5), w=[(sT, s9_)])
            s11_ = V(lambda e: e.reciprocal(out=rnorm, in_=rn0), w=[(sV, s10_)])
            s12_ = T(lambda e: e.transpose(PS[0:1, 3586:3618], rnorm,
                                           csb["idnf32"][0:32, 0:32]), w=[(sV, s11_)])
            s13_ = V(lambda e: e.tensor_copy(out=rnrow, in_=PS[0:1, 3586:3618]),
                     w=[(sT, s12_)])
            s14_ = T(lambda e: e.matmul(PS[0:32, 3618:3650], csb["ones32f"], rnrow,
                                        start=True, stop=True, skip_group_check=True),
                     w=[(sV, s13_)])
            s15_ = V(lambda e: e.tensor_copy(out=rnb32, in_=PS[0:32, 3618:3650]),
                     w=[(sT, s14_)])
            s16_ = V(lambda e: e.tensor_tensor(out=A2b, in0=ATf, in1=rnb32, op=ALU.mult),
                     w=[(sV, s15_)])
            bfree(7, sV, s15_)
            twt = None
            for c in range(NCH):
                twt = T(lambda e, c=c, hp=hp: e.matmul(
                    PS[:, 3072 + 32 * c: 3072 + 32 * (c + 1)],
                    pcsS[hp][0:32, 128 * c: 128 * (c + 1)], A2b,
                    start=True, stop=True, skip_group_check=True),
                    w=[(sV, s16_)] + (bwait(6) if c == 0 else []), inc=(c == NCH - 1))
            war_pcs[hp] = [(sT, twt)]
            t_wTr = V(lambda e, hp=hp: e.tensor_copy(out=wTh[hp], in_=PS[:, 3072:3584]),
                      w=[(sT, twt)] + war_wTh[hp])
            zz1 = V(lambda e: e.reciprocal(out=ztAll, in_=ztAll),
                    w=[(sV, t_ztred[NT - 1])])
            zz2 = V(lambda e: e.tensor_scalar_mul(out=ztAll, in0=ztAll,
                                                  scalar1=twcol),
                    w=[(sV, zz1), (sV, t_twcol)])
            zz3 = T(lambda e: e.transpose(PS[0:16, 3584:3712], ztAll,
                                          csb["idnf32"]),
                    w=[(sV, zz2)] + bwait(7))
            t_ztT_read = zz3
            zz4 = V(lambda e: e.tensor_copy(out=ztT, in_=PS[0:16, 3584:3712]),
                    w=[(sT, zz3)])
            t_ztTb = V(lambda e: e.tensor_copy(out=ztTb[0:16, :], in_=ztT),
                       w=[(sV, zz4)])
            bfree(7, sV, zz4)
            if stop_stage == 22 and h == 0:
                return _finish_early(nc, P, [(sT, sT.n), (sV, sV.n), (sS, sS.n),
                                             (dS, dS.n), (dG, dG.n)])

        Tdrain()
        sfree = {0: list(bank_free[0]), 1: list(bank_free[1])}
        t_stm = {}
        t_wg = {}
        t_pv = [0] * NCH
        t_bmm = [0] * NCH
        t_b2m = [0] * NCH
        t_rzev = [0] * NCH
        t_cmbm = [0] * NCH
        t_cmb = [0] * NCH
        t_b1w = [0] * NCH
        def emit_fin(c):
            pr2 = c % 2
            bac = 3072 + pr2 * 128
            rzc = 3328 + pr2 * 128
            wfin = []
            if c > 1:
                wfin += [(sV, t_rzev[c - 2]), (sV, t_cmbm[c - 2])]
            else:
                wfin += war_fin[pr2] + bwait(6)
            wbm = [(sV, t_b2m[c])] + wfin
            if c > 0:
                wbm += [(sS, t_b1w[c])]
                vsl1 = v_tm[:, (c - 1) * 1024 + h * 64: (c - 1) * 1024 + (h + 1) * 64]
                T(lambda e, c=c, vs=vsl1, hb=hb, tp=tp, bac=bac: e.matmul(
                    PS[hb:hb + 64, bac:bac + 128], vs, band1[c % 3],
                    start=True, stop=False, skip_group_check=True,
                    tile_position=tp), w=wbm, inc=False)
            vsl = v_tm[:, c * 1024 + h * 64: c * 1024 + (h + 1) * 64]
            t_bmm[c] = T(lambda e, c=c, vs=vsl, hb=hb, tp=tp, bac=bac: e.matmul(
                PS[hb:hb + 64, bac:bac + 128], vs, band2P[c % 2],
                start=(c == 0), stop=True, skip_group_check=True,
                tile_position=tp), w=wbm)
            Tdrain()
            tbc = T(lambda e, c=c, hb=hb, tp=tp, rzc=rzc: e.matmul(
                PS[hb:hb + 64, rzc:rzc + 128],
                csb["e16"][:, c * 64: c * 64 + 64], ztTb,
                start=True, stop=True, skip_group_check=True,
                tile_position=tp), w=wfin + [(sV, t_ztTb)])
            u4 = V(lambda e, hb=hb, pr2=pr2, rzc=rzc: e.tensor_copy(
                out=rztw_sbP[pr2][hb:hb + 64, :],
                in_=PS[hb:hb + 64, rzc:rzc + 128]), w=[(sT, tbc)])
            t_rzev[c] = u4
            u5 = V(lambda e, hb=hb, pr2=pr2, bac=bac: e.tensor_tensor(
                out=tcombP[pr2][hb:hb + 64, :], in0=PS[hb:hb + 64, bac:bac + 128],
                in1=rztw_sbP[pr2][hb:hb + 64, :], op=ALU.mult),
                w=[(sV, u4), (sT, t_bmm[c])])
            t_cmbm[c] = u5
            qc0 = 1024 + 512 * (c // 4) + (c % 4) * 128
            u6 = V(lambda e, qc0=qc0, hb=hb, pr2=pr2: e.tensor_tensor(
                out=t2combP[pr2][hb:hb + 64, :], in0=PS[hb:hb + 64, qc0:qc0 + 128],
                in1=owrep2[hb:hb + 64, :], op=ALU.mult),
                w=[(sT, t_pv[c]), (sV, t_owrep)])
            u7 = V(lambda e, c=c, hb=hb, icb=icb, pr2=pr2: e.tensor_tensor(
                out=y_fm[hb:hb + 64, icb + c * 128: icb + (c + 1) * 128],
                in0=tcombP[pr2][hb:hb + 64, :], in1=t2combP[pr2][hb:hb + 64, :],
                op=ALU.add), w=[(sV, u5), (sV, u6)])
            t_cmb[c] = u7

        for c in range(NCH):
            Gmin = c // 4
            Tdrain()
            ksl = ksl_all[:, c * 128:(c + 1) * 128]
            G2b = Gmin
            G1b = (128 * c + 128) // 512 if c < NCH - 1 else -1
            off0 = 128 * c - 512 * G2b
            sb0 = G2b % 2
            tb2 = None
            tb1 = None
            for Gi in range(Gmin, 4):
                n0, n1 = max(128 * c, 512 * Gi), 512 * (Gi + 1)
                sb = Gi % 2
                off = n0 - 512 * Gi
                w0 = list(sfree[sb])
                sfree[sb] = []
                tsm = T(lambda e, n0=n0, n1=n1, sb=sb, off=off, k=ksl,
                        qs=qsl_all: e.matmul(
                    PS[:, sb * 512 + off: sb * 512 + off + (n1 - n0)],
                    k, qs[:, n0:n1], start=True, stop=True, skip_group_check=True),
                    w=w0)
                t_stm[(c, Gi)] = tsm
                wact = []
                if Gi == G2b:
                    wb2 = [(sT, tsm)]
                    if c > 1:
                        wb2 += [(sT, t_bmm[c - 2])]
                    else:
                        wb2 += war_band2[c % 2]
                    tb2 = S(lambda e, sb0=sb0, off0=off0, c=c: e.activation(
                        out=band2P[c % 2],
                        in_=PS[:, sb0 * 512 + off0: sb0 * 512 + off0 + 128],
                        func=AF.Exp, scale=0.125), w=wb2)
                    sfree[sb0].append((sS, tb2))
                    wact.append((sS, tb2))
                if Gi == G1b:
                    colB1 = 128 * c + 128
                    off1 = colB1 - 512 * G1b
                    sb1 = G1b % 2
                    wb1 = [(sT, tsm)]
                    if c > 1:
                        wb1 += [(sT, t_bmm[c - 2])]
                    else:
                        wb1 += war_band1[(c + 1) % 3]
                    tb1 = S(lambda e, sb1=sb1, off1=off1, c=c: e.activation(
                        out=band1[(c + 1) % 3][64:128, 0:64],
                        in_=PS[64:128, sb1 * 512 + off1: sb1 * 512 + off1 + 64],
                        func=AF.Exp, scale=0.125), w=wb1)
                    t_b1w[c + 1] = tb1
                    sfree[sb1].append((sS, tb1))
                    wact.append((sS, tb1))
                nr = (n1 - n0) // 64
                r0 = n0 // 64
                wwg = [(sT, tsm), (sV, t_wTr)] + wact
                if c >= 2:
                    wwg.append((sT, t_pv[c - 2]))
                wwg += war_pt[c % 2]
                twg = V(lambda e, n0=n0, n1=n1, sb=sb, off=off, nr=nr, r0=r0,
                        c=c, hp=hp: e.tensor_tensor(
                    out=stripB[c % 2][:, n0:n1].rearrange("p (r z) -> p r z", z=64),
                    in0=PS[:, sb * 512 + off: sb * 512 + off + (n1 - n0)]
                        .rearrange("p (r z) -> p r z", z=64),
                    in1=wTh[hp][:, 32 * c + r0: 32 * c + r0 + nr]
                        .unsqueeze(2).broadcast_to([128, nr, 64]),
                    op=ALU.mult), w=wwg)
                t_wg[(c, Gi)] = twg
                sfree[sb].append((sV, twg))
            war_pt[c % 2] = []
            t_b2m[c] = V(lambda e, c=c: e.tensor_tensor(out=band2P[c % 2],
                                               in0=band2P[c % 2],
                                               in1=csb["tri01"],
                                               op=ALU.mult), w=[(sS, tb2)])
            if stop_stage == 51 and h == 0 and c == 0:
                return _finish_early(nc, P, [(sT, sT.n), (sV, sV.n), (sS, sS.n),
                                             (dS, dS.n), (dG, dG.n)])
            tpv = None
            for Qd in range(Gmin, 4):
                n0q, n1q = max(512 * Qd, 128 * c), 512 * (Qd + 1)
                qoff = n0q - 512 * Qd
                wpv = [(sV, t_wg[(c, Qd)])]
                if c == 0:
                    wpv += bwait(2 + Qd)
                    if t_comb_prev:
                        wpv.append((sV, t_comb_prev))
                vsl = v_tm[:, c * 1024 + h * 64: c * 1024 + (h + 1) * 64]
                tpv = T(lambda e, Qd=Qd, n0q=n0q, n1q=n1q, qoff=qoff, c=c,
                        vs=vsl, hb=hb, tp=tp: e.matmul(
                    PS[hb:hb + 64,
                       1024 + 512 * Qd + qoff: 1024 + 512 * Qd + qoff + (n1q - n0q)],
                    vs, stripB[c % 2][:, n0q:n1q],
                    start=(c == 0), stop=(c == 4 * Qd + 3), skip_group_check=True,
                    tile_position=tp), w=wpv, inc=(Qd == 3))
            t_pv[c] = tpv
            if stop_stage == 52 and h == 0 and c == 0:
                return _finish_early(nc, P, [(sT, sT.n), (sV, sV.n), (sS, sS.n),
                                             (dS, dS.n), (dG, dG.n)])
            emit_fin(c)

            if stop_stage == 23 + c and h == 0:
                return _finish_early(nc, P, [(sT, sT.n), (sV, sV.n), (sS, sS.n),
                                             (dS, dS.n), (dG, dG.n)])
        t_comb_prev = t_cmb[NCH - 1]
        war_pt[0] = [(sT, t_pv[NCH - 2])]
        war_pt[1] = [(sT, t_pv[NCH - 1])]
        war_wTh[hp] = [(sV, t_wg[(NCH - 1, 3)])]
        war_band1 = [[(sT, t_bmm[15])], [(sT, t_bmm[13])], [(sT, t_bmm[14])]]
        war_band2 = [[(sT, t_bmm[NCH - 2])], [(sT, t_bmm[NCH - 1])]]
        war_fin = [[(sV, t_rzev[NCH - 2]), (sV, t_cmbm[NCH - 2])],
                   [(sV, t_rzev[NCH - 1]), (sV, t_cmbm[NCH - 1])]]
        bank_free[6] = [(sV, t_cmbm[NCH - 1])]
        for qd in range(4):
            bank_free[2 + qd] = [(sV, t_cmb[NCH - 1])]
        for sb in (0, 1):
            bank_free[sb] = list(sfree[sb])
        if stop_stage == 3 and h == 0:
            return _finish_early(nc, P, [(sT, sT.n), (sV, sV.n), (sS, sS.n),
                                         (dS, dS.n), (dG, dG.n)])

    t_attn_T = sT.n
    if stop_stage <= 4:
        return _finish_early(nc, P, [(sT, sT.n), (sV, sV.n), (sS, sS.n),
                                     (dS, dS.n), (dG, dG.n)])
    t_attn_V = sV.n
    t_attn_S = sS.n

    Tdrain()
    w_att = [(sT, t_attn_T), (sV, t_attn_V), (sS, t_attn_S)]
    twc = None
    for ic in range(8):
        twc = SY(lambda e, ic=ic: e.dma_start(
            out=wcT[:, ic * 1024:(ic + 1) * 1024],
            in_=wbf[3][:, ic * 128:(ic + 1) * 128], transpose=True),
            w=[(dG, tw_wbf[3])] + w_att)
    t_wcT = twc
    t_b1 = G(lambda e: e.dma_start(
        out=b1row, in_=b1d.rearrange("e h -> (e h)").unsqueeze(0)), w=w_att)
    t_e8 = G(lambda e: e.dma_start(out=csb["e8"], in_=cin["e8"]), w=w_att)
    t_b2 = G(lambda e: e.dma_start(out=b2_sb, in_=b2d), w=w_att)

    t_w1use = {}
    t_w2use = {}
    t_hm = {0: 0, 1: 0}
    t_eo_hl = {}
    t_evac = {}
    t_odma = {}
    for s_ in range(4):
        for e_ in range(NE):
            eb = e_ % 2
            m = s_ * NE + e_
            wld = [(dG, t_w1bf[e_])] + w_att
            if m >= 2:
                wld.append((sT, t_w1use[m - 2]))
            tld = SY(lambda e, i=e_, eb=eb: e.dma_start(out=w1b[eb], in_=w1bf[i]),
                     w=wld)
            gcb = 2 + eb
            tgbc = T(lambda e, i=e_, s=s_, gcb=gcb: e.matmul(
                PS[:, gcb * 512:(gcb + 1) * 512], csb["e8"][:, i * 128:(i + 1) * 128],
                g_t8[0:8, s * 512:(s + 1) * 512], start=True, stop=True,
                skip_group_check=True),
                w=[(sV, t_ln_done), (dG, t_e8)] + bwait(gcb))
            tlast = None
            thm = None
            for hc in range(4):
                bk = (e_ * 4 + hc) % 2
                w0 = [(dS, tld)] + bwait(bk)
                for icc in range(8):
                    T(lambda e, icc=icc, hc=hc, eb=eb, s=s_, bk=bk: e.matmul(
                        PS[:, bk * 512:(bk + 1) * 512],
                        w1b[eb][:, icc * 512 + hc * 128: icc * 512 + (hc + 1) * 128],
                        xn_fm[:, icc * 2048 + s * 512: icc * 2048 + (s + 1) * 512],
                        start=(icc == 0), stop=False, skip_group_check=True),
                        w=w0, inc=False)
                tb1m = T(lambda e, i=e_, hc=hc, bk=bk: e.matmul(
                    PS[:, bk * 512:(bk + 1) * 512],
                    b1row[0:1, i * 512 + hc * 128: i * 512 + (hc + 1) * 128],
                    csb["onesrow"], start=False, stop=True, skip_group_check=True),
                    w=[(dG, t_b1)])
                tlast = tb1m
                wgl = [(sT, tb1m)]
                if t_hm[bk]:
                    wgl.append((sV, t_hm[bk]))
                tgl = S(lambda e, bk=bk: e.activation(
                    out=hgel[bk], in_=PS[:, bk * 512:(bk + 1) * 512], func=AF.Gelu),
                    w=wgl)
                bfree(bk, sS, tgl)
                whm = [(sS, tgl), (sT, tgbc)]
                if s_ > 0 and (s_ * NE + e_) in ():
                    pass
                if s_ > 0:
                    whm.append((sT, t_eo_hl[s_ - 1]))
                thm = V(lambda e, i=e_, hc=hc, bk=bk, gcb=gcb: e.tensor_tensor(
                    out=hprime[:, (i * 4 + hc) * 512:(i * 4 + hc + 1) * 512],
                    in0=hgel[bk], in1=PS[:, gcb * 512:(gcb + 1) * 512], op=ALU.mult),
                    w=whm)
                t_hm[bk] = thm
            bfree(gcb, sV, thm)
            t_w1use[m] = tlast
        treg = [0] * 8
        for e_ in range(NE):
            eb = e_ % 2
            m = s_ * NE + e_
            wld = [(dG, t_w2bf[e_])] + w_att
            if m >= 2:
                wld.append((sT, t_w2use[m - 2]))
            tld = SY(lambda e, i=e_, eb=eb: e.dma_start(out=w2b[eb], in_=w2bf[i]),
                     w=wld)
            tm = None
            for hc in range(4):
                for reg in range(8):
                    tc4, og = reg // 2, reg % 2
                    first = (e_ == 0 and hc == 0)
                    w0 = [(dS, tld)]
                    if first:
                        w0 += bwait(reg)
                    hsl = hprime[:, (e_ * 4 + hc) * 512 + tc4 * 128:
                                 (e_ * 4 + hc) * 512 + (tc4 + 1) * 128]
                    tm = T(lambda e, hs=hsl, eb=eb, hc=hc, og=og, reg=reg,
                           first=first: e.matmul(
                        PS[:, reg * 512:(reg + 1) * 512], hs,
                        w2b[eb][:, hc * 1024 + og * 512: hc * 1024 + (og + 1) * 512],
                        start=first, stop=False, skip_group_check=True),
                        w=w0, inc=(hc == 3 and reg == 7))
            t_w2use[m] = tm
        t_eo_hl[s_] = t_w2use[s_ * NE + NE - 1]
        for reg in range(8):
            tc4, og = reg // 2, reg % 2
            gtc = s_ * 4 + tc4
            T(lambda e, gtc=gtc, og=og, reg=reg: e.matmul(
                PS[:, reg * 512:(reg + 1) * 512],
                g_t8[0:8, gtc * 128:(gtc + 1) * 128],
                b2_sb[0:8, og * 512:(og + 1) * 512],
                start=False, stop=False, skip_group_check=True),
                w=[(dG, t_b2)], inc=False)
            for icc in range(8):
                treg[reg] = T(lambda e, icc=icc, gtc=gtc, og=og, reg=reg: e.matmul(
                    PS[:, reg * 512:(reg + 1) * 512],
                    y_fm[:, icc * 2048 + gtc * 128: icc * 2048 + (gtc + 1) * 128],
                    wcT[:, icc * 1024 + og * 512: icc * 1024 + (og + 1) * 512],
                    start=False, stop=(icc == 7), skip_group_check=True),
                    w=[(dS, t_wcT)], inc=(icc == 7))
        for tc4 in range(4):
            gtc = s_ * 4 + tc4
            pe = gtc % 2
            wxl = w_att + ([(sV, t_evac[gtc - 2])] if gtc >= 2 else [])
            txl = SY(lambda e, gtc=gtc, pe=pe: e.dma_start(
                out=xst2[pe], in_=x[gtc * 128:(gtc + 1) * 128, :]), w=wxl)
            tev = None
            for og in range(2):
                reg = tc4 * 2 + og
                wev = [(sT, treg[reg]), (dS, txl)]
                if gtc >= 2:
                    wev.append((sO, t_odma[gtc - 2]))
                tev = V(lambda e, pe=pe, og=og, reg=reg: e.tensor_tensor(
                    out=outst[pe][:, og * 512:(og + 1) * 512],
                    in0=PS[:, reg * 512:(reg + 1) * 512],
                    in1=xst2[pe][:, og * 512:(og + 1) * 512], op=ALU.add), w=wev)
                bfree(reg, sV, tev)
            t_evac[gtc] = tev
            t_odma[gtc] = SYo(lambda e, gtc=gtc, pe=pe: e.dma_start(
                out=out[gtc * 128:(gtc + 1) * 128, :], in_=outst[pe]),
                w=[(sV, tev)])

    if DEBUG:
        wd = [(sO, t_odma[NT - 1])]
        G(lambda e: e.dma_start(out=dbg["d_xn"], in_=xn_fm), w=wd)
        G(lambda e: e.dma_start(out=dbg["d_q"], in_=q_fm), w=wd)
        G(lambda e: e.dma_start(out=dbg["d_k"], in_=k_fm), w=wd)
        G(lambda e: e.dma_start(out=dbg["d_v"], in_=v_tm), w=wd)
        G(lambda e: e.dma_start(out=dbg["d_y"], in_=y_fm), w=wd)
        G(lambda e: e.dma_start(out=dbg["d_g"], in_=g_t8), w=wd)
        G(lambda e: e.dma_start(out=dbg["d_pcs"], in_=pcsS[1][0:32, :]), w=wd)
        G(lambda e: e.dma_start(out=dbg["d_wt"], in_=wTh[1]), w=wd)
        P.add("gpsimd", lambda e: e.nop(), w=[(dG, dG.n)])

    P.add("sync", lambda e: e.nop(), w=[(sO, 16 * NT)])

    with nc.Block() as block:
        @block.sync
        def _(e):
            P.replay("sync", e)

        @block.tensor
        def _(e):
            P.replay("tensor", e)

        @block.vector
        def _(e):
            P.replay("vector", e)

        @block.scalar
        def _(e):
            P.replay("scalar", e)

        @block.gpsimd
        def _(e):
            P.replay("gpsimd", e)

    return nc


_NC_CACHE = None


def kernel(**inputs):
    global _NC_CACHE
    xs = np.asarray(inputs["L0_attn_hidden_states"])
    base = {
        "Wq": np.ascontiguousarray(inputs["L0_attn_Wq"], np.float32),
        "Wk": np.ascontiguousarray(inputs["L0_attn_Wk"], np.float32),
        "Wv": np.ascontiguousarray(inputs["L0_attn_Wv"], np.float32),
        "Wc": np.ascontiguousarray(inputs["L0_attn_Wc"], np.float32),
        "terr": np.ascontiguousarray(inputs["L0_attn_terrace_mix"],
                                     np.float32).reshape(1, SEG),
        "idxd": np.ascontiguousarray(inputs["L0_attn_index_decays"], np.float32),
        "megad": np.ascontiguousarray(inputs["L0_attn_mega_decays"],
                                      np.float32).reshape(1, H),
        "w1": np.ascontiguousarray(inputs["L0_ffn_w1"], np.float32),
        "b1": np.ascontiguousarray(inputs["L0_ffn_b1"], np.float32),
        "w2": np.ascontiguousarray(inputs["L0_ffn_w2"], np.float32),
        "b2": np.ascontiguousarray(inputs["L0_ffn_b2"], np.float32),
        "cent": np.ascontiguousarray(inputs["L0_ffn_centroids"], np.float32),
        "wrt": np.ascontiguousarray(inputs["L0_ffn_w_route"], np.float32),
    }
    for k, v in _host_consts().items():
        base["c_" + k] = v
    if _NC_CACHE is None:
        _NC_CACHE = build()
    nc = _NC_CACHE
    in_maps = []
    for c in range(NCORE):
        m = dict(base)
        m["x"] = np.ascontiguousarray(xs[c], np.float32)
        in_maps.append(m)
    res = run_bass_kernel_spmd(nc, in_maps, core_ids=list(range(NCORE)))
    return np.stack([np.asarray(res.results[c]["out"]) for c in range(NCORE)],
                    axis=0)


# version 15 (final)
# speedup vs baseline: 1.6061x; 1.0358x over previous
import numpy as np
import ml_dtypes
import concourse.bass as bass
import concourse.mybir as mybir
from concourse.bass_utils import run_bass_kernel_spmd

F32 = mybir.dt.float32
BF16 = mybir.dt.bfloat16
U8 = mybir.dt.uint8
AF = mybir.ActivationFunctionType
ALU = mybir.AluOpType
AX = mybir.AxisListType.X

L, D, H, SEG = 2048, 1024, 16, 64
NT, NCH = 16, 16
NE = 8
NCORE = 8
LN32 = float(np.log(32.0))

DEBUG = False


def _host_consts():
    bf = ml_dtypes.bfloat16
    f = np.arange(128)
    jj, rr = np.meshgrid(np.arange(32), np.arange(32), indexing="ij")
    e8 = np.zeros((8, 1024), dtype=bf)
    for e in range(8):
        e8[e, e * 128:(e + 1) * 128] = 1
    return {
        "idnf32": np.eye(128, dtype=np.float32),
        "tri01": (f[None, :] >= f[:, None]).astype(bf),
        "strictu": (f[None, :] < f[:, None]).astype(bf),
        "negi": (-30000.0 * np.eye(128)).astype(bf),
        "rep2": (np.arange(128)[None, :] % 64 == np.arange(64)[:, None]).astype(np.float32),
        "emat": np.clip(rr - 1 - jj, 0, 31).astype(np.float32),
        "ltri": (jj < rr).astype(np.float32),
        "e8": e8,
        "onesrow": np.ones((1, 512), dtype=bf),
        "e16": (np.arange(128)[:, None] == (np.arange(1024)[None, :] // 64)).astype(bf),
        "ones64": np.ones((1, 64), dtype=bf),
        "onescol": np.ones((128, 1), dtype=bf),
        "ones32f": np.ones((1, 32), dtype=np.float32),
        "onesf128": np.ones((1, 128), dtype=np.float32),
    }


class Sem:
    __slots__ = ("h", "n")

    def __init__(self, h):
        self.h = h
        self.n = 0


class Tape:
    ENGS = ("sync", "tensor", "vector", "scalar", "gpsimd")

    def __init__(self):
        self.ops = {e: [] for e in self.ENGS}
        self.waited = {e: {} for e in self.ENGS}

    def add(self, eng, fn, w=(), inc=None, amt=1):
        ws = []
        for s, v in w:
            assert v is not None and v > 0, f"bad wait {v} on {eng}"
            k = id(s)
            if self.waited[eng].get(k, 0) >= v:
                continue
            self.waited[eng][k] = v
            ws.append((s.h, v))
        pair, tick = None, None
        if inc is not None:
            inc.n += amt
            tick = inc.n
            pair = (inc.h, amt)
        self.ops[eng].append((fn, tuple(ws), pair))
        return tick

    def replay(self, eng, e):
        for fn, ws, pair in self.ops[eng]:
            for h, v in ws:
                e.wait_ge(h, v)
            inst = fn(e)
            if pair is not None:
                inst.then_inc(pair[0], pair[1])


def _finish_early(nc, P, waits):
    P.add("sync", lambda e: e.nop(), w=[(s, v) for s, v in waits if v > 0])
    with nc.Block() as block:
        @block.sync
        def _(e):
            P.replay("sync", e)

        @block.tensor
        def _(e):
            P.replay("tensor", e)

        @block.vector
        def _(e):
            P.replay("vector", e)

        @block.scalar
        def _(e):
            P.replay("scalar", e)

        @block.gpsimd
        def _(e):
            P.replay("gpsimd", e)
    return nc


def build(stop_stage=99):
    import os
    nc = bass.Bass("TRN2", target_bir_lowering=False,
                   detect_race_conditions=False)
    nc.dram_tensor("salt" + os.environ.get("KSALT", "0"), [1, 1], F32,
                   kind="Internal")
    HC = _host_consts()

    x = nc.dram_tensor("x", [L, D], F32, kind="ExternalInput").ap()
    Wmats = [nc.dram_tensor(n, [D, D], F32, kind="ExternalInput").ap()
             for n in ("Wq", "Wk", "Wv", "Wc")]
    terr = nc.dram_tensor("terr", [1, SEG], F32, kind="ExternalInput").ap()
    idxd = nc.dram_tensor("idxd", [H, SEG], F32, kind="ExternalInput").ap()
    megad = nc.dram_tensor("megad", [1, H], F32, kind="ExternalInput").ap()
    w1d = nc.dram_tensor("w1", [NE, D, 512], F32, kind="ExternalInput").ap()
    b1d = nc.dram_tensor("b1", [NE, 512], F32, kind="ExternalInput").ap()
    w2d = nc.dram_tensor("w2", [NE, 512, D], F32, kind="ExternalInput").ap()
    b2d = nc.dram_tensor("b2", [NE, D], F32, kind="ExternalInput").ap()
    cent = nc.dram_tensor("cent", [NE, D], F32, kind="ExternalInput").ap()
    wrt = nc.dram_tensor("wrt", [NE, D], F32, kind="ExternalInput").ap()
    cin = {k: nc.dram_tensor("c_" + k, list(v.shape),
                             F32 if v.dtype == np.float32 else BF16,
                             kind="ExternalInput").ap()
           for k, v in HC.items()}
    out = nc.dram_tensor("out", [L, D], F32, kind="ExternalOutput").ap()
    wbf = [nc.dram_tensor(f"wbf{i}", [D, D], BF16, kind="Internal").ap()
           for i in range(4)]
    w1bf = nc.dram_tensor("w1bf", [NE, 128, 4096], BF16, kind="Internal").ap()
    w2bf = nc.dram_tensor("w2bf", [NE, 128, 4096], BF16, kind="Internal").ap()
    dbg = {}
    if DEBUG:
        for n, shp, dt in [("d_xn", [128, 16384], BF16), ("d_q", [128, 16384], BF16),
                           ("d_k", [128, 16384], BF16), ("d_v", [128, 16384], BF16),
                           ("d_y", [128, 16384], BF16), ("d_g", [8, 2048], BF16),
                           ("d_pcs", [32, 2048], BF16), ("d_wt", [128, 512], F32)]:
            dbg[n] = nc.dram_tensor(n, shp, dt, kind="ExternalOutput").ap()

    ar_xn = nc.alloc_sbuf_tensor("ar_xn", [128, 32768], U8).ap()
    ar_q = nc.alloc_sbuf_tensor("ar_q", [128, 32768], U8).ap()
    ar_k = nc.alloc_sbuf_tensor("ar_k", [128, 32768], U8).ap()
    ar_v = nc.alloc_sbuf_tensor("ar_v", [128, 32768], U8).ap()
    ar_y = nc.alloc_sbuf_tensor("ar_y", [128, 32768], U8).ap()
    ar_t = nc.alloc_sbuf_tensor("ar_t", [128, 28672], U8).ap()

    xn_fm = ar_xn.bitcast(BF16)
    q_fm = ar_q.bitcast(BF16)
    k_fm = ar_k.bitcast(BF16)
    v_tm = ar_v.bitcast(BF16)
    y_fm = ar_y.bitcast(BF16)
    wTb = [ar_y[:, 0:16384].bitcast(BF16), ar_y[:, 16384:32768].bitcast(BF16)]
    hprime = ar_q.bitcast(BF16)
    wcT = ar_k[:, 0:16384].bitcast(BF16)
    xst2 = [ar_k[:, 16384:20480].bitcast(F32), ar_k[:, 20480:24576].bitcast(F32)]
    outst = [ar_k[:, 24576:28672].bitcast(F32), ar_k[:, 28672:32768].bitcast(F32)]
    w1b = [ar_v[:, 0:8192].bitcast(BF16), ar_v[:, 8192:16384].bitcast(BF16)]
    w2b = [ar_v[:, 16384:24576].bitcast(BF16), ar_v[:, 24576:32768].bitcast(BF16)]

    xstage = [ar_t[:, 0:4096].bitcast(F32), ar_t[:, 4096:8192].bitcast(F32)]
    xn_st = [ar_t[:, 8192:12288].bitcast(F32), ar_t[:, 12288:16384].bitcast(F32)]
    sqdump = ar_t[:, 16384:20480].bitcast(F32)
    lnS = ar_t[:, 20480:24576].bitcast(F32)
    prepS = ar_t[:, 24576:28672].bitcast(F32)
    Ptm = [ar_t[:, 0:4096].bitcast(BF16), ar_t[:, 4096:8192].bitcast(BF16)]
    stripB = Ptm
    band1 = [ar_t[:, 8192:8448].bitcast(BF16), ar_t[:, 8448:8704].bitcast(BF16),
             None]
    band2P = [nc.alloc_sbuf_tensor("band2", [128, 128], BF16).ap(),
              ar_t[:, 8704:8960].bitcast(BF16)]
    pcsS = [ar_t[:, 8960:13056].bitcast(BF16), ar_t[:, 13056:17152].bitcast(BF16)]
    wTh = [ar_t[:, 17152:19200].bitcast(F32), ar_t[:, 19200:21248].bitcast(F32)]
    atS = ar_t[:, 21248:25344].bitcast(F32)
    hgel = [ar_t[:, 0:2048].bitcast(F32), ar_t[:, 2048:4096].bitcast(F32)]
    b1row = ar_t[0:1, 4096:12288].bitcast(BF16)
    b2_sb = ar_t[0:8, 12288:14336].bitcast(BF16)

    csb = {k: nc.alloc_sbuf_tensor("s_" + k, list(HC[k].shape),
                                   F32 if HC[k].dtype == np.float32 else BF16).ap()
           for k in HC if k != "e8"}
    csb["e8"] = ar_t[0:8, 14336:16384].bitcast(BF16)
    g_t8 = nc.alloc_sbuf_tensor("g_t8", [8, 2048], BF16).ap()
    twrow = nc.alloc_sbuf_tensor("twrow", [1, 128], F32).ap()
    owrep2 = nc.alloc_sbuf_tensor("owrep2", [128, 128], F32).ap()
    crep = nc.alloc_sbuf_tensor("crep", [128, 16], F32).ap()
    lnmega = nc.alloc_sbuf_tensor("lnmega", [1, 16], F32).ap()
    cnrb = nc.alloc_sbuf_tensor("cnrb", [128, 128], BF16).ap()
    centF = nc.alloc_sbuf_tensor("centF", [8, 1024], F32).ap()
    routeF = nc.alloc_sbuf_tensor("routeF", [8, 1024], F32).ap()

    msum, sq2 = lnS[:, 0:2], lnS[:, 2:4]
    mmean, varv = lnS[:, 4:6], lnS[:, 6:8]
    rstd, negm = lnS[:, 8:10], lnS[:, 10:12]
    recn, tvar = lnS[:, 12:14], lnS[:, 24:26]
    m1c, m2c, nm1 = lnS[:, 14:16], lnS[:, 16:18], lnS[:, 18:20]
    gsum, grec = lnS[:, 20:22], lnS[:, 22:24]
    lgec = [lnS[:, 32:48], lnS[:, 48:64]]
    lgf = [lnS[:, 64:72], lnS[:, 72:80]]
    ge1 = [lnS[:, 96:104], lnS[:, 104:112]]
    mskd = [lnS[:, 112:120], lnS[:, 120:128]]
    kept = [lnS[:, 128:136], lnS[:, 136:144]]
    egk = [lnS[:, 144:152], lnS[:, 152:160]]
    gtm = [lnS[:, 160:168], lnS[:, 168:176]]

    cn2, lncn, rcn = prepS[0:8, 0:1], prepS[0:8, 1:2], prepS[0:8, 2:3]
    idxF = prepS[0:16, 8:72]
    c16 = prepS[0:16, 72:136]
    callT = prepS[0:64, 136:152]
    megae, megac = prepS[0:1, 152:168], prepS[0:1, 168:184]
    twe, twp, tw64 = prepS[0:1, 184:248], prepS[0:1, 248:312], prepS[0:1, 312:376]
    ow64 = prepS[0:1, 376:440]
    owrow = prepS[0:1, 440:568]
    sq8b = prepS[0:8, 504:1016]

    zparts = atS[:, 0:8]
    Zc, rZc = atS[:, 8:10], atS[:, 10:12]
    EA, ATf = atS[0:32, 16:48], atS[0:32, 48:80]
    rnb32 = atS[0:32, 80:112]
    lnm32, Pcol = atS[0:32, 112:113], atS[0:32, 113:114]
    rn0, rnorm = atS[0:32, 114:115], atS[0:32, 115:116]
    rnrow = atS[0:1, 116:148]
    ztAll = atS[:, 148:164]
    ztT = atS[0:16, 164:292]
    ztTb = atS[:, 292:356].bitcast(BF16)
    twcol = atS[:, 356:357]
    rztw_sbP = [atS[:, 404:532], prepS[:, 320:448]]
    tcombP = [atS[:, 532:660], prepS[:, 448:576]]
    t2combP = [atS[:, 660:788], prepS[:, 576:704]]
    ATb = prepS[0:32, 708:724].bitcast(BF16)
    A2b = prepS[0:32, 724:740].bitcast(BF16)
    Pcolb = prepS[0:32, 704:705].bitcast(BF16)
    rztwbP = [prepS[0:1, 740:804].bitcast(BF16),
              ar_t[0:1, 25600:25856].bitcast(BF16)]
    cvecP = [prepS[:, 804:820].bitcast(BF16), prepS[:, 820:836].bitcast(BF16)]
    band1[2] = prepS[:, 836:900].bitcast(BF16)

    PS = nc.alloc_psum_tensor("PS", [128, 4096], F32).ap()

    sT, sV, sS = Sem(nc.alloc_semaphore("sT")), Sem(nc.alloc_semaphore("sV")), \
        Sem(nc.alloc_semaphore("sS"))
    dS, dG, sO = Sem(nc.alloc_semaphore("dS")), Sem(nc.alloc_semaphore("dG")), \
        Sem(nc.alloc_semaphore("sO"))

    P = Tape()

    def T(fn, w=(), inc=True):
        return P.add("tensor", fn, w, sT if inc else None)

    def V(fn, w=(), inc=True):
        return P.add("vector", fn, w, sV if inc else None)

    def S(fn, w=(), inc=True):
        return P.add("scalar", fn, w, sS if inc else None)

    def SY(fn, w=()):
        return P.add("sync", fn, w, dS, 16)

    def SYo(fn, w=()):
        return P.add("sync", fn, w, sO, 16)

    def G(fn, w=()):
        return P.add("gpsimd", fn, w, dG, 16)

    def Tdrain():
        P.add("tensor", lambda e: e.nop(), w=[(sT, sT.n)] if sT.n else [])

    bank_free = {b: [] for b in range(8)}

    def bfree(b, sem, tick):
        bank_free[b] = [(sem, tick)]

    def bwait(*banks):
        ws = []
        for b in banks:
            ws += bank_free[b]
        return ws

    for k in HC:
        if k == "e8":
            continue
        SY(lambda e, d=csb[k], s=cin[k]: e.dma_start(out=d, in_=s))
    SY(lambda e: e.dma_start(out=idxF, in_=idxd))
    SY(lambda e: e.dma_start(out=megae, in_=megad))
    SY(lambda e: e.dma_start(out=twe, in_=terr))
    SY(lambda e: e.dma_start(out=centF, in_=cent))
    SY(lambda e: e.dma_start(out=routeF, in_=wrt))
    t_prein = dS.n
    tw_wbf = [G(lambda e, d=wbf[i], s=Wmats[i]: e.dma_start(out=d, in_=s))
              for i in range(4)]
    t_w1bf, t_w2bf = [], []
    for e_ in range(NE):
        t_w1bf.append(G(lambda e, i=e_: e.dma_start(
            out=w1bf[i], in_=w1d[i].rearrange("(ic p) h -> p ic h", p=128))))
        t_w2bf.append(G(lambda e, i=e_: e.dma_start(
            out=w2bf[i], in_=w2d[i].rearrange("(hc p) o -> p hc o", p=128))))

    wp = [(dS, t_prein)]
    a1 = S(lambda e: e.activation(out=sq8b, in_=centF[:, 0:512], func=AF.Square,
                                  accum_out=cn2), w=wp)
    a2 = S(lambda e: e.activation(out=sq8b, in_=centF[:, 512:1024], func=AF.Square,
                                  accum_out=lncn), w=[(sS, a1)])
    a3 = V(lambda e: e.tensor_tensor(out=cn2, in0=cn2, in1=lncn, op=ALU.add),
           w=[(sS, a2)])
    a4 = S(lambda e: e.activation(out=lncn, in_=cn2, func=AF.Ln), w=[(sV, a3)])
    a5 = S(lambda e: e.activation(out=rcn, in_=lncn, func=AF.Exp, scale=-0.5),
           w=[(sS, a4)])
    a6 = V(lambda e: e.tensor_scalar(out=centF, in0=centF, scalar1=rcn,
                                     scalar2=1.0 / 32.0, op0=ALU.mult,
                                     op1=ALU.mult), w=[(sS, a5)])
    tt = None
    for ic in range(8):
        T(lambda e, ic=ic: e.transpose(
            PS[:, 3584 + ic * 16:3592 + ic * 16],
            centF[0:8, ic * 128:(ic + 1) * 128], csb["idnf32"][0:8, 0:8]),
            w=[(sV, a6)], inc=False)
        tt = T(lambda e, ic=ic: e.transpose(
            PS[:, 3592 + ic * 16:3600 + ic * 16],
            routeF[0:8, ic * 128:(ic + 1) * 128], csb["idnf32"][0:8, 0:8]),
            w=[(dS, t_prein)], inc=(ic == 7))
    a7 = V(lambda e: e.tensor_copy(out=cnrb, in_=PS[:, 3584:3712]), w=[(sT, tt)])
    bfree(7, sV, a7)
    b1_ = S(lambda e: e.activation(out=c16, in_=idxF, func=AF.Exp), w=wp)
    b2_ = T(lambda e: e.transpose(PS[0:64, 3072:3088], c16[0:16, 0:64],
                                  csb["idnf32"][0:16, 0:16]), w=[(sS, b1_)])
    b3 = V(lambda e: e.tensor_copy(out=callT, in_=PS[0:64, 3072:3088]), w=[(sT, b2_)])
    b4 = T(lambda e: e.matmul(PS[:, 3088:3104], csb["rep2"], callT,
                              start=True, stop=True), w=[(sV, b3)])
    t_crep = V(lambda e: e.tensor_copy(out=crep, in_=PS[:, 3088:3104]), w=[(sT, b4)])
    c1_ = S(lambda e: e.activation(out=megac, in_=megae, func=AF.Exp, scale=64.0), w=wp)
    c2_ = V(lambda e: e.tensor_scalar_min(out=megac, in0=megac, scalar1=2.0),
            w=[(sS, c1_)])
    t_lnmega = S(lambda e: e.activation(out=lnmega, in_=megac, func=AF.Ln),
                 w=[(sV, c2_)])
    d1 = S(lambda e: e.activation(out=twp, in_=twe, func=AF.Exp, scale=-1.0), w=wp)
    d2 = V(lambda e: e.tensor_scalar_add(out=twp, in0=twp, scalar1=1.0), w=[(sS, d1)])
    d3 = V(lambda e: e.reciprocal(out=tw64, in_=twp), w=[(sV, d2)])
    d4 = V(lambda e: e.tensor_copy(out=twrow[0:1, 0:64], in_=tw64), w=[(sV, d3)])
    t_twrow = V(lambda e: e.tensor_copy(out=twrow[0:1, 64:128], in_=tw64),
                w=[(sV, d3)])
    d6 = V(lambda e: e.tensor_scalar(out=ow64, in0=tw64, scalar1=-1.0, scalar2=1.0,
                                     op0=ALU.mult, op1=ALU.add), w=[(sV, d3)])
    d7 = V(lambda e: e.tensor_copy(out=owrow[0:1, 0:64], in_=ow64), w=[(sV, d6)])
    d8 = V(lambda e: e.tensor_copy(out=owrow[0:1, 64:128], in_=ow64), w=[(sV, d6)])
    d9 = T(lambda e: e.matmul(PS[:, 3104:3232], csb["onesf128"], owrow,
                              start=True, stop=True), w=[(sV, d7), (sV, d8)])
    t_owrep = V(lambda e: e.tensor_copy(out=owrep2, in_=PS[:, 3104:3232]),
                w=[(sT, d9)])
    tD = G(lambda e: e.dma_start(out=twcol, in_=terr.broadcast_to([2, 64])))
    e1_ = S(lambda e: e.activation(out=twcol, in_=twcol, func=AF.Exp, scale=-1.0),
            w=[(dG, tD)])
    e2_ = V(lambda e: e.tensor_scalar_add(out=twcol, in0=twcol, scalar1=1.0),
            w=[(sS, e1_)])
    t_twcol = V(lambda e: e.reciprocal(out=twcol, in_=twcol), w=[(sV, e2_)])
    bfree(6, sV, t_owrep)

    t_xn = [0] * NT
    t_lnS = [0] * NT
    t_lnV = [0] * NT
    t_xtr = [0] * NT
    for tc in range(NT):
        par = tc % 2
        w_x = [(sS, t_lnS[tc - 2]), (sV, t_lnV[tc - 2])] if tc >= 2 else []
        tx = SY(lambda e, tc=tc, p=par: e.dma_start(
            out=xstage[p], in_=x[tc * 128:(tc + 1) * 128, :]), w=w_x)
        wx = [(dS, tx)]
        va = V(lambda e, p=par: e.reduce_sum(out=msum[:, p:p + 1], in_=xstage[p],
                                             axis=AX), w=wx)
        sa = S(lambda e, p=par: e.activation(out=sqdump, in_=xstage[p],
                                             func=AF.Square,
                                             accum_out=sq2[:, p:p + 1]), w=wx)
        vb = V(lambda e, p=par: e.tensor_scalar_mul(
            out=mmean[:, p:p + 1], in0=msum[:, p:p + 1], scalar1=-1.0 / D),
            w=[(sV, va)])
        vc = V(lambda e, p=par: e.tensor_scalar(
            out=tvar[:, p:p + 1], in0=mmean[:, p:p + 1], scalar1=mmean[:, p:p + 1],
            scalar2=None, op0=ALU.mult), w=[(sV, vb)])
        vd = V(lambda e, p=par: e.tensor_scalar_mul(
            out=varv[:, p:p + 1], in0=sq2[:, p:p + 1], scalar1=1.0 / D),
            w=[(sS, sa), (sV, vc)])
        ve = V(lambda e, p=par: e.tensor_tensor(
            out=varv[:, p:p + 1], in0=varv[:, p:p + 1], in1=tvar[:, p:p + 1],
            op=ALU.subtract), w=[(sV, vd)])
        vf = V(lambda e, p=par: e.tensor_scalar_add(
            out=tvar[:, p:p + 1], in0=varv[:, p:p + 1], scalar1=1e-5), w=[(sV, ve)])
        sb_ = S(lambda e, p=par: e.activation(out=rstd[:, p:p + 1],
                                              in_=tvar[:, p:p + 1], func=AF.Ln),
                w=[(sV, vf)])
        sc_ = S(lambda e, p=par: e.activation(out=rstd[:, p:p + 1],
                                              in_=rstd[:, p:p + 1], func=AF.Exp,
                                              scale=-0.5), w=[(sS, sb_)])
        vh = V(lambda e, p=par: e.tensor_scalar(
            out=recn[:, p:p + 1], in0=varv[:, p:p + 1], scalar1=rstd[:, p:p + 1],
            scalar2=rstd[:, p:p + 1], op0=ALU.mult, op1=ALU.mult), w=[(sS, sc_)])
        si = S(lambda e, p=par: e.activation(out=recn[:, p:p + 1],
                                             in_=recn[:, p:p + 1], func=AF.Ln),
               w=[(sV, vh)])
        sj = S(lambda e, p=par: e.activation(out=recn[:, p:p + 1],
                                             in_=recn[:, p:p + 1], func=AF.Exp,
                                             scale=-0.5), w=[(sS, si)])
        w_xn = [(sS, sc_)] + ([(sT, t_xtr[tc - 2])] if tc >= 2 else [])
        sk = V(lambda e, p=par: e.tensor_scalar(
            out=xn_st[p], in0=xstage[p], scalar1=mmean[:, p:p + 1],
            scalar2=rstd[:, p:p + 1], op0=ALU.add, op1=ALU.mult), w=w_xn)
        t_lnS[tc] = sa
        base = par * 1024
        wtr = [(sV, sk)] + (bwait(par * 2, par * 2 + 1) if tc >= 2 else [])
        ttr = None
        for ic in range(8):
            ttr = T(lambda e, p=par, ic=ic, b=base: e.transpose(
                PS[:, b + ic * 128: b + (ic + 1) * 128],
                xn_st[p][:, ic * 128:(ic + 1) * 128], csb["idnf32"]),
                w=wtr, inc=(ic == 7))
        t_xtr[tc] = ttr
        vk = V(lambda e, tc=tc, b=base: e.tensor_copy(
            out=xn_fm.rearrange("p (ic t) -> p ic t", ic=8)[:, :, tc * 128:(tc + 1) * 128],
            in_=PS[:, b:b + 1024].rearrange("p (ic j) -> p ic j", ic=8)),
            w=[(sT, ttr)])
        t_xn[tc] = vk
        bfree(par * 2, sV, vk)
        bfree(par * 2 + 1, sV, vk)
        rb = 2048 + par * 512
        wr = [(sV, vk), (sV, a7)] + (bwait(4 + par) if tc >= 2 else [])
        trm = None
        for ic in range(8):
            trm = T(lambda e, ic=ic, tc=tc, rb=rb: e.matmul(
                PS[:, rb:rb + 16],
                xn_fm[:, ic * 2048 + tc * 128: ic * 2048 + (tc + 1) * 128],
                cnrb[:, ic * 16:(ic + 1) * 16],
                start=(ic == 0), stop=(ic == 7), skip_group_check=True),
                w=wr, inc=(ic == 7))
        r1 = V(lambda e, p=par, rb=rb: e.tensor_copy(out=lgec[p], in_=PS[:, rb:rb + 16]),
               w=[(sT, trm)])
        bfree(4 + par, sV, r1)
        r2 = V(lambda e, p=par: e.scalar_tensor_tensor(
            out=lgf[p], in0=lgec[p][:, 0:8], scalar=recn[:, p:p + 1],
            in1=lgec[p][:, 8:16], op0=ALU.mult, op1=ALU.add),
            w=[(sV, r1), (sS, sj)])
        r3 = V(lambda e, p=par: e.reduce_max(out=m1c[:, p:p + 1], in_=lgf[p], axis=AX),
               w=[(sV, r2)])
        r4 = V(lambda e, p=par: e.tensor_scalar(
            out=ge1[p], in0=lgf[p], scalar1=m1c[:, p:p + 1], scalar2=None,
            op0=ALU.is_ge), w=[(sV, r3)])
        r5 = V(lambda e, p=par: e.scalar_tensor_tensor(
            out=mskd[p], in0=ge1[p], scalar=-1e30, in1=lgf[p],
            op0=ALU.mult, op1=ALU.add), w=[(sV, r4)])
        r6 = V(lambda e, p=par: e.reduce_max(out=m2c[:, p:p + 1], in_=mskd[p], axis=AX),
               w=[(sV, r5)])
        r7 = V(lambda e, p=par: e.tensor_scalar(
            out=kept[p], in0=lgf[p], scalar1=m2c[:, p:p + 1], scalar2=None,
            op0=ALU.is_ge), w=[(sV, r6)])
        r8 = V(lambda e, p=par: e.tensor_scalar_mul(
            out=nm1[:, p:p + 1], in0=m1c[:, p:p + 1], scalar1=-1.0), w=[(sV, r3)])
        r9 = S(lambda e, p=par: e.activation(out=egk[p], in_=lgf[p], func=AF.Exp,
                                             bias=nm1[:, p:p + 1]), w=[(sV, r8)])
        r10 = V(lambda e, p=par: e.tensor_tensor(out=egk[p], in0=egk[p], in1=kept[p],
                                                 op=ALU.mult), w=[(sS, r9), (sV, r7)])
        r11 = V(lambda e, p=par: e.reduce_sum(out=gsum[:, p:p + 1], in_=egk[p], axis=AX),
                w=[(sV, r10)])
        r12 = V(lambda e, p=par: e.reciprocal(out=grec[:, p:p + 1],
                                              in_=gsum[:, p:p + 1]), w=[(sV, r11)])
        r13 = V(lambda e, p=par: e.tensor_scalar_mul(
            out=gtm[p], in0=egk[p], scalar1=grec[:, p:p + 1]), w=[(sV, r12)])
        gb = 3072 + par * 128
        r14 = T(lambda e, p=par, gb=gb: e.transpose(
            PS[0:8, gb:gb + 128], gtm[p], csb["idnf32"]), w=[(sV, r13)] + bwait(6))
        r15 = V(lambda e, tc=tc, gb=gb: e.tensor_copy(
            out=g_t8[0:8, tc * 128:(tc + 1) * 128], in_=PS[0:8, gb:gb + 128]),
            w=[(sT, r14)])
        t_lnV[tc] = r15
        bfree(6, sV, r15)
    t_ln_done = t_lnV[NT - 1]
    t_xn_all = t_xn[NT - 1]
    if stop_stage <= 1:
        return _finish_early(nc, P, [(sT, sT.n), (sV, sV.n), (sS, sS.n),
                                     (dS, dS.n), (dG, dG.n)])

    t_wT = [0, 0, 0]
    for m in range(2):
        tk = None
        for ic in range(8):
            tk = SY(lambda e, m=m, ic=ic: e.dma_start(
                out=wTb[m][:, ic * 1024:(ic + 1) * 1024],
                in_=wbf[m][:, ic * 128:(ic + 1) * 128], transpose=True),
                w=[(dG, tw_wbf[m])])
        t_wT[m] = tk

    def mm_proj(dst_fm, wsrc, t_wready):
        tick_last = None
        for oc in range(8):
            for tg in range(4):
                bk = (oc * 4 + tg) % 2
                w0 = [(dS, t_wready), (sV, t_xn_all)] + bwait(bk)
                tm = None
                for ic in range(8):
                    tm = T(lambda e, oc=oc, tg=tg, ic=ic, bk=bk, ws=wsrc: e.matmul(
                        PS[:, bk * 512:(bk + 1) * 512],
                        ws[:, ic * 1024 + oc * 128: ic * 1024 + (oc + 1) * 128],
                        xn_fm[:, ic * 2048 + tg * 512: ic * 2048 + (tg + 1) * 512],
                        start=(ic == 0), stop=(ic == 7), skip_group_check=True),
                        w=w0, inc=(ic == 7))
                tv_ = V(lambda e, oc=oc, tg=tg, bk=bk, d=dst_fm: e.tensor_copy(
                    out=d[:, oc * 2048 + tg * 512: oc * 2048 + (tg + 1) * 512],
                    in_=PS[:, bk * 512:(bk + 1) * 512]), w=[(sT, tm)])
                bfree(bk, sV, tv_)
                tick_last = tm
        return tick_last

    t_qmm = mm_proj(q_fm, wTb[0], t_wT[0])
    mm_proj(k_fm, wTb[1], t_wT[1])
    tk = None
    for ic in range(8):
        tk = SY(lambda e, ic=ic: e.dma_start(
            out=wTb[0][:, ic * 1024:(ic + 1) * 1024],
            in_=wbf[2][:, ic * 128:(ic + 1) * 128], transpose=True),
            w=[(dG, tw_wbf[2]), (sT, t_qmm)])
    t_wT[2] = tk
    for tc in range(NT):
        for og in range(2):
            bk = (tc * 2 + og) % 2
            w0 = [(dS, t_wT[2])] + bwait(bk)
            tm = None
            for ic in range(8):
                tm = T(lambda e, tc=tc, og=og, ic=ic, bk=bk: e.matmul(
                    PS[:, bk * 512:(bk + 1) * 512],
                    xn_fm[:, ic * 2048 + tc * 128: ic * 2048 + (tc + 1) * 128],
                    wTb[0][:, ic * 1024 + og * 512: ic * 1024 + (og + 1) * 512],
                    start=(ic == 0), stop=(ic == 7), skip_group_check=True),
                    w=w0, inc=(ic == 7))
            tv_ = V(lambda e, tc=tc, og=og, bk=bk: e.tensor_copy(
                out=v_tm[:, tc * 1024 + og * 512: tc * 1024 + (og + 1) * 512],
                in_=PS[:, bk * 512:(bk + 1) * 512]), w=[(sT, tm)])
            bfree(bk, sV, tv_)
    t_qkv_T = sT.n
    if stop_stage <= 2:
        return _finish_early(nc, P, [(sT, sT.n), (sV, sV.n), (sS, sS.n),
                                     (dS, dS.n), (dG, dG.n)])

    war_pt = [[], []]
    war_cvec = []
    war_pcs = [[], []]
    war_wTh = [[], []]
    war_band1 = [[], []]
    war_band2 = [[], []]
    war_fin = [[], []]
    t_comb_prev = 0

    tbz0 = V(lambda e: e.memset(band1[0], 0.0), w=[(sT, t_qkv_T)])
    tbz1 = V(lambda e: e.memset(band1[1], 0.0), w=[(sT, t_qkv_T)])
    tbz2 = V(lambda e: e.memset(band1[2], 0.0), w=[(sT, t_qkv_T)])
    V(lambda e: e.memset(ztTb, 0.0), w=[(sT, t_qkv_T)], inc=False)
    war_band1 = [[(sV, tbz0)], [(sV, tbz1)], [(sV, tbz2)]]

    for h in range(H):
        hp = h % 2
        hb = hp * 64
        icb = (h // 2) * 2048
        tp = (0, 64) if hp else None
        qsl_all = q_fm[hb:hb + 64, icb:icb + 2048]
        ksl_all = k_fm[hb:hb + 64, icb:icb + 2048]

        import os as _os2
        _kmin = _os2.environ.get("KMIN", "")
        if _kmin:
            t_wTr = V(lambda e, hp=hp: e.memset(wTh[hp], 0.001),
                      w=[(sT, t_qkv_T)])
            t_pcs = [sT.n if sT.n else 1] * NT
            t_pcsev = t_wTr
            twt = sT.n if sT.n else 1
        _kmin_noA = _kmin in ("1", "3")
        _kmin_noScan = _kmin in ("1", "2")
        Tdrain()
        if _kmin_noA if _kmin else False:
            tcv = None
        else:
            V(lambda e: e.memset(cvecP[0], 0.0), w=war_cvec, inc=False)
            tcv = V(lambda e: e.memset(cvecP[1], 0.0), w=war_cvec)
        t_pcs = [0] * NT
        t_exp = {}
        t_cv = [0] * NT
        t_ztred = [0] * NT

        def emit_pcs(i):
            W = 128 * (i + 1)
            ngrp = (W + 511) // 512
            Tdrain()
            tpc = None
            for Gi in range(ngrp):
                n0, n1 = 512 * Gi, min(W, 512 * Gi + 512)
                first = (i == 4 * Gi)
                wpc = [(sV, t_cv[i]), (sS, t_exp[(i, Gi)])]
                if first and n1 < 512 * (Gi + 1):
                    tz = V(lambda e, p=i % 2, n1=n1, Gi=Gi: e.memset(
                        Ptm[p][:, n1:512 * (Gi + 1)], 0.0),
                        w=[(sT, t_pcs[i - 2])] if i >= 2 else [])
                    wpc.append((sV, tz))
                    n1 = 512 * (Gi + 1)
                if first:
                    wpc += bwait(4 + Gi) + war_pcs[hp]
                tpc = T(lambda e, n0=n0, n1=n1, first=first, i=i: e.matmul(
                    PS[0:32, 2048 + n0: 2048 + n1], cvecP[i % 2],
                    Ptm[i % 2][:, n0:n1],
                    start=first, stop=(i == NT - 1), skip_group_check=True),
                    w=wpc, inc=(Gi == ngrp - 1))
            t_pcs[i] = tpc

        for i in ([] if (_kmin and _kmin_noA) else range(NT)):
            W = 128 * (i + 1)
            par = i % 2
            ngrp = (W + 511) // 512
            qsl = qsl_all[:, i * 128:(i + 1) * 128]
            tsm_l = {}
            for Gi in range(ngrp):
                n0, n1 = 512 * Gi, min(W, 512 * Gi + 512)
                w0 = bwait(Gi)
                if h == 0 and i == 0 and Gi == 0:
                    w0 = w0 + [(sT, t_qkv_T)]
                lastg = (Gi == ngrp - 1)
                if not lastg:
                    tsm = T(lambda e, n0=n0, n1=n1, q=qsl, ks=ksl_all: e.matmul(
                        PS[:, n0:n1], q, ks[:, n0:n1],
                        start=True, stop=True, skip_group_check=True), w=w0)
                else:
                    T(lambda e, n0=n0, n1=n1, q=qsl, ks=ksl_all: e.matmul(
                        PS[:, n0:n1], q, ks[:, n0:n1],
                        start=True, stop=False, skip_group_check=True),
                        w=w0, inc=False)
                    tsm = T(lambda e, i=i: e.matmul(
                        PS[:, i * 128:(i + 1) * 128], csb["strictu"], csb["negi"],
                        start=False, stop=True, skip_group_check=True))
                tsm_l[Gi] = tsm
            if i >= 1:
                emit_pcs(i - 1)
            for Gi in range(ngrp):
                n0, n1 = 512 * Gi, min(W, 512 * Gi + 512)
                wx = [(sT, tsm_l[Gi])]
                if i >= 2:
                    wx.append((sT, t_pcs[i - 2]))
                wx += war_pt[par]
                te = S(lambda e, p=par, n0=n0, n1=n1, g=Gi: e.activation(
                    out=Ptm[p][:, n0:n1], in_=PS[:, n0:n1], func=AF.Exp,
                    scale=0.125, accum_out=zparts[:, p * 4 + g: p * 4 + g + 1]),
                    w=wx)
                t_exp[(i, Gi)] = te
                bfree(Gi, sS, te)
            war_pt[par] = []
            z1 = V(lambda e, p=par, ng=ngrp: e.reduce_sum(
                out=Zc[:, p:p + 1], in_=zparts[:, p * 4: p * 4 + ng], axis=AX),
                w=[(sS, t_exp[(i, ngrp - 1)])])
            z2 = V(lambda e, p=par: e.reciprocal(out=rZc[:, p:p + 1],
                                                 in_=Zc[:, p:p + 1]), w=[(sV, z1)])
            wc = [(sV, z2), (sV, tcv), (sV, t_crep)]
            if i >= 2:
                V(lambda e, i=i, p=par: e.memset(
                    cvecP[p][:, 2 * i - 4: 2 * i - 2], 0.0),
                    w=[(sT, t_pcs[i - 2])], inc=False)
            cv1 = V(lambda e, i=i, h=h, p=par: e.tensor_tensor(
                out=cvecP[p][0:64, 2 * i: 2 * i + 1], in0=crep[0:64, h:h + 1],
                in1=rZc[0:64, p:p + 1], op=ALU.mult), w=wc)
            cv2 = V(lambda e, i=i, h=h, p=par: e.tensor_tensor(
                out=cvecP[p][64:128, 2 * i + 1: 2 * i + 2], in0=crep[64:128, h:h + 1],
                in1=rZc[64:128, p:p + 1], op=ALU.mult), w=wc)
            t_cv[i] = cv2
            c0a, c1a = max(0, 128 * i - 64), 128 * i + 64
            c0b, c1b = 128 * i, W
            wzr = [(sS, t_exp[(i, g)]) for g in
                   sorted({c0a // 512, (c1a - 1) // 512, c0b // 512, (c1b - 1) // 512})]
            if h > 0:
                wzr.append((sT, t_ztT_read))
            V(lambda e, i=i, p=par, c0a=c0a, c1a=c1a: e.reduce_sum(
                out=ztAll[0:64, i:i + 1], in_=Ptm[p][0:64, c0a:c1a], axis=AX),
                w=wzr, inc=False)
            tzr = V(lambda e, i=i, p=par, c0b=c0b, c1b=c1b: e.reduce_sum(
                out=ztAll[64:128, i:i + 1], in_=Ptm[p][64:128, c0b:c1b], axis=AX),
                w=wzr)
            t_ztred[i] = tzr
        if not (_kmin and _kmin_noA):
            emit_pcs(NT - 1)
        if _kmin and _kmin_noA:
            pass
        else:
            war_cvec = [(sT, t_pcs[NT - 1])]
        if (not _kmin) or not _kmin_noScan:
            if stop_stage == 21 and h == 0:
                return _finish_early(nc, P, [(sT, sT.n), (sV, sV.n), (sS, sS.n),
                                             (dS, dS.n), (dG, dG.n)])
            tevp = None
            for g in range(4):
                tevp = V(lambda e, g=g, hp=hp: e.tensor_copy(
                    out=pcsS[hp][0:32, 512 * g: 512 * (g + 1)],
                    in_=PS[0:32, 2048 + 512 * g: 2048 + 512 * (g + 1)]),
                    w=[(sT, t_pcs[NT - 1])] + war_pcs[hp])
                bfree(4 + g, sV, tevp)
            t_pcsev = tevp
            war_pcs[hp] = []

            s1_ = T(lambda e, h=h: e.matmul(PS[0:32, 3584:3585], csb["ones32f"],
                                            lnmega[0:1, h:h + 1], start=True, stop=True,
                                            skip_group_check=True),
                    w=[(sS, t_lnmega)] + bwait(7))
            s2_ = V(lambda e: e.tensor_copy(out=lnm32, in_=PS[0:32, 3584:3585]),
                    w=[(sT, s1_)])
            s3_ = V(lambda e: e.tensor_scalar_mul(out=EA, in0=csb["emat"], scalar1=lnm32),
                    w=[(sV, s2_)])
            s4_ = S(lambda e: e.activation(out=EA, in_=EA, func=AF.Exp), w=[(sV, s3_)])
            s5_ = V(lambda e: e.tensor_tensor(out=ATf, in0=EA, in1=csb["ltri"],
                                              op=ALU.mult), w=[(sS, s4_)])
            s6_ = V(lambda e: e.tensor_copy(out=ATb, in_=ATf), w=[(sV, s5_)])
            s7_ = V(lambda e, hp=hp: e.reduce_sum(out=Pcol, in_=pcsS[hp][0:32, :], axis=AX),
                    w=[(sV, t_pcsev)])
            s8_ = V(lambda e: e.tensor_copy(out=Pcolb[0:32, 0:1], in_=Pcol), w=[(sV, s7_)])
            s9_ = T(lambda e: e.matmul(PS[0:32, 3585:3586], ATb, Pcolb[0:32, 0:1],
                                       start=True, stop=True, skip_group_check=True),
                    w=[(sV, s6_), (sV, s8_)])
            s10_ = V(lambda e: e.tensor_scalar_add(out=rn0, in0=PS[0:32, 3585:3586],
                                                   scalar1=1e-5), w=[(sT, s9_)])
            s11_ = V(lambda e: e.reciprocal(out=rnorm, in_=rn0), w=[(sV, s10_)])
            s12_ = T(lambda e: e.transpose(PS[0:1, 3586:3618], rnorm,
                                           csb["idnf32"][0:32, 0:32]), w=[(sV, s11_)])
            s13_ = V(lambda e: e.tensor_copy(out=rnrow, in_=PS[0:1, 3586:3618]),
                     w=[(sT, s12_)])
            s14_ = T(lambda e: e.matmul(PS[0:32, 3618:3650], csb["ones32f"], rnrow,
                                        start=True, stop=True, skip_group_check=True),
                     w=[(sV, s13_)])
            s15_ = V(lambda e: e.tensor_copy(out=rnb32, in_=PS[0:32, 3618:3650]),
                     w=[(sT, s14_)])
            s16_ = V(lambda e: e.tensor_tensor(out=A2b, in0=ATf, in1=rnb32, op=ALU.mult),
                     w=[(sV, s15_)])
            bfree(7, sV, s15_)
            twt = None
            for c in range(NCH):
                twt = T(lambda e, c=c, hp=hp: e.matmul(
                    PS[:, 3072 + 32 * c: 3072 + 32 * (c + 1)],
                    pcsS[hp][0:32, 128 * c: 128 * (c + 1)], A2b,
                    start=True, stop=True, skip_group_check=True),
                    w=[(sV, s16_)] + (bwait(6) if c == 0 else []), inc=(c == NCH - 1))
            war_pcs[hp] = [(sT, twt)]
            t_wTr = V(lambda e, hp=hp: e.tensor_copy(out=wTh[hp], in_=PS[:, 3072:3584]),
                      w=[(sT, twt)] + war_wTh[hp])
            zz1 = V(lambda e: e.reciprocal(out=ztAll, in_=ztAll),
                    w=[(sV, t_ztred[NT - 1])])
            zz2 = V(lambda e: e.tensor_scalar_mul(out=ztAll, in0=ztAll,
                                                  scalar1=twcol),
                    w=[(sV, zz1), (sV, t_twcol)])
            zz3 = T(lambda e: e.transpose(PS[0:16, 3584:3712], ztAll,
                                          csb["idnf32"]),
                    w=[(sV, zz2)] + bwait(7))
            t_ztT_read = zz3
            zz4 = V(lambda e: e.tensor_copy(out=ztT, in_=PS[0:16, 3584:3712]),
                    w=[(sT, zz3)])
            t_ztTb = V(lambda e: e.tensor_copy(out=ztTb[0:16, :], in_=ztT),
                       w=[(sV, zz4)])
            bfree(7, sV, zz4)
            if stop_stage == 22 and h == 0:
                return _finish_early(nc, P, [(sT, sT.n), (sV, sV.n), (sS, sS.n),
                                             (dS, dS.n), (dG, dG.n)])

        Tdrain()
        sfree = {0: list(bank_free[0]), 1: list(bank_free[1])}
        t_stm = {}
        t_wg = {}
        t_pv = [0] * NCH
        t_bmm = [0] * NCH
        t_b2m = [0] * NCH
        t_rzev = [0] * NCH
        t_cmbm = [0] * NCH
        t_cmb = [0] * NCH
        t_b1w = [0] * NCH
        def emit_fin(c):
            pr2 = c % 2
            bac = 3072 + pr2 * 128
            rzc = 3328 + pr2 * 128
            wfin = []
            if c > 1:
                wfin += [(sV, t_rzev[c - 2]), (sV, t_cmbm[c - 2])]
            else:
                wfin += war_fin[pr2] + bwait(6)
            wbm = [(sV, t_b2m[c])] + wfin
            if c > 0:
                wbm += [(sS, t_b1w[c])]
                vsl1 = v_tm[:, (c - 1) * 1024 + h * 64: (c - 1) * 1024 + (h + 1) * 64]
                T(lambda e, c=c, vs=vsl1, hb=hb, tp=tp, bac=bac: e.matmul(
                    PS[hb:hb + 64, bac:bac + 128], vs, band1[c % 3],
                    start=True, stop=False, skip_group_check=True,
                    tile_position=tp), w=wbm, inc=False)
            vsl = v_tm[:, c * 1024 + h * 64: c * 1024 + (h + 1) * 64]
            t_bmm[c] = T(lambda e, c=c, vs=vsl, hb=hb, tp=tp, bac=bac: e.matmul(
                PS[hb:hb + 64, bac:bac + 128], vs, band2P[c % 2],
                start=(c == 0), stop=True, skip_group_check=True,
                tile_position=tp), w=wbm)
            tbc = T(lambda e, c=c, hb=hb, tp=tp, rzc=rzc: e.matmul(
                PS[hb:hb + 64, rzc:rzc + 128],
                csb["e16"][:, c * 64: c * 64 + 64], ztTb,
                start=True, stop=True, skip_group_check=True,
                tile_position=tp), w=wfin + [(sV, t_ztTb)])
            u4 = V(lambda e, hb=hb, pr2=pr2, rzc=rzc: e.tensor_copy(
                out=rztw_sbP[pr2][hb:hb + 64, :],
                in_=PS[hb:hb + 64, rzc:rzc + 128]), w=[(sT, tbc)])
            t_rzev[c] = u4
            u5 = V(lambda e, hb=hb, pr2=pr2, bac=bac: e.tensor_tensor(
                out=tcombP[pr2][hb:hb + 64, :], in0=PS[hb:hb + 64, bac:bac + 128],
                in1=rztw_sbP[pr2][hb:hb + 64, :], op=ALU.mult),
                w=[(sV, u4), (sT, t_bmm[c])])
            t_cmbm[c] = u5
            qc0 = 1024 + 512 * (c // 4) + (c % 4) * 128
            u6 = V(lambda e, qc0=qc0, hb=hb, pr2=pr2: e.tensor_tensor(
                out=t2combP[pr2][hb:hb + 64, :], in0=PS[hb:hb + 64, qc0:qc0 + 128],
                in1=owrep2[hb:hb + 64, :], op=ALU.mult),
                w=[(sT, t_pv[c]), (sV, t_owrep)])
            u7 = V(lambda e, c=c, hb=hb, icb=icb, pr2=pr2: e.tensor_tensor(
                out=y_fm[hb:hb + 64, icb + c * 128: icb + (c + 1) * 128],
                in0=tcombP[pr2][hb:hb + 64, :], in1=t2combP[pr2][hb:hb + 64, :],
                op=ALU.add), w=[(sV, u5), (sV, u6)])
            t_cmb[c] = u7

        for c in range(NCH):
            Gmin = c // 4
            Tdrain()
            ksl = ksl_all[:, c * 128:(c + 1) * 128]
            G2b = Gmin
            G1b = (128 * c + 128) // 512 if c < NCH - 1 else -1
            off0 = 128 * c - 512 * G2b
            sb0 = G2b % 2
            tb2 = None
            tb1 = None
            for Gi in range(Gmin, 4):
                n0, n1 = max(128 * c, 512 * Gi), 512 * (Gi + 1)
                sb = Gi % 2
                off = n0 - 512 * Gi
                w0 = list(sfree[sb])
                sfree[sb] = []
                tsm = T(lambda e, n0=n0, n1=n1, sb=sb, off=off, k=ksl,
                        qs=qsl_all: e.matmul(
                    PS[:, sb * 512 + off: sb * 512 + off + (n1 - n0)],
                    k, qs[:, n0:n1], start=True, stop=True, skip_group_check=True),
                    w=w0)
                t_stm[(c, Gi)] = tsm
                wact = []
                if Gi == G2b:
                    wb2 = [(sT, tsm)]
                    if c > 1:
                        wb2 += [(sT, t_bmm[c - 2])]
                    else:
                        wb2 += war_band2[c % 2]
                    tb2 = S(lambda e, sb0=sb0, off0=off0, c=c: e.activation(
                        out=band2P[c % 2],
                        in_=PS[:, sb0 * 512 + off0: sb0 * 512 + off0 + 128],
                        func=AF.Exp, scale=0.125), w=wb2)
                    sfree[sb0].append((sS, tb2))
                    wact.append((sS, tb2))
                if Gi == G1b:
                    colB1 = 128 * c + 128
                    off1 = colB1 - 512 * G1b
                    sb1 = G1b % 2
                    wb1 = [(sT, tsm)]
                    if c > 1:
                        wb1 += [(sT, t_bmm[c - 2])]
                    else:
                        wb1 += war_band1[(c + 1) % 3]
                    tb1 = S(lambda e, sb1=sb1, off1=off1, c=c: e.activation(
                        out=band1[(c + 1) % 3][64:128, 0:64],
                        in_=PS[64:128, sb1 * 512 + off1: sb1 * 512 + off1 + 64],
                        func=AF.Exp, scale=0.125), w=wb1)
                    t_b1w[c + 1] = tb1
                    sfree[sb1].append((sS, tb1))
                    wact.append((sS, tb1))
                nr = (n1 - n0) // 64
                r0 = n0 // 64
                wwg = [(sT, tsm), (sV, t_wTr)] + wact
                if c >= 2:
                    wwg.append((sT, t_pv[c - 2]))
                wwg += war_pt[c % 2]
                twg = V(lambda e, n0=n0, n1=n1, sb=sb, off=off, nr=nr, r0=r0,
                        c=c, hp=hp: e.tensor_tensor(
                    out=stripB[c % 2][:, n0:n1].rearrange("p (r z) -> p r z", z=64),
                    in0=PS[:, sb * 512 + off: sb * 512 + off + (n1 - n0)]
                        .rearrange("p (r z) -> p r z", z=64),
                    in1=wTh[hp][:, 32 * c + r0: 32 * c + r0 + nr]
                        .unsqueeze(2).broadcast_to([128, nr, 64]),
                    op=ALU.mult), w=wwg)
                t_wg[(c, Gi)] = twg
                sfree[sb].append((sV, twg))
            war_pt[c % 2] = []
            t_b2m[c] = V(lambda e, c=c: e.tensor_tensor(out=band2P[c % 2],
                                               in0=band2P[c % 2],
                                               in1=csb["tri01"],
                                               op=ALU.mult), w=[(sS, tb2)])
            if stop_stage == 51 and h == 0 and c == 0:
                return _finish_early(nc, P, [(sT, sT.n), (sV, sV.n), (sS, sS.n),
                                             (dS, dS.n), (dG, dG.n)])
            tpv = None
            for Qd in range(Gmin, 4):
                n0q, n1q = max(512 * Qd, 128 * c), 512 * (Qd + 1)
                qoff = n0q - 512 * Qd
                wpv = [(sV, t_wg[(c, Qd)])]
                if c == 0:
                    wpv += bwait(2 + Qd)
                    if t_comb_prev:
                        wpv.append((sV, t_comb_prev))
                vsl = v_tm[:, c * 1024 + h * 64: c * 1024 + (h + 1) * 64]
                tpv = T(lambda e, Qd=Qd, n0q=n0q, n1q=n1q, qoff=qoff, c=c,
                        vs=vsl, hb=hb, tp=tp: e.matmul(
                    PS[hb:hb + 64,
                       1024 + 512 * Qd + qoff: 1024 + 512 * Qd + qoff + (n1q - n0q)],
                    vs, stripB[c % 2][:, n0q:n1q],
                    start=(c == 0), stop=(c == 4 * Qd + 3), skip_group_check=True,
                    tile_position=tp), w=wpv, inc=(Qd == 3))
            t_pv[c] = tpv
            if stop_stage == 52 and h == 0 and c == 0:
                return _finish_early(nc, P, [(sT, sT.n), (sV, sV.n), (sS, sS.n),
                                             (dS, dS.n), (dG, dG.n)])
            emit_fin(c)

            if stop_stage == 23 + c and h == 0:
                return _finish_early(nc, P, [(sT, sT.n), (sV, sV.n), (sS, sS.n),
                                             (dS, dS.n), (dG, dG.n)])
        t_comb_prev = t_cmb[NCH - 1]
        war_pt[0] = [(sT, t_pv[NCH - 2])]
        war_pt[1] = [(sT, t_pv[NCH - 1])]
        war_wTh[hp] = [(sV, t_wg[(NCH - 1, 3)])]
        war_band1 = [[(sT, t_bmm[15])], [(sT, t_bmm[13])], [(sT, t_bmm[14])]]
        war_band2 = [[(sT, t_bmm[NCH - 2])], [(sT, t_bmm[NCH - 1])]]
        war_fin = [[(sV, t_rzev[NCH - 2]), (sV, t_cmbm[NCH - 2])],
                   [(sV, t_rzev[NCH - 1]), (sV, t_cmbm[NCH - 1])]]
        bank_free[6] = [(sV, t_cmbm[NCH - 1])]
        for qd in range(4):
            bank_free[2 + qd] = [(sV, t_cmb[NCH - 1])]
        for sb in (0, 1):
            bank_free[sb] = list(sfree[sb])
        if stop_stage == 3 and h == 0:
            return _finish_early(nc, P, [(sT, sT.n), (sV, sV.n), (sS, sS.n),
                                         (dS, dS.n), (dG, dG.n)])

    t_attn_T = sT.n
    if stop_stage <= 4:
        return _finish_early(nc, P, [(sT, sT.n), (sV, sV.n), (sS, sS.n),
                                     (dS, dS.n), (dG, dG.n)])
    t_attn_V = sV.n
    t_attn_S = sS.n

    Tdrain()
    w_att = [(sT, t_attn_T), (sV, t_attn_V), (sS, t_attn_S)]
    twc = None
    for ic in range(8):
        twc = SY(lambda e, ic=ic: e.dma_start(
            out=wcT[:, ic * 1024:(ic + 1) * 1024],
            in_=wbf[3][:, ic * 128:(ic + 1) * 128], transpose=True),
            w=[(dG, tw_wbf[3])] + w_att)
    t_wcT = twc
    t_b1 = G(lambda e: e.dma_start(
        out=b1row, in_=b1d.rearrange("e h -> (e h)").unsqueeze(0)), w=w_att)
    t_e8 = G(lambda e: e.dma_start(out=csb["e8"], in_=cin["e8"]), w=w_att)
    t_b2 = G(lambda e: e.dma_start(out=b2_sb, in_=b2d), w=w_att)

    t_w1use = {}
    t_w2use = {}
    t_hm = {0: 0, 1: 0}
    t_eo_hl = {}
    t_evac = {}
    t_odma = {}
    for s_ in range(4):
        for e_ in range(NE):
            eb = e_ % 2
            m = s_ * NE + e_
            wld = [(dG, t_w1bf[e_])] + w_att
            if m >= 2:
                wld.append((sT, t_w1use[m - 2]))
            tld = SY(lambda e, i=e_, eb=eb: e.dma_start(out=w1b[eb], in_=w1bf[i]),
                     w=wld)
            gcb = 2 + eb
            tgbc = T(lambda e, i=e_, s=s_, gcb=gcb: e.matmul(
                PS[:, gcb * 512:(gcb + 1) * 512], csb["e8"][:, i * 128:(i + 1) * 128],
                g_t8[0:8, s * 512:(s + 1) * 512], start=True, stop=True,
                skip_group_check=True),
                w=[(sV, t_ln_done), (dG, t_e8)] + bwait(gcb))
            tlast = None
            thm = None
            for hc in range(4):
                bk = (e_ * 4 + hc) % 2
                w0 = [(dS, tld)] + bwait(bk)
                for icc in range(8):
                    T(lambda e, icc=icc, hc=hc, eb=eb, s=s_, bk=bk: e.matmul(
                        PS[:, bk * 512:(bk + 1) * 512],
                        w1b[eb][:, icc * 512 + hc * 128: icc * 512 + (hc + 1) * 128],
                        xn_fm[:, icc * 2048 + s * 512: icc * 2048 + (s + 1) * 512],
                        start=(icc == 0), stop=False, skip_group_check=True),
                        w=w0, inc=False)
                tb1m = T(lambda e, i=e_, hc=hc, bk=bk: e.matmul(
                    PS[:, bk * 512:(bk + 1) * 512],
                    b1row[0:1, i * 512 + hc * 128: i * 512 + (hc + 1) * 128],
                    csb["onesrow"], start=False, stop=True, skip_group_check=True),
                    w=[(dG, t_b1)])
                tlast = tb1m
                wgl = [(sT, tb1m)]
                if t_hm[bk]:
                    wgl.append((sV, t_hm[bk]))
                tgl = S(lambda e, bk=bk: e.activation(
                    out=hgel[bk], in_=PS[:, bk * 512:(bk + 1) * 512], func=AF.Gelu),
                    w=wgl)
                bfree(bk, sS, tgl)
                whm = [(sS, tgl), (sT, tgbc)]
                if s_ > 0 and (s_ * NE + e_) in ():
                    pass
                if s_ > 0:
                    whm.append((sT, t_eo_hl[s_ - 1]))
                thm = V(lambda e, i=e_, hc=hc, bk=bk, gcb=gcb: e.tensor_tensor(
                    out=hprime[:, (i * 4 + hc) * 512:(i * 4 + hc + 1) * 512],
                    in0=hgel[bk], in1=PS[:, gcb * 512:(gcb + 1) * 512], op=ALU.mult),
                    w=whm)
                t_hm[bk] = thm
            bfree(gcb, sV, thm)
            t_w1use[m] = tlast
        treg = [0] * 8
        for e_ in range(NE):
            eb = e_ % 2
            m = s_ * NE + e_
            wld = [(dG, t_w2bf[e_])] + w_att
            if m >= 2:
                wld.append((sT, t_w2use[m - 2]))
            tld = SY(lambda e, i=e_, eb=eb: e.dma_start(out=w2b[eb], in_=w2bf[i]),
                     w=wld)
            tm = None
            for hc in range(4):
                for reg in range(8):
                    tc4, og = reg // 2, reg % 2
                    first = (e_ == 0 and hc == 0)
                    w0 = [(dS, tld)]
                    if first:
                        w0 += bwait(reg)
                    hsl = hprime[:, (e_ * 4 + hc) * 512 + tc4 * 128:
                                 (e_ * 4 + hc) * 512 + (tc4 + 1) * 128]
                    tm = T(lambda e, hs=hsl, eb=eb, hc=hc, og=og, reg=reg,
                           first=first: e.matmul(
                        PS[:, reg * 512:(reg + 1) * 512], hs,
                        w2b[eb][:, hc * 1024 + og * 512: hc * 1024 + (og + 1) * 512],
                        start=first, stop=False, skip_group_check=True),
                        w=w0, inc=(hc == 3 and reg == 7))
            t_w2use[m] = tm
        t_eo_hl[s_] = t_w2use[s_ * NE + NE - 1]
        for reg in range(8):
            tc4, og = reg // 2, reg % 2
            gtc = s_ * 4 + tc4
            T(lambda e, gtc=gtc, og=og, reg=reg: e.matmul(
                PS[:, reg * 512:(reg + 1) * 512],
                g_t8[0:8, gtc * 128:(gtc + 1) * 128],
                b2_sb[0:8, og * 512:(og + 1) * 512],
                start=False, stop=False, skip_group_check=True),
                w=[(dG, t_b2)], inc=False)
            for icc in range(8):
                treg[reg] = T(lambda e, icc=icc, gtc=gtc, og=og, reg=reg: e.matmul(
                    PS[:, reg * 512:(reg + 1) * 512],
                    y_fm[:, icc * 2048 + gtc * 128: icc * 2048 + (gtc + 1) * 128],
                    wcT[:, icc * 1024 + og * 512: icc * 1024 + (og + 1) * 512],
                    start=False, stop=(icc == 7), skip_group_check=True),
                    w=[(dS, t_wcT)], inc=(icc == 7))
        for tc4 in range(4):
            gtc = s_ * 4 + tc4
            pe = gtc % 2
            wxl = w_att + ([(sV, t_evac[gtc - 2])] if gtc >= 2 else [])
            txl = SY(lambda e, gtc=gtc, pe=pe: e.dma_start(
                out=xst2[pe], in_=x[gtc * 128:(gtc + 1) * 128, :]), w=wxl)
            tev = None
            for og in range(2):
                reg = tc4 * 2 + og
                wev = [(sT, treg[reg]), (dS, txl)]
                if gtc >= 2:
                    wev.append((sO, t_odma[gtc - 2]))
                tev = V(lambda e, pe=pe, og=og, reg=reg: e.tensor_tensor(
                    out=outst[pe][:, og * 512:(og + 1) * 512],
                    in0=PS[:, reg * 512:(reg + 1) * 512],
                    in1=xst2[pe][:, og * 512:(og + 1) * 512], op=ALU.add), w=wev)
                bfree(reg, sV, tev)
            t_evac[gtc] = tev
            t_odma[gtc] = SYo(lambda e, gtc=gtc, pe=pe: e.dma_start(
                out=out[gtc * 128:(gtc + 1) * 128, :], in_=outst[pe]),
                w=[(sV, tev)])

    if DEBUG:
        wd = [(sO, t_odma[NT - 1])]
        G(lambda e: e.dma_start(out=dbg["d_xn"], in_=xn_fm), w=wd)
        G(lambda e: e.dma_start(out=dbg["d_q"], in_=q_fm), w=wd)
        G(lambda e: e.dma_start(out=dbg["d_k"], in_=k_fm), w=wd)
        G(lambda e: e.dma_start(out=dbg["d_v"], in_=v_tm), w=wd)
        G(lambda e: e.dma_start(out=dbg["d_y"], in_=y_fm), w=wd)
        G(lambda e: e.dma_start(out=dbg["d_g"], in_=g_t8), w=wd)
        G(lambda e: e.dma_start(out=dbg["d_pcs"], in_=pcsS[1][0:32, :]), w=wd)
        G(lambda e: e.dma_start(out=dbg["d_wt"], in_=wTh[1]), w=wd)
        P.add("gpsimd", lambda e: e.nop(), w=[(dG, dG.n)])

    P.add("sync", lambda e: e.nop(), w=[(sO, 16 * NT)])

    with nc.Block() as block:
        @block.sync
        def _(e):
            P.replay("sync", e)

        @block.tensor
        def _(e):
            P.replay("tensor", e)

        @block.vector
        def _(e):
            P.replay("vector", e)

        @block.scalar
        def _(e):
            P.replay("scalar", e)

        @block.gpsimd
        def _(e):
            P.replay("gpsimd", e)

    return nc


_NC_CACHE = None


def kernel(**inputs):
    global _NC_CACHE
    xs = np.asarray(inputs["L0_attn_hidden_states"])
    base = {
        "Wq": np.ascontiguousarray(inputs["L0_attn_Wq"], np.float32),
        "Wk": np.ascontiguousarray(inputs["L0_attn_Wk"], np.float32),
        "Wv": np.ascontiguousarray(inputs["L0_attn_Wv"], np.float32),
        "Wc": np.ascontiguousarray(inputs["L0_attn_Wc"], np.float32),
        "terr": np.ascontiguousarray(inputs["L0_attn_terrace_mix"],
                                     np.float32).reshape(1, SEG),
        "idxd": np.ascontiguousarray(inputs["L0_attn_index_decays"], np.float32),
        "megad": np.ascontiguousarray(inputs["L0_attn_mega_decays"],
                                      np.float32).reshape(1, H),
        "w1": np.ascontiguousarray(inputs["L0_ffn_w1"], np.float32),
        "b1": np.ascontiguousarray(inputs["L0_ffn_b1"], np.float32),
        "w2": np.ascontiguousarray(inputs["L0_ffn_w2"], np.float32),
        "b2": np.ascontiguousarray(inputs["L0_ffn_b2"], np.float32),
        "cent": np.ascontiguousarray(inputs["L0_ffn_centroids"], np.float32),
        "wrt": np.ascontiguousarray(inputs["L0_ffn_w_route"], np.float32),
    }
    for k, v in _host_consts().items():
        base["c_" + k] = v
    if _NC_CACHE is None:
        _NC_CACHE = build()
    nc = _NC_CACHE
    in_maps = []
    for c in range(NCORE):
        m = dict(base)
        m["x"] = np.ascontiguousarray(xs[c], np.float32)
        in_maps.append(m)
    res = run_bass_kernel_spmd(nc, in_maps, core_ids=list(range(NCORE)))
    return np.stack([np.asarray(res.results[c]["out"]) for c in range(NCORE)],
                    axis=0)
